# version 01
# baseline (speedup 1.0000x reference)
import numpy as np
import ml_dtypes
import concourse.bass as bass
import concourse.mybir as mybir
from concourse.bass_utils import run_bass_kernel_spmd
from contextlib import ExitStack

F32 = mybir.dt.float32
BF16 = mybir.dt.bfloat16
ALU = mybir.AluOpType
AF = mybir.ActivationFunctionType

BF = ml_dtypes.bfloat16

B, T, D = 4, 4096, 2048
H = 8192
HEADS, GROUPS, DH = 16, 4, 128
QG = HEADS // GROUPS
SWA = 1024
NQ = 2048
NS = 3072
NQT = 16
QB = 512
NQB = 4
MEGA_QB = 2
SMEGA = 2048
UN = 12
UNW = UN * 128
NEG = -1.0e30
KV_DO_V = True
KV_DO_NORM = True
KV_DO_K = True

M32 = 12582912.0
M64 = 6755399441055744.0
ATANH05 = 0.5493061443340548457
A05_HI = float(np.float32(ATANH05))
A05_LO = float(np.float32(ATANH05 - A05_HI))
ISQ128 = 1.0 / np.sqrt(np.float64(128.0))
ISQ_HI = float(np.float32(ISQ128))
ISQ_LO = float(np.float32(ISQ128 - ISQ_HI))



class Prog:

    ENGINES = ("sync", "tensor", "vector", "scalar", "gpsimd")

    def __init__(self, nc, ctx):
        self.nc = nc
        self.ctx = ctx
        self.block = ctx.enter_context(nc.Block())
        self.sems = {}
        self.cnt = {}
        self.idx = {}
        self.segs = {e: [] for e in self.ENGINES}

    def sem(self, name):
        if name not in self.sems:
            self.sems[name] = self.ctx.enter_context(self.nc.semaphore(name))
            self.cnt[name] = 0
        return self.sems[name]

    def bump(self, name, k=1):
        self.sem(name)
        self.cnt[name] += k
        return self.cnt[name]

    def seg(self, engine):
        dec = getattr(self.block, engine)

        def wrap(fn):
            dec(fn)
            return None
        return wrap

    NDMA = 4

    def tok(self, q, i):
        return (f"{q}{i % self.NDMA}", (i // self.NDMA + 1) * 16)

    def wge(self, e, token):
        e.wait_ge(self.sem(token[0]), token[1])

    def dma(self, e, out, in_, q):
        i = self.idx.setdefault(q, 0)
        self.idx[q] = i + 1
        name, val = self.tok(q, i)
        s = self.sem(name)
        if val > 16:
            e.wait_ge(s, val - 16)
        e.dma_start(out=out, in_=in_).then_inc(s, 16)
        self.cnt[name] = val
        return (name, val)

    def ev(self, instr, q, k=1):
        s = self.sem(q)
        instr.then_inc(s, k)
        self.cnt[q] += k
        return self.cnt[q]

    def _sp(self, e, eng, col):
        sc = self.spc if eng == "v" else self.spc2
        w = sc.shape[1] // 4
        sl = slice(col * w, (col + 1) * w)
        if eng == "v":
            return e.tensor_scalar(out=sc[:, sl], in0=sc[:, sl], scalar1=1.0,
                                   scalar2=None, op0=ALU.mult, op1=ALU.bypass)
        return e.copy(out=sc[:, sl], in_=sc[:, sl])

    def vsp(self, e, n=1):
        ins = None
        for k in range(n):
            ins = self._sp(e, "v", k % 2)
        return ins

    def ssp(self, e, n=1):
        ins = None
        for k in range(n):
            ins = self._sp(e, "s", k % 2)
        return ins

    def vbar(self, e):
        self._sp(e, "v", 0)
        n = self.ev(self._sp(e, "v", 1), "vsq")
        e.wait_ge(self.sems["vsq"], n)

    def sbar(self, e):
        self._sp(e, "s", 0)
        n = self.ev(self._sp(e, "s", 1), "ssq")
        e.wait_ge(self.sems["ssq"], n)

    def vfence(self, e, q, k=1):
        self._sp(e, "v", 0)
        r = self.ev(self._sp(e, "v", 1), q, k)
        n = self.ev(self._sp(e, "v", 2), "vsq")
        e.wait_ge(self.sems["vsq"], n)
        return r

    def sfence(self, e, q, k=1):
        self._sp(e, "s", 0)
        r = self.ev(self._sp(e, "s", 1), q, k)
        n = self.ev(self._sp(e, "s", 2), "ssq")
        e.wait_ge(self.sems["ssq"], n)
        return r

    def tfence(self, e, q, k=1):
        ins = e.transpose(self.dps2[0:128, 0:128], self.g_ident[:], self.g_ident[:])
        return self.ev(ins, q, k)



def _masks_for_core(half):
    out = np.zeros((NQT, 128, UNW), dtype=np.float32)
    for qt in range(NQT):
        j = qt // 4
        g = half * 16 + qt
        qidx = g * 128 + np.arange(128)[:, None]
        kbase = (4 * j + half * 16 - 8) * 128
        kidx = kbase + np.arange(UNW)[None, :]
        allowed = (kidx <= qidx) & (qidx - kidx < SWA) & (kidx >= 0)
        out[qt] = np.where(allowed, 0.0, NEG)
    return out.astype(BF)


def _stage_core(core, x, Wq, Wk, Wv, Wo, bo, w1T, w2T, w3T, g1, g2, g3):
    b, half = core // 2, core % 2
    xb = x[b]
    lo = half * NQ - SWA
    hi = half * NQ + NQ
    xt = np.zeros((D, NS), dtype=np.float32)
    s0 = max(lo, 0)
    xt[:, s0 - lo:hi - lo] = xb[s0:hi].T
    d = {
        "xT": np.ascontiguousarray(xt),
        "x_nat": np.ascontiguousarray(xb[half * NQ:half * NQ + NQ]),
        "Wq": Wq, "Wk": Wk, "Wv": Wv, "Wo": Wo,
        "bo_b": np.ascontiguousarray(np.broadcast_to(bo, (128, D))),
        "w1T": w1T, "w2T": w2T, "w3T": w3T,
        "g1b": np.ascontiguousarray(np.broadcast_to(g1, (128, D))),
        "g2b": np.ascontiguousarray(np.broadcast_to(g2, (128, D))),
        "g3b": np.ascontiguousarray(np.broadcast_to(g3, (128, H))),
        "g3c": np.ascontiguousarray(g3.reshape(H, 1)),
        "g3t": np.ascontiguousarray(g3.reshape(64, 128).T),
        "masks": _masks_for_core(half),
        "ident": np.eye(128, dtype=np.float32),
    }
    return d



def build_program(debug_taps=(), stop_after=None):
    nc = bass.Bass("TRN2", target_bir_lowering=False, debug=False)
    ctx = ExitStack()

    def inp(name, shape, dt=F32):
        return nc.dram_tensor(name, list(shape), dt, kind="ExternalInput").ap()

    def scr(name, shape, dt=F32):
        kind = "ExternalOutput" if name in debug_taps else "Internal"
        return nc.dram_tensor(name, list(shape), dt, kind=kind).ap()

    t_in = {
        "xT": inp("xT", (D, NS)),
        "x_nat": inp("x_nat", (NQ, D)),
        "Wq": inp("Wq", (D, D)),
        "Wk": inp("Wk", (D, 512)),
        "Wv": inp("Wv", (D, 512)),
        "Wo": inp("Wo", (D, D)),
        "bo_b": inp("bo_b", (128, D)),
        "w1T": inp("w1T", (D, H)),
        "w2T": inp("w2T", (D, H)),
        "w3T": inp("w3T", (H, D)),
        "g1b": inp("g1b", (128, D)),
        "g2b": inp("g2b", (128, D)),
        "g3b": inp("g3b", (128, H)),
        "g3c": inp("g3c", (H, 1)),
        "g3t": inp("g3t", (128, 64)),
        "masks": inp("masks", (NQT, 128, UNW), BF16),
        "ident": inp("ident", (128, 128)),
    }
    t_scr = {
        "wq_hl": scr("wq_hl", (2, HEADS, D, DH), BF16),
        "wo_hl": scr("wo_hl", (2, 4, D, QB), BF16),
        "t1": scr("t1", (D, H), BF16),
        "t2": scr("t2", (D, H), BF16),
        "t3": scr("t3", (16, H, 128), BF16),
        "xq1": scr("xq1", (NQ, D), BF16),
        "xq2": scr("xq2", (NQ, D), BF16),
        "x1": scr("x1", (NQ, D)),
        "hT": scr("hT", (H, NQ)),
        "stats": scr("stats", (128, 64)),
    }
    out = nc.dram_tensor("out", [NQ, D], F32, kind="ExternalOutput").ap()

    P = Prog(nc, ctx)
    emit_all(P, t_in, t_scr, out, stop_after=stop_after)
    ctx.close()
    return nc



def emit_all(P, t_in, t_scr, out, stop_after=None):
    nc = P.nc

    sb = lambda n, s, d=F32: P.ctx.enter_context(nc.sbuf_tensor(n, s, d))

    g = {}
    g["ident"] = sb("c_ident", [128, 128])
    g["identb"] = sb("c_identb", [128, 128], BF16)
    g["ones_col"] = sb("c_ones_col", [128, 1])
    g["ones_row"] = sb("c_ones_row", [1, 128])
    g["meff"] = sb("c_meff", [128, 1])
    g["sig"] = sb("c_sig", [128, 1])
    g["cs"] = sb("c_cs", [128, 1])
    g["drv"] = sb("c_drv", [128, 1])
    P.spc = sb("c_spc", [128, 64])
    P.spc2 = sb("c_spc2", [128, 16])
    g["drs"] = sb("c_drs", [128, 1])
    g["drg"] = sb("c_drg", [128, 1])
    for n in ("bnd1", "bnd2", "bnd3", "nbnd1", "nbnd2", "nbnd3",
              "a1", "a2", "a3"):
        g[n] = sb("c_" + n, [128, 1])
    for n in ("beta1", "beta2", "gam1", "gam2", "ssq3", "am3"):
        g[n] = sb("st_" + n, [128, NQT])
    g["mk2"] = sb("st_mk2", [1, 8])
    g["scalars"] = sb("st_scalars", [1, 32])
    g["dps"] = P.ctx.enter_context(nc.psum_tensor("dps", [128, 512], F32))
    P.g_ident = g["ident"]

    emit_p0_consts(P, t_in, g)
    emit_p1_weight_prep(P, t_in, t_scr, g)
    barrier(P, g)
    if stop_after == "p1":
        emit_debug_dump(P, g, t_scr)
        return
    for mega in range(2):
        if stop_after == "p2" and mega == 1:
            emit_debug_dump(P, g, t_scr)
            return
        with ExitStack() as mctx:
            kv = {}
            kv["k_hi"] = mctx.enter_context(nc.sbuf_tensor(f"kv_khi{mega}", [128, 4 * SMEGA], BF16))
            kv["k_lo"] = mctx.enter_context(nc.sbuf_tensor(f"kv_klo{mega}", [128, 4 * SMEGA], BF16))
            kv["v"] = mctx.enter_context(nc.sbuf_tensor(f"kv_v{mega}", [128, 4 * SMEGA], F32))
            emit_kv_build(P, t_in, g, kv, mega)
            barrier(P, g)
            if stop_after == "p2":
                continue
            for j in range(MEGA_QB):
                emit_attention_qb(P, t_in, t_scr, g, kv, mega, mega * MEGA_QB + j)
                barrier(P, g)
    if stop_after == "p3":
        emit_debug_dump(P, g, t_scr)
        return
    emit_ffn12(P, t_in, t_scr, g)
    barrier(P, g)
    if stop_after == "p4":
        emit_stats_dump(P, g, t_scr)
        emit_debug_dump(P, g, t_scr)
        return
    emit_ffn3(P, t_in, t_scr, g, out)


def emit_stats_dump(P, g, t_scr):
    @P.seg("gpsimd")
    def _(e):
        t = P.dma(e, t_scr["stats"][:, 16:32], g["ssq3"][:], "d_g")
        P.wge(e, t)
        t = P.dma(e, t_scr["stats"][:, 48:64], g["am3"][:], "d_g")
        P.wge(e, t)


def barrier(P, g):
    bar = P.sem("bar")
    base = P.cnt["bar"]
    target = base + 5
    P.cnt["bar"] = target
    dma_waits = [(P.sems[n], P.cnt[n]) for n in list(P.cnt)
                 if n.startswith("d_") and P.cnt[n] > 0]

    @P.seg("sync")
    def _(e):
        for s, v in dma_waits:
            e.wait_ge(s, v)
        e.nop().then_inc(bar, 1)
        e.wait_ge(bar, target)

    @P.seg("tensor")
    def _(e):
        e.wait_ge(bar, base + 4)
        e.transpose(g["dps"][0:128, 0:128], g["ident"][:], g["ident"][:]).then_inc(bar, 1)
        e.wait_ge(bar, target)

    @P.seg("vector")
    def _(e):
        e.tensor_copy(out=g["drv"][:], in_=P.spc[:, 0:1]).then_inc(bar, 1)
        e.wait_ge(bar, target)

    @P.seg("scalar")
    def _(e):
        e.copy(out=g["drs"][:], in_=P.spc2[:, 0:1]).then_inc(bar, 1)
        e.wait_ge(bar, target)

    @P.seg("gpsimd")
    def _(e):
        e.tensor_copy(out=g["drg"][:], in_=g["drg"][:]).then_inc(bar, 1)
        e.wait_ge(bar, target)


def emit_p0_consts(P, t_in, g):
    w = {}

    @P.seg("sync")
    def _(e):
        w["c"] = P.dma(e, g["ident"][:], t_in["ident"], "d_s")

    @P.seg("vector")
    def _(e):
        P.wge(e, w["c"])

        def init(dst, val):
            ii = g["ident"][0:dst.shape[0], 0:dst.shape[1]]
            e.tensor_scalar(out=dst, in0=ii, scalar1=0.0, scalar2=val,
                            op0=ALU.mult, op1=ALU.add)

        init(P.spc[:, 0:32], 1.0)
        ins = e.tensor_scalar(out=P.spc[:, 32:64], in0=g["ident"][:, 32:64],
                              scalar1=0.0, scalar2=1.0, op0=ALU.mult, op1=ALU.add)
        n = P.ev(ins, "vsq")
        e.wait_ge(P.sem("vsq"), n)
        init(g["sig"][:], 0.75)
        init(g["ones_col"][:], 1.0)
        init(g["ones_row"][:], 1.0)
        init(g["cs"][:], ISQ_HI)
        init(g["scalars"][:], 0.0)
        init(g["mk2"][:], 0.0)
        P.vbar(e)
        e.tensor_scalar(out=g["sig"][:], in0=g["sig"][:], scalar1=M64,
                        scalar2=None, op0=ALU.add, op1=ALU.bypass)
        e.tensor_scalar(out=g["cs"][:], in0=g["cs"][:], scalar1=ISQ_LO,
                        scalar2=None, op0=ALU.add, op1=ALU.bypass)
        P.vbar(e)
        e.tensor_scalar(out=g["sig"][:], in0=g["sig"][:], scalar1=M64,
                        scalar2=None, op0=ALU.subtract, op1=ALU.bypass)
        P.vbar(e)
        e.tensor_scalar(out=g["meff"][:], in0=g["sig"][:], scalar1=(M64 - M32),
                        scalar2=M32, op0=ALU.mult, op1=ALU.add)
        e.tensor_copy(out=g["identb"][:], in_=g["ident"][:])
        P.vfence(e, "v_ev")

    @P.seg("scalar")
    def _(e):
        P.wge(e, w["c"])
        e.activation(out=P.spc2[:, 0:8], in_=g["ident"][:, 0:8], func=AF.Copy,
                     bias=1.0, scale=0.0)
        ins = e.activation(out=P.spc2[:, 8:16], in_=g["ident"][:, 8:16],
                           func=AF.Copy, bias=1.0, scale=0.0)
        n = P.ev(ins, "ssq")
        e.wait_ge(P.sem("ssq"), n)

    @P.seg("gpsimd")
    def _(e):
        P.wge(e, w["c"])
        ins = e.tensor_scalar(out=g["drg"][:], in0=g["ident"][:, 0:1], scalar1=0.0,
                              scalar2=1.0, op0=ALU.mult, op1=ALU.add)
        n = P.ev(ins, "gsq")
        e.wait_ge(P.sem("gsq"), n)


def emit_p1_weight_prep(P, t_in, t_scr, g):
    nc = P.nc
    with ExitStack() as ctx:
        sbf = lambda n, s, d=F32: ctx.enter_context(nc.sbuf_tensor(n, s, d))
        land = sbf("p1_land", [128, H])
        land2 = sbf("p1_land2", [128, H])
        stage_h = sbf("p1_sh", [128, 2 * D], BF16)
        stage_l = sbf("p1_sl", [128, 2 * D], BF16)
        parts = sbf("p1_parts", [128, 4])
        P.dps2 = ctx.enter_context(nc.psum_tensor("p1_fence", [128, 512], F32))
        waste = sbf("p1_waste", [128, H])
        accs = sbf("p1_accs", [128, 8])
        ubuf = sbf("p1_u", [128, 2 * D])
        tstage = sbf("p1_t", [128, 2 * D], BF16)

        v_ev, t_ev = P.sem("v_ev"), P.sem("t_ev")

        ds0 = P.idx.get("d_s", 0)
        dg0 = P.idx.get("d_g", 0)
        v0 = P.cnt["v_ev"]
        v1 = v0 + 97
        v4 = v1 + 32
        t0 = P.cnt["t_ev"]

        @P.seg("vector")
        def _(e):
            e.tensor_scalar(out=accs[:], in0=g["ident"][:, 0:8], scalar1=0.0,
                            scalar2=None, op0=ALU.mult, op1=ALU.bypass)
            e.tensor_scalar(out=parts[:], in0=g["ident"][:, 0:4], scalar1=0.0,
                            scalar2=None, op0=ALU.mult, op1=ALU.bypass)
            P.vbar(e)
            for c in range(96):
                w3c = c >= 32
                buf = land if c % 2 == 0 else land2
                width = D if w3c else H
                mm = 2 if w3c else c // 16
                P.wge(e, P.tok("d_s", ds0 + c))
                e.tensor_scalar(out=waste[:, 0:width], in0=buf[:, 0:width],
                                scalar1=0.0, scalar2=None,
                                op0=ALU.max, op1=ALU.add,
                                accum_out=parts[:, 0:1])
                P.vbar(e)
                ins = e.tensor_scalar(out=buf[:, 0:width], in0=buf[:, 0:width],
                                      scalar1=0.0, scalar2=None,
                                      op0=ALU.min, op1=ALU.add,
                                      accum_out=parts[:, 1:2])
                P.ev(ins, "v_ev")
                P.vbar(e)
                e.tensor_tensor(out=accs[:, mm:mm + 1], in0=accs[:, mm:mm + 1],
                                in1=parts[:, 0:1], op=ALU.add)
                e.tensor_tensor(out=accs[:, 4 + mm:5 + mm], in0=accs[:, 4 + mm:5 + mm],
                                in1=parts[:, 1:2], op=ALU.add)
                P.vbar(e)
            e.tensor_tensor(out=accs[:, 0:3], in0=accs[:, 0:3],
                            in1=accs[:, 4:7], op=ALU.subtract)
            P.vfence(e, "v_ev")
            assert P.cnt["v_ev"] == v1

            e.wait_ge(t_ev, t0 + 1)
            sc = g["scalars"]
            e.tensor_scalar(out=sc[0:1, 0:3], in0=g["dps"][0:1, 0:3],
                            scalar1=1.0 / (D * H), scalar2=None,
                            op0=ALU.mult, op1=ALU.bypass)
            e.tensor_scalar(out=sc[0:1, 21:22], in0=g["ident"][0:1, 0:1],
                            scalar1=0.0, scalar2=A05_HI, op0=ALU.mult, op1=ALU.add)
            P.vbar(e)
            e.tensor_scalar(out=sc[0:1, 3:6], in0=sc[0:1, 0:3],
                            scalar1=-1.0, scalar2=1.0,
                            op0=ALU.mult, op1=ALU.add)
            e.tensor_scalar(out=sc[0:1, 21:22], in0=sc[0:1, 21:22],
                            scalar1=A05_LO, scalar2=None, op0=ALU.add, op1=ALU.bypass)
            e.tensor_scalar(out=sc[0:1, 18:21], in0=sc[0:1, 0:3],
                            scalar1=1e-8, scalar2=None, op0=ALU.add, op1=ALU.bypass)
            P.vbar(e)
            e.reciprocal(out=sc[0:1, 6:9], in_=sc[0:1, 3:6])
            P.vbar(e)
            e.tensor_tensor(out=sc[0:1, 9:12], in0=sc[0:1, 6:9],
                            in1=sc[0:1, 0:3], op=ALU.mult)
            e.tensor_scalar(out=sc[0:1, 24:27], in0=sc[0:1, 18:21],
                            scalar1=sc[0:1, 21:22], scalar2=None,
                            op0=ALU.mult, op1=ALU.bypass)
            P.vbar(e)
            e.tensor_scalar(out=sc[0:1, 9:12], in0=sc[0:1, 9:12],
                            scalar1=2.0, scalar2=None, op0=ALU.mult, op1=ALU.bypass)
            P.vbar(e)
            e.tensor_scalar(out=sc[0:1, 12:15], in0=sc[0:1, 9:12],
                            scalar1=0.2, scalar2=-0.25, op0=ALU.mult, op1=ALU.add)
            P.vbar(e)
            for coef in (1.0 / 3.0, -0.5, 1.0):
                e.tensor_tensor(out=sc[0:1, 12:15], in0=sc[0:1, 12:15],
                                in1=sc[0:1, 9:12], op=ALU.mult)
                P.vbar(e)
                e.tensor_scalar(out=sc[0:1, 12:15], in0=sc[0:1, 12:15],
                                scalar1=coef, scalar2=None, op0=ALU.add, op1=ALU.bypass)
                P.vbar(e)
            e.tensor_tensor(out=sc[0:1, 15:18], in0=sc[0:1, 12:15],
                            in1=sc[0:1, 9:12], op=ALU.mult)
            P.vbar(e)
            e.tensor_scalar(out=sc[0:1, 15:18], in0=sc[0:1, 15:18],
                            scalar1=0.5, scalar2=None, op0=ALU.mult, op1=ALU.bypass)
            P.vfence(e, "v_ev")

            e.wait_ge(t_ev, t0 + 2)
            for m, n in enumerate(("a1", "a2", "a3")):
                e.tensor_copy(out=g[n][:], in_=g["dps"][:, 15 + m:16 + m])
            for m, n in enumerate(("bnd1", "bnd2", "bnd3")):
                e.tensor_copy(out=g[n][:], in_=g["dps"][:, 24 + m:25 + m])
            for m, n in enumerate(("nbnd1", "nbnd2", "nbnd3")):
                e.tensor_scalar(out=g[n][:], in0=g["dps"][:, 24 + m:25 + m],
                                scalar1=-1.0, scalar2=None,
                                op0=ALU.mult, op1=ALU.bypass)
            P.vfence(e, "v_ev")

        v1b = v1 + 2
        v4 = v1b + 32

        @P.seg("vector")
        def _(e):
            for c in range(32):
                buf = land if c % 2 == 0 else land2
                p = c % 2
                P.wge(e, P.tok("d_s", ds0 + 96 + c))
                if c >= 2:
                    P.wge(e, P.tok("d_g", dg0 + 2 * (c - 2)))
                    P.wge(e, P.tok("d_g", dg0 + 2 * (c - 2) + 1))
                e.tensor_copy(out=stage_h[:, p * D:(p + 1) * D], in_=buf[:, 0:D])
                P.vbar(e)
                ins = e.tensor_tensor(out=stage_l[:, p * D:(p + 1) * D],
                                      in0=buf[:, 0:D],
                                      in1=stage_h[:, p * D:(p + 1) * D],
                                      op=ALU.subtract)
                P.ev(ins, "v_ev")
            assert P.cnt["v_ev"] == v4, (P.cnt["v_ev"], v4)

        @P.seg("vector")
        def _(e):
            for c in range(64):
                sl = c % 4
                buf = (land, land2)[sl // 2]
                off = (sl % 2) * D
                p = c % 2
                P.wge(e, P.tok("d_s", ds0 + 128 + c))
                if c >= 2:
                    P.wge(e, P.tok("d_g", dg0 + 64 + (c - 2)))
                e.tensor_scalar(out=ubuf[:, p * D:(p + 1) * D],
                                in0=buf[:, off:off + D],
                                scalar1=g["nbnd3"][:], scalar2=1.0,
                                op0=ALU.is_ge, op1=ALU.subtract)
                P.vbar(e)
                ins = e.scalar_tensor_tensor(out=tstage[:, p * D:(p + 1) * D],
                                             in0=buf[:, off:off + D],
                                             scalar=g["bnd3"][:],
                                             in1=ubuf[:, p * D:(p + 1) * D],
                                             op0=ALU.is_ge, op1=ALU.add)
                P.ev(ins, "v_ev")
            assert P.cnt["v_ev"] == v4 + 64

        @P.seg("tensor")
        def _(e):
            e.wait_ge(v_ev, v1)
            e.matmul(g["dps"][0:1, 0:3], g["ones_col"][:],
                     accs[:, 0:3], start=True, stop=True)
            P.tfence(e, "t_ev")
            e.wait_ge(v_ev, v1 + 1)
            e.matmul(g["dps"][:, 15:27], g["ones_row"][:],
                     g["scalars"][0:1, 15:27], start=True, stop=True)
            P.tfence(e, "t_ev")

        @P.seg("gpsimd")
        def _(e):
            for c in range(32):
                p = c % 2
                e.wait_ge(v_ev, v1 + 2 + c + 1)
                r = c % 16
                for hl, stg in ((0, stage_h), (1, stage_l)):
                    src = stg[:, p * D:(p + 1) * D]
                    if c < 16:
                        dst = t_scr["wq_hl"][hl, :, r * 128:(r + 1) * 128, :]
                        P.dma(e, dst.rearrange("h p m -> p h m"),
                              src.rearrange("p (h m) -> p h m", h=16), "d_g")
                    else:
                        dst = t_scr["wo_hl"][hl, :, r * 128:(r + 1) * 128, :]
                        P.dma(e, dst.rearrange("a p b -> p a b"),
                              src.rearrange("p (a b) -> p a b", a=4), "d_g")
            for c in range(64):
                p = c % 2
                e.wait_ge(v_ev, v4 + c + 1)
                P.dma(e, t_scr["t3"][:, c * 128:(c + 1) * 128, :]
                      .rearrange("a p d -> p a d"),
                      tstage[:, p * D:(p + 1) * D]
                      .rearrange("p (a d) -> p a d", a=16), "d_g")

        @P.seg("sync")
        def _(e):
            mats = [t_in["w1T"], t_in["w2T"]]
            for c in range(96):
                if c >= 2:
                    e.wait_ge(v_ev, v0 + c - 1)
                buf = land if c % 2 == 0 else land2
                if c < 32:
                    m, r = divmod(c, 16)
                    P.dma(e, buf[:, 0:H], mats[m][r * 128:(r + 1) * 128, 0:H], "d_s")
                else:
                    r = c - 32
                    P.dma(e, buf[:, 0:D], t_in["w3T"][r * 128:(r + 1) * 128, 0:D], "d_s")
            for c in range(32):
                if c >= 2:
                    e.wait_ge(v_ev, v1 + 2 + c - 1)
                else:
                    e.wait_ge(v_ev, v0 + 96)
                src = t_in["Wq"] if c < 16 else t_in["Wo"]
                r = c % 16
                buf = land if c % 2 == 0 else land2
                P.dma(e, buf[:, 0:D], src[r * 128:(r + 1) * 128, 0:D], "d_s")
            for c in range(64):
                if c >= 4:
                    e.wait_ge(v_ev, v4 + c - 3)
                else:
                    e.wait_ge(v_ev, v4)
                sl = c % 4
                buf = (land, land2)[sl // 2]
                off = (sl % 2) * D
                P.dma(e, buf[:, off:off + D],
                      t_in["w3T"][c * 128:(c + 1) * 128, 0:D], "d_s")


def emit_debug_dump(P, g, t_scr):
    cols = [("sig", g["sig"]), ("meff", g["meff"]), ("cs", g["cs"]),
            ("a1", g["a1"]), ("a2", g["a2"]), ("a3", g["a3"]),
            ("bnd1", g["bnd1"]), ("bnd2", g["bnd2"]), ("bnd3", g["bnd3"]),
            ("nbnd1", g["nbnd1"]), ("nbnd2", g["nbnd2"]), ("nbnd3", g["nbnd3"])]

    @P.seg("gpsimd")
    def _(e):
        with P.nc.allow_non_contiguous_dma(reason="debug dump"):
            for i, (n, t) in enumerate(cols):
                tk = P.dma(e, t_scr["stats"][:, i:i + 1], t[:], "d_g")
                P.wge(e, tk)
            tk = P.dma(e, t_scr["stats"][0:1, 32:48][0:1, 0:16], g["scalars"][0:1, 0:16], "d_g")
            P.wge(e, tk)
        for s in range(P.NDMA):
            nm = f"d_g{s}"
            if nm in P.cnt:
                e.wait_ge(P.sems[nm], P.cnt[nm])


def emit_kv_build(P, t_in, g, kv, mega):
    nc = P.nc
    with ExitStack() as ctx:
        sbf = lambda n, s, d=F32: ctx.enter_context(nc.sbuf_tensor(n, s, d))
        wland = sbf(f"kv_wl{mega}", [128, 4096])
        wkhi = sbf(f"kv_wkhi{mega}", [128, H], BF16)
        wklo = sbf(f"kv_wklo{mega}", [128, H], BF16)
        wvhi = sbf(f"kv_wvhi{mega}", [128, H], BF16)
        wvlo = sbf(f"kv_wvlo{mega}", [128, H], BF16)
        xland = sbf(f"kv_xl{mega}", [128, 4096])
        xhi = sbf(f"kv_xhi{mega}", [128, 4096], BF16)
        xlo = sbf(f"kv_xlo{mega}", [128, 4096], BF16)
        ksq = sbf(f"kv_ksq{mega}", [128, 2 * 256])
        vt = sbf(f"kv_vt{mega}", [128, 2 * 256])
        kps = [ctx.enter_context(nc.psum_tensor(f"kv_kp{mega}_{i}", [128, 512], F32))[:, 0:256] for i in range(2)]
        vps = [ctx.enter_context(nc.psum_tensor(f"kv_vp{mega}_{i}", [128, 512], F32))[:, 0:256] for i in range(2)]
        tps = [ctx.enter_context(nc.psum_tensor(f"kv_tp{mega}_{i}", [128, 512], F32))[:, 0:256] for i in range(2)]

        v_ev, s_ev, t_ev = P.sem("v_ev"), P.sem("s_ev"), P.sem("t_ev")

        ds0 = P.idx.get("d_s", 0)
        v0 = P.cnt["v_ev"]
        s0 = P.cnt["s_ev"]
        t0 = P.cnt["t_ev"]
        scol0 = 1024 * mega

        VSPLIT = {}
        VKLO = {}
        VKSQ = {}
        VVT = {}
        VVFIN = {}
        VMK = {}
        SKHI = {idx: s0 + 2 * idx + 1 for idx in range(32)}
        SKSQ = {idx: s0 + 2 * idx + 2 for idx in range(32)}
        TK = {idx: t0 + idx * 4 + 1 for idx in range(32)}
        TV = {idx: t0 + idx * 4 + 2 for idx in range(32)}
        TVT = {idx: t0 + idx * 4 + 3 for idx in range(32)}
        TNORM = {idx: t0 + idx * 4 + 4 for idx in range(32)}
        P.cnt["s_ev"] += 64
        P.cnt["t_ev"] += 128

        @P.seg("vector")
        def _(e):
            for hidx, (whi, wlo) in enumerate(((wkhi, wklo), (wkhi, wklo),
                                               (wvhi, wvlo), (wvhi, wvlo))):
                hf = hidx % 2
                sl = slice(hf * 4096, (hf + 1) * 4096)
                P.wge(e, P.tok("d_s", ds0 + hidx))
                e.tensor_copy(out=whi[:, sl], in_=wland[:])
                P.vbar(e)
                ins = e.tensor_tensor(out=wlo[:, sl], in0=wland[:],
                                      in1=whi[:, sl], op=ALU.subtract)
                VSPLIT["w%d" % hidx] = P.ev(ins, "v_ev")
            VSPLIT["v"] = VSPLIT["w3"]

        @P.seg("vector")
        def _(e):
            for c in range(8):
                xsl = slice(0, 4096)
                P.wge(e, P.tok("d_s", ds0 + 4 + c))
                if c >= 1:
                    e.wait_ge(t_ev, TV[(c - 1) * 4 + 3])
                e.tensor_copy(out=xhi[:, xsl], in_=xland[:, xsl])
                P.vbar(e)
                ins = e.tensor_tensor(out=xlo[:, xsl], in0=xland[:, xsl],
                                      in1=xhi[:, xsl], op=ALU.subtract)
                VSPLIT[c] = P.ev(ins, "v_ev")
                for gg in range(4):
                    idx = c * 4 + gg
                    pb = idx % 2
                    off = 2048 * gg + 256 * c
                    if KV_DO_K:
                        e.wait_ge(t_ev, TK[idx])
                        P.vbar(e)
                        e.wait_ge(s_ev, SKSQ[idx])
                        ins = e.tensor_tensor(out=kv["k_lo"][:, off:off + 256],
                                              in0=kps[pb][:],
                                              in1=kv["k_hi"][:, off:off + 256],
                                              op=ALU.subtract)
                        VKLO[idx] = P.ev(ins, "v_ev")
                    else:
                        VKLO[idx] = P.vfence(e, "v_ev")
                    if KV_DO_V:
                        e.wait_ge(t_ev, TV[idx])
                        P.vbar(e)
                        ins = e.tensor_copy(out=vt[:, pb * 256:(pb + 1) * 256],
                                            in_=vps[pb][:])
                        VVT[idx] = P.ev(ins, "v_ev")
                        e.wait_ge(t_ev, TVT[idx])
                        P.vbar(e)
                        ins = e.tensor_copy(out=kv["v"][:, off:off + 256], in_=tps[pb][:])
                        VVFIN[idx] = P.ev(ins, "v_ev")
                    else:
                        VVT[idx] = P.vfence(e, "v_ev")
                        VVFIN[idx] = P.vfence(e, "v_ev")
                    if not KV_DO_NORM:
                        VMK[idx] = P.vfence(e, "v_ev")
                        continue
                    e.wait_ge(t_ev, TNORM[idx])
                    P.vbar(e)
                    e.reduce_max(out=g["scalars"][0:1, 28 + pb:29 + pb],
                                 in_=g["dps"][0:1, 0:256],
                                 axis=mybir.AxisListType.X)
                    P.vbar(e)
                    ins = e.tensor_tensor(out=g["mk2"][0:1, gg:gg + 1],
                                          in0=g["mk2"][0:1, gg:gg + 1],
                                          in1=g["scalars"][0:1, 28 + pb:29 + pb],
                                          op=ALU.max)
                    VMK[idx] = P.ev(ins, "v_ev")

        @P.seg("scalar")
        def _(e):
            for c in range(8):
                for gg in range(4):
                    idx = c * 4 + gg
                    pb = idx % 2
                    off = 2048 * gg + 256 * c
                    if KV_DO_K:
                        e.wait_ge(t_ev, TK[idx])
                        P.sbar(e)
                        e.copy(out=kv["k_hi"][:, off:off + 256],
                               in_=kps[pb][:]).then_inc(s_ev, 1)
                        e.activation(out=ksq[:, pb * 256:(pb + 1) * 256],
                                     in_=kps[pb][:], func=AF.Square).then_inc(s_ev, 1)
                    else:
                        P.sbar(e)
                        e.copy(out=ksq[:, 0:16], in_=P.spc2[:]).then_inc(s_ev, 1)
                        e.copy(out=ksq[:, 0:16], in_=P.spc2[:]).then_inc(s_ev, 1)

        @P.seg("tensor")
        def _(e):
            e.wait_ge(P.sems["v_ev"], VSPLIT["v"])
            for c in range(8):
                xsl0 = 0
                for gg in range(4):
                    idx = c * 4 + gg
                    pb = idx % 2
                    if gg == 0:
                        e.wait_ge(P.sems["v_ev"], VSPLIT[c])
                    if not KV_DO_K:
                        for _ in range(4 if not KV_DO_V else 2):
                            e.transpose(tps[pb][:, 0:128], g["ident"][:],
                                        g["ident"][:]).then_inc(t_ev, 1)
                        continue
                    if idx >= 2:
                        e.wait_ge(P.sems["v_ev"], VKLO[idx - 2])
                        e.wait_ge(P.sems["s_ev"], SKSQ[idx - 2])
                    last = None
                    for ec in range(16):
                        lh = wkhi[:, ec * 512 + gg * 128: ec * 512 + gg * 128 + 128]
                        ll = wklo[:, ec * 512 + gg * 128: ec * 512 + gg * 128 + 128]
                        rh = xhi[:, xsl0 + ec * 256: xsl0 + (ec + 1) * 256]
                        rl = xlo[:, xsl0 + ec * 256: xsl0 + (ec + 1) * 256]
                        e.matmul(kps[pb][:], lh, rh, start=(ec == 0), stop=False)
                        e.matmul(kps[pb][:], lh, rl, start=False, stop=False)
                        last = e.matmul(kps[pb][:], ll, rh, start=False,
                                        stop=(ec == 15))
                    last.then_inc(t_ev, 1)
                    if not KV_DO_V:
                        e.transpose(tps[pb][:, 0:128], g["ident"][:], g["ident"][:]).then_inc(t_ev, 1)
                        e.transpose(tps[pb][:, 0:128], g["ident"][:], g["ident"][:]).then_inc(t_ev, 1)
                        if KV_DO_NORM:
                            e.wait_ge(P.sems["s_ev"], SKSQ[idx])
                            if idx >= 1:
                                e.wait_ge(P.sems["v_ev"], VMK[idx - 1])
                            ins = e.matmul(g["dps"][0:1, 0:256], g["ones_col"][:],
                                           ksq[:, pb * 256:(pb + 1) * 256],
                                           start=True, stop=True)
                            ins.then_inc(t_ev, 1)
                        else:
                            e.transpose(tps[pb][:, 0:128], g["ident"][:], g["ident"][:]).then_inc(t_ev, 1)
                        continue
                    if idx >= 2:
                        e.wait_ge(P.sems["v_ev"], VVT[idx - 2])
                    for ec in range(16):
                        lh = wvhi[:, ec * 512 + gg * 128: ec * 512 + gg * 128 + 128]
                        ll = wvlo[:, ec * 512 + gg * 128: ec * 512 + gg * 128 + 128]
                        rh = xhi[:, xsl0 + ec * 256: xsl0 + (ec + 1) * 256]
                        rl = xlo[:, xsl0 + ec * 256: xsl0 + (ec + 1) * 256]
                        e.matmul(vps[pb][:], lh, rh, start=(ec == 0), stop=False)
                        e.matmul(vps[pb][:], lh, rl, start=False, stop=False)
                        last = e.matmul(vps[pb][:], ll, rh, start=False,
                                        stop=(ec == 15))
                    last.then_inc(t_ev, 1)
                    e.wait_ge(P.sems["v_ev"], VVT[idx])
                    if idx >= 2:
                        e.wait_ge(P.sems["v_ev"], VVFIN[idx - 2])
                    e.transpose(tps[pb][:, 0:128], vt[:, pb * 256:pb * 256 + 128],
                                g["ident"][:])
                    ins = e.transpose(tps[pb][:, 128:256],
                                      vt[:, pb * 256 + 128:pb * 256 + 256],
                                      g["ident"][:])
                    ins.then_inc(t_ev, 1)
                    e.wait_ge(P.sems["s_ev"], SKSQ[idx])
                    if idx >= 1:
                        e.wait_ge(P.sems["v_ev"], VMK[idx - 1])
                    ins = e.matmul(g["dps"][0:1, 0:256], g["ones_col"][:],
                                   ksq[:, pb * 256:(pb + 1) * 256],
                                   start=True, stop=True)
                    ins.then_inc(t_ev, 1)

        @P.seg("sync")
        def _(e):
            for hidx, wsrc in enumerate((t_in["Wk"], t_in["Wk"],
                                         t_in["Wv"], t_in["Wv"])):
                if hidx >= 1:
                    e.wait_ge(P.sems["v_ev"], VSPLIT["w%d" % (hidx - 1)])
                hf = hidx % 2
                P.dma(e, wland[:].rearrange("p (a d) -> p a d", a=8),
                      wsrc[hf * 1024:(hf + 1) * 1024].rearrange(
                          "(a p) d -> p a d", p=128), "d_s")
            for c in range(8):
                if c == 0:
                    e.wait_ge(P.sems["v_ev"], VSPLIT["w3"])
                else:
                    e.wait_ge(P.sems["v_ev"], VSPLIT[c - 1])
                col = scol0 + c * 256
                P.dma(e, xland[:].rearrange("p (a s) -> p a s", a=16),
                      t_in["xT"][:, col:col + 256].rearrange("(a p) s -> p a s", p=128),
                      "d_s")


def emit_attention_qb(P, t_in, t_scr, g, kv, mega, qb):
    nc = P.nc
    j = qb - 2 * mega
    v_ev, s_ev, t_ev = P.sem("v_ev"), P.sem("s_ev"), P.sem("t_ev")

    with ExitStack() as qctx:
        sbq = lambda n, s, d=F32: qctx.enter_context(nc.sbuf_tensor(f"{n}_{qb}", s, d))
        att_hi = sbq("at_hi", [128, 16 * 512], BF16)
        att_lo = sbq("at_lo", [128, 16 * 512], BF16)
        S_a = qctx.enter_context(nc.psum_tensor(f"ps_sa{qb}", [128, 1536], F32))
        S_b = qctx.enter_context(nc.psum_tensor(f"ps_sb{qb}", [128, 1536], F32))
        G = qctx.enter_context(nc.psum_tensor(f"ps_g{qb}", [128, 512], F32))
        SPS = [S_a, S_b]

        ev = {}

        with ExitStack() as actx:
            sba = lambda n, s, d=F32: actx.enter_context(nc.sbuf_tensor(f"{n}_{qb}", s, d))
            xqhi = sba("xqhi", [128, 8192], BF16)
            xqlo = sba("xqlo", [128, 8192], BF16)
            xql_ctx = ExitStack()
            xql = xql_ctx.enter_context(nc.sbuf_tensor(f"xql_{qb}", [128, 4096], F32))

            qc = (4 * qb + 8) * 128

            @P.seg("sync")
            def _(e):
                ev["xq_dma0"] = P.dma(
                    e, xql[:].rearrange("p (a s) -> p a s", a=8),
                    t_in["xT"][0:1024, qc:qc + 512].rearrange("(a p) s -> p a s", p=128),
                    "d_s")

            def xqsplit_half(half, key, dkey):
                @P.seg("vector")
                def _(e):
                    P.wge(e, ev[dkey])
                    sl = slice(half * 4096, (half + 1) * 4096)
                    e.tensor_copy(out=xqhi[:, sl], in_=xql[:])
                    P.vbar(e)
                    ins = e.tensor_tensor(out=xqlo[:, sl], in0=xql[:],
                                          in1=xqhi[:, sl], op=ALU.subtract)
                    ev[key] = P.ev(ins, "v_ev")

            xqsplit_half(0, "xqsplit0", "xq_dma0")

            @P.seg("sync")
            def _(e):
                e.wait_ge(P.sems["v_ev"], ev["xqsplit0"])
                ev["xq_dma1"] = P.dma(
                    e, xql[:].rearrange("p (a s) -> p a s", a=8),
                    t_in["xT"][1024:2048, qc:qc + 512].rearrange("(a p) s -> p a s", p=128),
                    "d_s")

            xqsplit_half(1, "xqsplit", "xq_dma1")
            xql_ctx.close()

            wqh = [sba(f"wqh{k}", [128, 2048], BF16) for k in range(2)]
            wql = [sba(f"wql{k}", [128, 2048], BF16) for k in range(2)]
            qth = sba("qth", [128, 512], BF16)
            qtl = sba("qtl", [128, 512], BF16)
            p_sb = sba("p_sb", [128, 2 * 1536])
            pt_sb = sba("pt_sb", [128, 12 * 512])
            mask_sb = sba("mask_sb", [128, 4 * 1536], BF16)
            qsq = sba("qsq", [128, 512])
            rrow = sba("rrow", [1, 512])
            recipb = sba("recipb", [128, 512])
            attn_sb = sba("attn_sb", [128, 512])
            rows = sba("rows", [128, 16])

            @P.seg("gpsimd")
            def _(e):
                ev["mask_dma"] = [
                    P.dma(e, mask_sb[:, i * 1536:(i + 1) * 1536],
                          t_in["masks"][4 * qb + i], "d_g")
                    for i in range(4)]

            for h in range(16):
                gg = h // 4
                kvo = gg * 2048 + 4 * j * 128

                @P.seg("sync")
                def _(e):
                    if h >= 2:
                        e.wait_ge(t_ev, ev[("qp", h - 2)])
                    elif h == 0:
                        e.wait_ge(v_ev, ev["xqsplit"])
                    toks = []
                    for hl, buf in ((0, wqh[h % 2]), (1, wql[h % 2])):
                        toks.append(P.dma(
                            e, buf[:].rearrange("p (a m) -> p a m", a=16),
                            t_scr["wq_hl"][hl, h].rearrange("(a p) m -> p a m", p=128),
                            "d_s"))
                    ev[("wq", h)] = toks

                @P.seg("tensor")
                def _(e):
                    for t in ev[("wq", h)]:
                        P.wge(e, t)
                    if h == 0:
                        e.wait_ge(v_ev, ev["xqsplit"])
                    else:
                        e.wait_ge(v_ev, ev[("attn", h - 1)])
                    last = None
                    for ec in range(16):
                        lh = wqh[h % 2][:, ec * 128:(ec + 1) * 128]
                        ll = wql[h % 2][:, ec * 128:(ec + 1) * 128]
                        rh = xqhi[:, ec * 512:(ec + 1) * 512]
                        rl = xqlo[:, ec * 512:(ec + 1) * 512]
                        e.matmul(G[:], lh, rh, start=(ec == 0), stop=False)
                        e.matmul(G[:], lh, rl, start=False, stop=False)
                        last = e.matmul(G[:], ll, rh, start=False, stop=(ec == 15))
                    ev[("qp", h)] = P.ev(last, "t_ev")

                @P.seg("scalar")
                def _(e):
                    e.wait_ge(t_ev, ev[("qp", h)])
                    P.sbar(e)
                    ins = e.copy(out=qth[:], in_=G[:])
                    ev[("qh", h)] = P.ev(ins, "s_ev")
                    ins = e.activation(out=qsq[:], in_=G[:], func=AF.Square)
                    ev[("qsq", h)] = P.ev(ins, "s_ev")

                @P.seg("vector")
                def _(e):
                    e.wait_ge(s_ev, ev[("qsq", h)])
                    ins = e.tensor_tensor(out=qtl[:], in0=G[:], in1=qth[:],
                                          op=ALU.subtract)
                    ev[("ql", h)] = P.ev(ins, "v_ev")

                @P.seg("tensor")
                def _(e):
                    e.wait_ge(s_ev, ev[("qsq", h)])
                    if h >= 1:
                        e.wait_ge(v_ev, ev[("attn", h - 1)])
                    ins = e.matmul(g["dps"][0:1, 0:512], g["ones_col"][:],
                                   qsq[:], start=True, stop=True)
                    ev[("qsqm", h)] = P.ev(ins, "t_ev")

                @P.seg("vector")
                def _(e):
                    e.wait_ge(t_ev, ev[("qsqm", h)])
                    P.vbar(e)
                    e.reduce_max(out=rows[0:1, 9:10], in_=g["dps"][0:1, 0:512],
                                 axis=mybir.AxisListType.X)
                    P.vbar(e)
                    e.tensor_tensor(out=rows[0:1, 10:11], in0=rows[0:1, 9:10],
                                    in1=g["mk2"][0:1, gg:gg + 1], op=ALU.add)
                    P.vbar(e)
                    e.tensor_scalar(out=rows[0:1, 11:12], in0=rows[0:1, 10:11],
                                    scalar1=g["cs"][0:1, 0:1], scalar2=-0.5,
                                    op0=ALU.mult, op1=ALU.mult)
                    ev[("bias1", h)] = P.vfence(e, "v_ev")

                @P.seg("tensor")
                def _(e):
                    e.wait_ge(v_ev, ev[("bias1", h)])
                    ins = e.matmul(g["dps"][:, 0:1], g["ones_row"][:],
                                   rows[0:1, 11:12], start=True, stop=True)
                    ev[("bbc", h)] = P.ev(ins, "t_ev")

                @P.seg("vector")
                def _(e):
                    e.wait_ge(t_ev, ev[("bbc", h)])
                    P.vbar(e)
                    e.tensor_copy(out=rows[:, 8:9], in_=g["dps"][:, 0:1])
                    ev[("bias", h)] = P.vfence(e, "v_ev")

                def sc(i):
                    si = h * 4 + i

                    @P.seg("tensor")
                    def _(e):
                        if si >= 2:
                            e.wait_ge(v_ev, ev[("ptev", si - 2)])
                        e.wait_ge(v_ev, ev[("ql", h)])
                        S = SPS[si % 2]
                        for li, (lt, rsel) in enumerate(
                                ((qth, "h"), (qth, "l"), (qtl, "h"))):
                            for c in range(3):
                                rh = (kv["k_hi"] if rsel == "h" else kv["k_lo"])
                                rhs = rh[:, kvo + c * 512:kvo + (c + 1) * 512]
                                ins = e.matmul(S[:, c * 512:(c + 1) * 512],
                                               lt[:, i * 128:(i + 1) * 128], rhs,
                                               start=(li == 0), stop=(li == 2))
                        ev[("sc", si)] = P.ev(ins, "t_ev")

                    @P.seg("vector")
                    def _(e):
                        e.wait_ge(t_ev, ev[("sc", si)])
                        if h == 0:
                            for t in ev["mask_dma"]:
                                P.wge(e, t)
                        S = SPS[si % 2]
                        P.vbar(e)
                        for c in range(3):
                            ins = e.tensor_tensor(
                                out=S[:, c * 512:(c + 1) * 512],
                                in0=S[:, c * 512:(c + 1) * 512],
                                in1=mask_sb[:, i * 1536 + c * 512:i * 1536 + (c + 1) * 512],
                                op=ALU.add)
                        ev[("mask", si)] = P.ev(ins, "v_ev")

                    @P.seg("scalar")
                    def _(e):
                        e.wait_ge(v_ev, ev[("mask", si)])
                        e.wait_ge(v_ev, ev[("bias", h)])
                        if si >= 2:
                            e.wait_ge(t_ev, ev[("pt", si - 2)])
                        psl = slice((si % 2) * 1536, (si % 2 + 1) * 1536)
                        ins = e.activation(out=p_sb[:, psl], in_=SPS[si % 2][:],
                                           func=AF.Exp, bias=rows[:, 8:9],
                                           scale=g["cs"][:],
                                           accum_out=rows[:, i:i + 1])
                        P.ev(ins, "s_ev")
                        ev[("exp", si)] = P.sfence(e, "s_ev")

                    @P.seg("vector")
                    def _(e):
                        e.wait_ge(s_ev, ev[("exp", si)])
                        P.vbar(e)
                        e.reciprocal(out=rows[:, 4 + i:5 + i], in_=rows[:, i:i + 1])

                def pt(i):
                    si = h * 4 + i

                    @P.seg("tensor")
                    def _(e):
                        e.wait_ge(s_ev, ev[("exp", si)])
                        S = SPS[si % 2]
                        po = (si % 2) * 1536
                        for k in range(12):
                            ins = e.transpose(S[:, k * 128:(k + 1) * 128],
                                              p_sb[:, po + k * 128:po + (k + 1) * 128],
                                              g["ident"][:])
                        ev[("pt", si)] = P.ev(ins, "t_ev")

                    @P.seg("vector")
                    def _(e):
                        e.wait_ge(t_ev, ev[("pt", si)])
                        P.vbar(e)
                        S = SPS[si % 2]
                        ptv = pt_sb[:].rearrange("p (k q) -> p k q", k=12)
                        for c in range(3):
                            ins = e.tensor_copy(
                                out=ptv[:, 4 * c:4 * c + 4, i * 128:(i + 1) * 128],
                                in_=S[:, c * 512:(c + 1) * 512].rearrange("p (a q) -> p a q", a=4))
                        ev[("ptev", si)] = P.ev(ins, "v_ev")

                for i in (0, 1):
                    sc(i)
                pt(0)
                sc(2)
                pt(1)
                sc(3)
                pt(2)
                pt(3)

                @P.seg("vector")
                def _(e):
                    ev[("rec", h)] = P.vfence(e, "v_ev")

                @P.seg("tensor")
                def _(e):
                    e.wait_ge(v_ev, ev[("rec", h)])
                    for i in range(4):
                        ins = e.transpose(g["dps"][0:1, i * 128:(i + 1) * 128],
                                          rows[:, 4 + i:5 + i], g["ident"][:])
                    ev[("rt", h)] = P.ev(ins, "t_ev")

                @P.seg("vector")
                def _(e):
                    e.wait_ge(t_ev, ev[("rt", h)])
                    P.vbar(e)
                    e.tensor_copy(out=rrow[:], in_=g["dps"][0:1, 0:512])
                    ev[("rr", h)] = P.vfence(e, "v_ev")

                @P.seg("tensor")
                def _(e):
                    e.wait_ge(v_ev, ev[("rr", h)])
                    ins = e.matmul(g["dps"][:, 0:512], g["ones_row"][:], rrow[:],
                                   start=True, stop=True)
                    ev[("rb", h)] = P.ev(ins, "t_ev")

                @P.seg("vector")
                def _(e):
                    e.wait_ge(t_ev, ev[("rb", h)])
                    P.vbar(e)
                    ins = e.tensor_copy(out=recipb[:], in_=g["dps"][:, 0:512])
                    ev[("rbe", h)] = P.ev(ins, "v_ev")

                @P.seg("tensor")
                def _(e):
                    for i in range(4):
                        e.wait_ge(v_ev, ev[("ptev", h * 4 + i)])
                    e.wait_ge(v_ev, ev[("ql", h)])
                    for k in range(12):
                        ins = e.matmul(G[:], kv["v"][:, kvo + k * 128:kvo + (k + 1) * 128],
                                       pt_sb[:, k * 512:(k + 1) * 512],
                                       start=(k == 0), stop=(k == 11))
                    ev[("av", h)] = P.ev(ins, "t_ev")

                @P.seg("vector")
                def _(e):
                    e.wait_ge(t_ev, ev[("av", h)])
                    e.wait_ge(v_ev, ev[("rbe", h)])
                    P.vbar(e)
                    e.tensor_tensor(out=attn_sb[:], in0=G[:], in1=recipb[:],
                                    op=ALU.mult)
                    P.vbar(e)
                    e.tensor_copy(out=att_hi[:, h * 512:(h + 1) * 512], in_=attn_sb[:])
                    P.vbar(e)
                    ins = e.tensor_tensor(out=att_lo[:, h * 512:(h + 1) * 512],
                                          in0=attn_sb[:],
                                          in1=att_hi[:, h * 512:(h + 1) * 512],
                                          op=ALU.subtract)
                    ev[("attn", h)] = P.ev(ins, "v_ev")

        emit_wo_epilogue(P, t_in, t_scr, g, qb, ev, att_hi, att_lo, SPS, G, qctx)


def emit_wo_epilogue(P, t_in, t_scr, g, qb, ev, att_hi, att_lo, SPS, G, qctx):
    nc = P.nc
    v_ev, s_ev, t_ev = P.sems["v_ev"], P.sems["s_ev"], P.sems["t_ev"]
    S_a, S_b = SPS

    sbb = lambda n, s, d=F32: qctx.enter_context(nc.sbuf_tensor(f"{n}_{qb}", s, d))
    wohs = sbb("wohs", [128, 2 * 4096], BF16)
    wols = sbb("wols", [128, 2 * 4096], BF16)
    x1pre = sbb("x1pre", [128, 4 * 2048])
    xnat = sbb("xnat", [128, 2048])
    gbuf = sbb("gbuf", [128, 2048])
    ubuf = sbb("ubuf", [128, 2048])
    qbuf = sbb("qbuf", [128, 2048], BF16)
    st2 = sbb("st2", [128, 32])

    def wbank(k):
        b = k % 6
        S = S_a if b < 3 else S_b
        return S[:, (b % 3) * 512:(b % 3) * 512 + 256]

    def wo_dma(e, ec, slot):
        ecb, hf = ec // 2, ec % 2
        toks = []
        for hl, buf in ((0, wohs), (1, wols)):
            src = t_scr["wo_hl"][hl, ecb][:, hf * 256:(hf + 1) * 256]
            toks.append(P.dma(e, buf[:, slot * 4096:(slot + 1) * 4096]
                              .rearrange("p (a e) -> p a e", a=16),
                              src.rearrange("(a p) e -> p a e", p=128), "d_s"))
        return toks

    @P.seg("sync")
    def _(e):
        e.wait_ge(v_ev, ev[("attn", 15)])
        ev[("wo", 0)] = wo_dma(e, 0, 0)

    @P.seg("gpsimd")
    def _(e):
        e.wait_ge(v_ev, ev[("attn", 15)])
        ev["bo_dma"] = P.dma(e, gbuf[:], t_in["bo_b"], "d_g")

    for ec in range(8):
        if ec + 1 < 8:
            @P.seg("sync")
            def _(e, ec=ec):
                if ec >= 1:
                    e.wait_ge(t_ev, ev[("wom", ec - 1, 3)])
                ev[("wo", ec + 1)] = wo_dma(e, ec + 1, (ec + 1) % 2)

        for i in range(4):
            k = ec * 4 + i

            @P.seg("tensor")
            def _(e, ec=ec, i=i, k=k):
                for t in ev[("wo", ec)]:
                    P.wge(e, t)
                if k >= 6:
                    e.wait_ge(v_ev, ev[("woe", k - 6)])
                sl = ec % 2
                last = None
                for mc in range(16):
                    ah = att_hi[:, mc * 512 + i * 128:mc * 512 + (i + 1) * 128]
                    al = att_lo[:, mc * 512 + i * 128:mc * 512 + (i + 1) * 128]
                    wh = wohs[:, sl * 4096 + mc * 256:sl * 4096 + (mc + 1) * 256]
                    wl = wols[:, sl * 4096 + mc * 256:sl * 4096 + (mc + 1) * 256]
                    e.matmul(wbank(k), ah, wh, start=(mc == 0), stop=False)
                    e.matmul(wbank(k), ah, wl, start=False, stop=False)
                    last = e.matmul(wbank(k), al, wh, start=False, stop=(mc == 15))
                ev[("wom", ec, i)] = P.ev(last, "t_ev")

            @P.seg("vector")
            def _(e, ec=ec, i=i, k=k):
                e.wait_ge(t_ev, ev[("wom", ec, i)])
                if k == 0:
                    P.wge(e, ev["bo_dma"])
                P.vbar(e)
                ins = e.scalar_tensor_tensor(
                    out=x1pre[:, i * 2048 + ec * 256:i * 2048 + (ec + 1) * 256],
                    in0=wbank(k), scalar=0.0, op0=ALU.add,
                    in1=gbuf[:, ec * 256:(ec + 1) * 256], op1=ALU.add)
                ev[("woe", k)] = P.ev(ins, "v_ev")

    for i in range(4):
        qt = 4 * qb + i

        @P.seg("gpsimd")
        def _(e, i=i, qt=qt):
            if i >= 1:
                e.wait_ge(v_ev, ev[("xadd", i - 1)])
            ev[("xnat", i)] = P.dma(e, xnat[:],
                                    t_in["x_nat"][qt * 128:(qt + 1) * 128, :], "d_g")

        @P.seg("vector")
        def _(e, i=i):
            P.wge(e, ev[("xnat", i)])
            e.wait_ge(v_ev, ev[("woe", 7 * 4 + i)])
            xsl = slice(i * 2048, (i + 1) * 2048)
            ins = e.tensor_tensor(out=x1pre[:, xsl], in0=x1pre[:, xsl],
                                  in1=xnat[:], op=ALU.add)
            ev[("xadd", i)] = P.ev(ins, "v_ev")

        @P.seg("scalar")
        def _(e, i=i):
            e.wait_ge(v_ev, ev[("xadd", i)])
            xsl = slice(i * 2048, (i + 1) * 2048)
            e.activation(out=x1pre[:, xsl], in_=x1pre[:, xsl], func=AF.Copy,
                         bias=0.0, scale=1.0, accum_out=st2[:, 0:1])
            P.sbar(e)
            ins = e.activation(out=ubuf[:], in_=x1pre[:, xsl], func=AF.Square,
                               accum_out=st2[:, 1:2])
            P.ev(ins, "s_ev")
            ev[("lns", i)] = P.sfence(e, "s_ev")

        @P.seg("vector")
        def _(e, i=i):
            e.wait_ge(s_ev, ev[("lns", i)])
            P.vbar(e)
            e.tensor_scalar(out=st2[:, 2:3], in0=st2[:, 0:1], scalar1=1.0 / D,
                            scalar2=None, op0=ALU.mult, op1=ALU.bypass)
            e.tensor_scalar(out=st2[:, 3:4], in0=st2[:, 1:2], scalar1=1.0 / D,
                            scalar2=None, op0=ALU.mult, op1=ALU.bypass)
            P.vbar(e)
            e.tensor_scalar(out=st2[:, 4:5], in0=st2[:, 2:3],
                            scalar1=st2[:, 2:3], scalar2=None,
                            op0=ALU.mult, op1=ALU.bypass)
            P.vbar(e)
            e.tensor_tensor(out=st2[:, 5:6], in0=st2[:, 3:4], in1=st2[:, 4:5],
                            op=ALU.subtract)
            P.vbar(e)
            ins = e.tensor_scalar(out=st2[:, 5:6], in0=st2[:, 5:6], scalar1=1e-5,
                                  scalar2=None, op0=ALU.add, op1=ALU.bypass)
            ev[("var", i)] = P.vfence(e, "v_ev")

        @P.seg("scalar")
        def _(e, i=i):
            e.wait_ge(v_ev, ev[("var", i)])
            e.activation(out=st2[:, 6:7], in_=st2[:, 5:6], func=AF.Ln)
            P.sbar(e)
            ins = e.activation(out=st2[:, 7:8], in_=st2[:, 6:7], func=AF.Exp,
                               scale=-0.5)
            P.ev(ins, "s_ev")
            ev[("rst0", i)] = P.sfence(e, "s_ev")

        @P.seg("vector")
        def _(e, i=i):
            e.wait_ge(s_ev, ev[("rst0", i)])
            P.vbar(e)
            for it in range(2):
                y = st2[:, 7:8]
                e.tensor_scalar(out=st2[:, 8:9], in0=y, scalar1=st2[:, 5:6],
                                scalar2=None, op0=ALU.mult, op1=ALU.bypass)
                P.vbar(e)
                e.tensor_scalar(out=st2[:, 8:9], in0=st2[:, 8:9], scalar1=y,
                                scalar2=-0.5, op0=ALU.mult, op1=ALU.mult)
                P.vbar(e)
                e.tensor_scalar(out=st2[:, 8:9], in0=st2[:, 8:9], scalar1=1.5,
                                scalar2=None, op0=ALU.add, op1=ALU.bypass)
                P.vbar(e)
                e.tensor_scalar(out=st2[:, 7:8], in0=y, scalar1=st2[:, 8:9],
                                scalar2=None, op0=ALU.mult, op1=ALU.bypass)
                P.vbar(e)
            e.tensor_scalar(out=st2[:, 9:10], in0=st2[:, 2:3], scalar1=st2[:, 7:8],
                            scalar2=-1.0, op0=ALU.mult, op1=ALU.mult)
            P.vbar(e)
            xsl = slice(i * 2048, (i + 1) * 2048)
            ins = e.tensor_scalar(out=x1pre[:, xsl], in0=x1pre[:, xsl],
                                  scalar1=st2[:, 7:8], scalar2=st2[:, 9:10],
                                  op0=ALU.mult, op1=ALU.add)
            ev[("x1", i)] = P.ev(ins, "v_ev")

        @P.seg("gpsimd")
        def _(e, i=i, qt=qt):
            e.wait_ge(v_ev, ev[("x1", i)])
            P.dma(e, t_scr["x1"][qt * 128:(qt + 1) * 128, :],
                  x1pre[:, i * 2048:(i + 1) * 2048], "d_g")

        @P.seg("scalar")
        def _(e, i=i):
            e.wait_ge(v_ev, ev[("x1", i)])
            ins = e.activation(out=ubuf[:], in_=x1pre[:, i * 2048:(i + 1) * 2048],
                               func=AF.Square, accum_out=st2[:, 10:11])
            P.ev(ins, "s_ev")
            ev[("ssq1", i)] = P.sfence(e, "s_ev")

        @P.seg("vector")
        def _(e, i=i):
            e.wait_ge(s_ev, ev[("ssq1", i)])
            P.vbar(e)
            ins = e.tensor_scalar(out=st2[:, 11:12], in0=st2[:, 10:11],
                                  scalar1=1.0 / D, scalar2=1e-8,
                                  op0=ALU.mult, op1=ALU.add)
            ev[("v3", i)] = P.vfence(e, "v_ev")

        @P.seg("scalar")
        def _(e, i=i):
            e.wait_ge(v_ev, ev[("v3", i)])
            e.activation(out=st2[:, 12:13], in_=st2[:, 11:12], func=AF.Ln)
            P.sbar(e)
            ins = e.activation(out=st2[:, 13:14], in_=st2[:, 12:13], func=AF.Exp,
                               scale=-0.5)
            P.ev(ins, "s_ev")
            ev[("r10", i)] = P.sfence(e, "s_ev")

        @P.seg("vector")
        def _(e, i=i):
            e.wait_ge(s_ev, ev[("r10", i)])
            P.vbar(e)
            for it in range(2):
                y = st2[:, 13:14]
                e.tensor_scalar(out=st2[:, 8:9], in0=y, scalar1=st2[:, 11:12],
                                scalar2=None, op0=ALU.mult, op1=ALU.bypass)
                P.vbar(e)
                e.tensor_scalar(out=st2[:, 8:9], in0=st2[:, 8:9], scalar1=y,
                                scalar2=-0.5, op0=ALU.mult, op1=ALU.mult)
                P.vbar(e)
                e.tensor_scalar(out=st2[:, 8:9], in0=st2[:, 8:9], scalar1=1.5,
                                scalar2=None, op0=ALU.add, op1=ALU.bypass)
                P.vbar(e)
                e.tensor_scalar(out=st2[:, 13:14], in0=y, scalar1=st2[:, 8:9],
                                scalar2=None, op0=ALU.mult, op1=ALU.bypass)
                P.vbar(e)
            ev[("r1", i)] = P.vfence(e, "v_ev")

    for gi, (gname, beta, xqs) in enumerate(
            (("g1b", "beta1", "xq1"), ("g2b", "beta2", "xq2"))):
        @P.seg("gpsimd")
        def _(e, gi=gi, gname=gname):
            if gi == 0:
                e.wait_ge(v_ev, ev[("woe", 31)])
            else:
                e.wait_ge(v_ev, ev[("qdone", 0, 3)])
            ev[("g_dma", gi)] = P.dma(e, gbuf[:], t_in[gname], "d_g")

        for i in range(4):
            qt = 4 * qb + i
            acol = "a1" if gi == 0 else "a2"

            @P.seg("vector")
            def _(e, gi=gi, i=i, qt=qt, acol=acol, beta=beta):
                P.wge(e, ev[("g_dma", gi)])
                e.wait_ge(v_ev, ev[("r1", i)])
                if (gi, i) != (0, 0):
                    P.wge(e, ev[("q_dma", gi, i - 1) if i >= 1 else ("q_dma", 0, 3)])
                xsl = slice(i * 2048, (i + 1) * 2048)
                e.tensor_tensor(out=ubuf[:], in0=x1pre[:, xsl], in1=gbuf[:],
                                op=ALU.mult)
                P.vbar(e)
                e.tensor_scalar(out=xnat[:], in0=ubuf[:], scalar1=1.0,
                                scalar2=None, op0=ALU.mult, op1=ALU.max,
                                accum_out=st2[:, 14:15])
                P.vbar(e)
                e.tensor_scalar(out=xnat[:], in0=ubuf[:], scalar1=1.0,
                                scalar2=None, op0=ALU.mult, op1=ALU.min,
                                accum_out=st2[:, 15:16])
                P.vbar(e)
                e.tensor_scalar(out=st2[:, 16:17], in0=st2[:, 15:16], scalar1=-1.0,
                                scalar2=None, op0=ALU.mult, op1=ALU.bypass)
                P.vbar(e)
                e.tensor_tensor(out=st2[:, 17:18], in0=st2[:, 14:15],
                                in1=st2[:, 16:17], op=ALU.max)
                P.vbar(e)
                e.tensor_scalar(out=st2[:, 18:19], in0=st2[:, 17:18],
                                scalar1=st2[:, 13:14], scalar2=None,
                                op0=ALU.mult, op1=ALU.bypass)
                P.vbar(e)
                e.tensor_scalar(out=st2[:, 18:19], in0=st2[:, 18:19], scalar1=1e-4,
                                scalar2=None, op0=ALU.max, op1=ALU.bypass)
                P.vbar(e)
                e.tensor_scalar(out=g[beta][:, qt:qt + 1], in0=st2[:, 18:19],
                                scalar1=g[acol][:], scalar2=1.0 / 127.0,
                                op0=ALU.mult, op1=ALU.mult)
                e.reciprocal(out=st2[:, 19:20], in_=st2[:, 18:19])
                P.vbar(e)
                e.tensor_scalar(out=st2[:, 20:21], in0=st2[:, 19:20],
                                scalar1=st2[:, 13:14], scalar2=127.0,
                                op0=ALU.mult, op1=ALU.mult)
                P.vbar(e)
                e.tensor_scalar(out=ubuf[:], in0=ubuf[:], scalar1=st2[:, 20:21],
                                scalar2=g["meff"][:], op0=ALU.mult, op1=ALU.add)
                P.vbar(e)
                ins = e.tensor_scalar(out=qbuf[:], in0=ubuf[:],
                                      scalar1=g["meff"][:], scalar2=None,
                                      op0=ALU.subtract, op1=ALU.bypass)
                ev[("qdone", gi, i)] = P.ev(ins, "v_ev")

            @P.seg("gpsimd")
            def _(e, gi=gi, i=i, qt=qt, xqs=xqs):
                e.wait_ge(v_ev, ev[("qdone", gi, i)])
                ev[("q_dma", gi, i)] = P.dma(
                    e, t_scr[xqs][qt * 128:(qt + 1) * 128, :], qbuf[:], "d_g")


def emit_ffn12(P, t_in, t_scr, g):
    nc = P.nc
    v_ev, s_ev, t_ev = P.sem("v_ev"), P.sem("s_ev"), P.sem("t_ev")

    for half in range(2):
        with ExitStack() as ctx:
            sbf = lambda n, s, d=F32: ctx.enter_context(
                nc.sbuf_tensor(f"f12_{n}_{half}", s, d))
            xq1t = sbf("xq1t", [128, 16 * 1024], BF16)
            xq2t = sbf("xq2t", [128, 16 * 1024], BF16)
            land = sbf("land", [128, H])
            t1 = sbf("t1", [128, H], BF16)
            t2 = sbf("t2", [128, H], BF16)
            u1 = sbf("u1", [128, H])
            g3c = sbf("g3c", [128, 512])
            s1 = sbf("s1", [128, 512])
            hsb = sbf("hsb", [128, 512])
            u3 = sbf("u3", [128, 512])
            stt3 = sbf("stt3", [128, 4])
            htst = sbf("htst", [128, 4 * 1024])
            qland = sbf("qland", [128, 2048], BF16)
            ps_o1 = ctx.enter_context(nc.psum_tensor(f"f12_o1_{half}", [128, 512], F32))
            ps_o2 = ctx.enter_context(nc.psum_tensor(f"f12_o2_{half}", [128, 512], F32))
            ps_tr = ctx.enter_context(nc.psum_tensor(f"f12_tr_{half}", [128, 1024], BF16))
            ps_ht = ctx.enter_context(nc.psum_tensor(f"f12_ht_{half}", [128, 512], F32))
            ev = {}

            for gi, (xqs, xqt) in enumerate((("xq1", xq1t), ("xq2", xq2t))):
                for r in range(8):
                    qt = half * 8 + r

                    @P.seg("gpsimd")
                    def _(e, qt=qt, gi=gi, r=r):
                        if (gi, r) != (0, 0):
                            e.wait_ge(t_ev, ev[("qtrt2", gi, r - 1) if r else ("qtrt2", 0, 7)])
                        ev[("qld", gi, r)] = P.dma(
                            e, qland[:], t_scr[xqs][qt * 128:(qt + 1) * 128, :], "d_g")

                    @P.seg("tensor")
                    def _(e, gi=gi, r=r):
                        P.wge(e, ev[("qld", gi, r)])
                        if (gi, r) != (0, 0):
                            e.wait_ge(v_ev, ev[("qtr", gi, r - 1) if r else ("qtr", 0, 7)])
                        for c in range(8):
                            ins = e.transpose(ps_tr[:, c * 128:(c + 1) * 128],
                                              qland[:, c * 128:(c + 1) * 128],
                                              g["identb"][:])
                        ev[("qtrt1", gi, r)] = P.ev(ins, "t_ev")

                    @P.seg("vector")
                    def _(e, gi=gi, r=r, xqt=xqt):
                        e.wait_ge(t_ev, ev[("qtrt1", gi, r)])
                        for c in range(8):
                            ins = e.tensor_copy(
                                out=xqt[:, c * 1024 + r * 128:c * 1024 + (r + 1) * 128],
                                in_=ps_tr[:, c * 128:(c + 1) * 128])
                        ev[("qev1", gi, r)] = P.ev(ins, "v_ev")

                    @P.seg("tensor")
                    def _(e, gi=gi, r=r):
                        e.wait_ge(v_ev, ev[("qev1", gi, r)])
                        for c in range(8):
                            ins = e.transpose(ps_tr[:, c * 128:(c + 1) * 128],
                                              qland[:, (8 + c) * 128:(9 + c) * 128],
                                              g["identb"][:])
                        ev[("qtrt2", gi, r)] = P.ev(ins, "t_ev")

                    @P.seg("vector")
                    def _(e, gi=gi, r=r, xqt=xqt):
                        e.wait_ge(t_ev, ev[("qtrt2", gi, r)])
                        for c in range(8):
                            ins = e.tensor_copy(
                                out=xqt[:, (8 + c) * 1024 + r * 128:(8 + c) * 1024 + (r + 1) * 128],
                                in_=ps_tr[:, c * 128:(c + 1) * 128])
                        ev[("qtr", gi, r)] = P.ev(ins, "v_ev")

            if half == 0:
                @P.seg("vector")
                def _(e):
                    e.tensor_scalar(out=g["ssq3"][:], in0=g["ident"][:, 0:NQT],
                                    scalar1=0.0, scalar2=None, op0=ALU.mult,
                                    op1=ALU.bypass)
                    ins = e.tensor_scalar(out=g["am3"][:], in0=g["ident"][:, 0:NQT],
                                          scalar1=0.0, scalar2=None, op0=ALU.mult,
                                          op1=ALU.bypass)
                    P.ev(ins, "v_ev")
                    P.vbar(e)

            for hc in range(16):
                hc0 = hc * 512

                @P.seg("sync")
                def _(e, hc=hc, hc0=hc0):
                    if hc >= 1:
                        e.wait_ge(v_ev, ev[("tern", hc - 1, 1)])
                    ev[("wl", hc, 0)] = P.dma(
                        e, land[:].rearrange("p (a d) -> p a d", a=16),
                        t_in["w1T"][:, hc0:hc0 + 512].rearrange("(a p) d -> p a d", p=128),
                        "d_s")

                @P.seg("vector")
                def _(e, hc=hc):
                    P.wge(e, ev[("wl", hc, 0)])
                    if hc >= 1:
                        e.wait_ge(t_ev, ev[("mm2", hc - 1, 7)])
                    e.tensor_scalar(out=u1[:], in0=land[:], scalar1=g["nbnd1"][:],
                                    scalar2=1.0, op0=ALU.is_ge, op1=ALU.subtract)
                    P.vbar(e)
                    ins = e.scalar_tensor_tensor(out=t1[:], in0=land[:],
                                                 scalar=g["bnd1"][:], in1=u1[:],
                                                 op0=ALU.is_ge, op1=ALU.add)
                    ev[("tern", hc, 0)] = P.ev(ins, "v_ev")

                @P.seg("sync")
                def _(e, hc=hc, hc0=hc0):
                    e.wait_ge(v_ev, ev[("tern", hc, 0)])
                    ev[("wl", hc, 1)] = P.dma(
                        e, land[:].rearrange("p (a d) -> p a d", a=16),
                        t_in["w2T"][:, hc0:hc0 + 512].rearrange("(a p) d -> p a d", p=128),
                        "d_s")

                @P.seg("gpsimd")
                def _(e, hc=hc, hc0=hc0):
                    if hc >= 1:
                        e.wait_ge(v_ev, ev[("m3", hc - 1, 7)])
                    ev[("g3", hc)] = P.dma(e, g3c[:, 0:512],
                                           t_in["g3b"][:, hc0:hc0 + 512], "d_g")

                @P.seg("vector")
                def _(e, hc=hc):
                    P.wge(e, ev[("wl", hc, 1)])
                    e.tensor_scalar(out=u1[:], in0=land[:], scalar1=g["nbnd2"][:],
                                    scalar2=1.0, op0=ALU.is_ge, op1=ALU.subtract)
                    P.vbar(e)
                    ins = e.scalar_tensor_tensor(out=t2[:], in0=land[:],
                                                 scalar=g["bnd2"][:], in1=u1[:],
                                                 op0=ALU.is_ge, op1=ALU.add)
                    ev[("tern", hc, 1)] = P.ev(ins, "v_ev")

                for r in range(8):
                    qt = half * 8 + r
                    qtc = qt

                    @P.seg("tensor")
                    def _(e, hc=hc, r=r):
                        if (hc, r) == (0, 0):
                            e.wait_ge(v_ev, ev[("qtr", 1, 7)])
                        e.wait_ge(v_ev, ev[("tern", hc, 1)])
                        if (hc + r) > 0:
                            e.wait_ge(s_ev, ev[("silu", hc, r - 1) if r else ("silu", hc - 1, 7)])
                        last = None
                        for ec in range(16):
                            last = e.matmul(ps_o1[:],
                                            xq1t[:, ec * 1024 + r * 128:ec * 1024 + (r + 1) * 128],
                                            t1[:, ec * 512:(ec + 1) * 512],
                                            start=(ec == 0), stop=(ec == 15))
                        ev[("mm1", hc, r)] = P.ev(last, "t_ev")
                        if (hc + r) > 0:
                            e.wait_ge(v_ev, ev[("h", hc, r - 1) if r else ("h", hc - 1, 7)])
                        for ec in range(16):
                            last = e.matmul(ps_o2[:],
                                            xq2t[:, ec * 1024 + r * 128:ec * 1024 + (r + 1) * 128],
                                            t2[:, ec * 512:(ec + 1) * 512],
                                            start=(ec == 0), stop=(ec == 15))
                        ev[("mm2", hc, r)] = P.ev(last, "t_ev")

                    @P.seg("scalar")
                    def _(e, hc=hc, r=r, qtc=qtc):
                        e.wait_ge(t_ev, ev[("mm1", hc, r)])
                        ins = e.activation(out=s1[:], in_=ps_o1[:], func=AF.Silu,
                                           scale=g["beta1"][:, qtc:qtc + 1])
                        ev[("silu", hc, r)] = P.ev(ins, "s_ev")

                    @P.seg("vector")
                    def _(e, hc=hc, r=r, qtc=qtc):
                        e.wait_ge(t_ev, ev[("mm2", hc, r)])
                        e.wait_ge(s_ev, ev[("silu", hc, r)])
                        if (hc + r) > 0:
                            e.wait_ge(t_ev, ev[("htr", hc, r - 1) if r else ("htr", hc - 1, 7)])
                        ins = e.scalar_tensor_tensor(out=hsb[:], in0=ps_o2[:],
                                                     scalar=g["beta2"][:, qtc:qtc + 1],
                                                     op0=ALU.mult, in1=s1[:],
                                                     op1=ALU.mult)
                        ev[("h", hc, r)] = P.ev(ins, "v_ev")

                    @P.seg("scalar")
                    def _(e, hc=hc, r=r, qtc=qtc):
                        e.wait_ge(v_ev, ev[("h", hc, r)])
                        ins = e.activation(out=s1[:], in_=hsb[:], func=AF.Square,
                                           accum_out=stt3[:, 3:4])
                        P.ev(ins, "s_ev")
                        ev[("sq", hc, r)] = P.sfence(e, "s_ev")

                    @P.seg("vector")
                    def _(e, hc=hc, r=r, qtc=qtc):
                        P.wge(e, ev[("g3", hc)])
                        P.vbar(e)
                        ins = e.tensor_tensor(out=u3[:], in0=hsb[:], in1=g3c[:, 0:512],
                                              op=ALU.mult)
                        P.vbar(e)
                        e.tensor_scalar(out=u3[:], in0=u3[:], scalar1=-1.0,
                                        scalar2=None, op0=ALU.mult, op1=ALU.max,
                                        accum_out=stt3[:, 0:1])
                        P.vbar(e)
                        e.tensor_scalar(out=u3[:], in0=u3[:], scalar1=-1.0,
                                        scalar2=None, op0=ALU.mult, op1=ALU.max,
                                        accum_out=stt3[:, 1:2])
                        P.vbar(e)
                        e.tensor_tensor(out=stt3[:, 2:3], in0=stt3[:, 0:1],
                                        in1=stt3[:, 1:2], op=ALU.max)
                        e.wait_ge(s_ev, ev[("sq", hc, r)])
                        e.tensor_tensor(out=g["ssq3"][:, qtc:qtc + 1],
                                        in0=g["ssq3"][:, qtc:qtc + 1],
                                        in1=stt3[:, 3:4], op=ALU.add)
                        P.vbar(e)
                        ins = e.tensor_tensor(out=g["am3"][:, qtc:qtc + 1],
                                              in0=g["am3"][:, qtc:qtc + 1],
                                              in1=stt3[:, 2:3], op=ALU.max)
                        ev[("m3", hc, r)] = P.ev(ins, "v_ev")
                        P.vbar(e)

                    @P.seg("tensor")
                    def _(e, hc=hc, r=r):
                        e.wait_ge(v_ev, ev[("h", hc, r)])
                        if (hc + r) > 0:
                            e.wait_ge(v_ev, ev[("hte", hc, r - 1) if r else ("hte", hc - 1, 7)])
                        for c in range(4):
                            ins = e.transpose(ps_ht[:, c * 128:(c + 1) * 128],
                                              hsb[:, c * 128:(c + 1) * 128],
                                              g["ident"][:])
                        ev[("htr", hc, r)] = P.ev(ins, "t_ev")

                    @P.seg("vector")
                    def _(e, hc=hc, r=r):
                        e.wait_ge(t_ev, ev[("htr", hc, r)])
                        P.vbar(e)
                        ins = e.tensor_copy(
                            out=htst[:, 0:4096].rearrange("p (a q) -> p a q", a=4)[:, :, r * 128:(r + 1) * 128],
                            in_=ps_ht[:, 0:512].rearrange("p (a q) -> p a q", a=4))
                        ev[("hte", hc, r)] = P.ev(ins, "v_ev")

                @P.seg("gpsimd")
                def _(e, hc=hc, half=half):
                    e.wait_ge(v_ev, ev[("hte", hc, 7)])
                    toks = []
                    for hs in range(4):
                        toks.append(P.dma(
                            e, t_scr["hT"][hc * 512 + hs * 128:hc * 512 + (hs + 1) * 128,
                                           half * 1024:(half + 1) * 1024],
                            htst[:, hs * 1024:(hs + 1) * 1024], "d_g"))
                    ev[("htw", hc)] = toks

                @P.seg("vector")
                def _(e, hc=hc):
                    for t in ev[("htw", hc)]:
                        P.wge(e, t)
        barrier(P, g)


def emit_ffn3(P, t_in, t_scr, g, out):
    nc = P.nc
    v_ev, s_ev, t_ev = P.sem("v_ev"), P.sem("s_ev"), P.sem("t_ev")
    ev = {}
    with ExitStack() as ctx:
        sbf = lambda n, s, d=F32: ctx.enter_context(nc.sbuf_tensor(f"f3_{n}", s, d))
        hq = sbf("hq", [128, 64 * 512], BF16)
        hland = sbf("hland", [128, 2 * 512])
        wbuf = sbf("wbuf", [128, 2 * 512])
        t3sb = sbf("t3sb", [128, 2 * 64 * 128], BF16)
        sc3b = sbf("sc3b", [128, 2048])
        g3t = sbf("g3t", [128, 64])
        o3t = sbf("o3t", [128, 16 * 512])
        x1sl = sbf("x1sl", [128, 2048])
        x2 = sbf("x2", [128, 2048])
        rrow3 = sbf("rrow3", [1, 2048])
        st3 = sbf("st3", [128, 16])
        o3ps = [ctx.enter_context(nc.psum_tensor(f"f3_o{i}", [128, 512], F32)) for i in range(2)]
        trps = [ctx.enter_context(nc.psum_tensor(f"f3_t{i}", [128, 512], F32)) for i in range(2)]

        @P.seg("gpsimd")
        def _(e):
            ev["g3t"] = P.dma(e, g3t[:, 0:64], t_in["g3t"], "d_g")

        @P.seg("vector")
        def _(e):
            e.tensor_scalar(out=st3[:, 0:16], in0=g["ssq3"][:], scalar1=1.0 / H,
                            scalar2=1e-8, op0=ALU.mult, op1=ALU.add)
            ev["v3"] = P.vfence(e, "v_ev")

        @P.seg("scalar")
        def _(e):
            e.wait_ge(v_ev, ev["v3"])
            e.activation(out=st3[:, 0:16], in_=st3[:, 0:16], func=AF.Ln)
            P.sbar(e)
            ins = e.activation(out=st3[:, 0:16], in_=st3[:, 0:16], func=AF.Exp,
                               scale=-0.5)
            P.ev(ins, "s_ev")
            ev["r30"] = P.sfence(e, "s_ev")

        @P.seg("vector")
        def _(e):
            e.wait_ge(s_ev, ev["r30"])
            P.vbar(e)
            for it in range(2):
                e.tensor_scalar(out=g["gam1"][:], in0=g["ssq3"][:], scalar1=1.0 / H,
                                scalar2=1e-8, op0=ALU.mult, op1=ALU.add)
                P.vbar(e)
                e.tensor_tensor(out=g["gam1"][:], in0=g["gam1"][:],
                                in1=st3[:, 0:16], op=ALU.mult)
                P.vbar(e)
                e.tensor_tensor(out=g["gam1"][:], in0=g["gam1"][:],
                                in1=st3[:, 0:16], op=ALU.mult)
                P.vbar(e)
                e.tensor_scalar(out=g["gam1"][:], in0=g["gam1"][:], scalar1=-0.5,
                                scalar2=1.5, op0=ALU.mult, op1=ALU.add)
                P.vbar(e)
                e.tensor_tensor(out=st3[:, 0:16], in0=st3[:, 0:16],
                                in1=g["gam1"][:], op=ALU.mult)
                P.vbar(e)
            e.tensor_tensor(out=g["gam1"][:], in0=g["am3"][:], in1=st3[:, 0:16],
                            op=ALU.mult)
            P.vbar(e)
            e.tensor_scalar(out=g["gam1"][:], in0=g["gam1"][:], scalar1=1e-4,
                            scalar2=None, op0=ALU.max, op1=ALU.bypass)
            P.vbar(e)
            e.tensor_scalar(out=g["gam2"][:], in0=g["gam1"][:],
                            scalar1=g["a3"][:], scalar2=1.0 / 127.0,
                            op0=ALU.mult, op1=ALU.mult)
            e.reciprocal(out=g["gam1"][:], in_=g["gam1"][:])
            P.vbar(e)
            e.tensor_tensor(out=g["gam1"][:], in0=g["gam1"][:], in1=st3[:, 0:16],
                            op=ALU.mult)
            P.vbar(e)
            ins = e.tensor_scalar(out=g["gam1"][:], in0=g["gam1"][:], scalar1=127.0,
                                  scalar2=None, op0=ALU.mult, op1=ALU.bypass)
            ev["m3tok"] = P.vfence(e, "v_ev")

        for grp in range(4):
            @P.seg("tensor")
            def _(e, grp=grp):
                if grp == 0:
                    e.wait_ge(v_ev, ev["m3tok"])
                else:
                    e.wait_ge(v_ev, ev[("sc3e", grp - 1)])
                for k in range(4):
                    col = grp * 4 + k
                    ins = e.transpose(g["dps"][0:1, k * 128:(k + 1) * 128],
                                      g["gam1"][:, col:col + 1], g["ident"][:])
                ev[("sc3t", grp)] = P.ev(ins, "t_ev")

            @P.seg("vector")
            def _(e, grp=grp):
                e.wait_ge(t_ev, ev[("sc3t", grp)])
                P.vbar(e)
                ins = e.tensor_copy(out=rrow3[0:1, grp * 512:(grp + 1) * 512],
                                    in_=g["dps"][0:1, 0:512])
                ev[("sc3e", grp)] = P.ev(ins, "v_ev")

        @P.seg("tensor")
        def _(e):
            e.wait_ge(v_ev, ev[("sc3e", 3)])
            ins = e.matmul(g["dps"][:, 0:512], g["ones_row"][:],
                           rrow3[0:1, 0:512], start=True, stop=True)
            ev["bc0"] = P.ev(ins, "t_ev")

        for c in range(4):
            @P.seg("vector")
            def _(e, c=c):
                e.wait_ge(t_ev, ev["bc0"] + c)
                P.vbar(e)
                ins = e.tensor_copy(out=sc3b[:, c * 512:(c + 1) * 512],
                                    in_=g["dps"][:, 0:512])
                ev[("sc3b", c)] = P.ev(ins, "v_ev")

            if c + 1 < 4:
                @P.seg("tensor")
                def _(e, c=c):
                    e.wait_ge(v_ev, ev[("sc3b", c)])
                    ins = e.matmul(g["dps"][:, 0:512], g["ones_row"][:],
                                   rrow3[0:1, (c + 1) * 512:(c + 2) * 512],
                                   start=True, stop=True)
                    P.ev(ins, "t_ev")

        for qg in range(4):
            for hc in range(64):
                sl = hc % 2

                @P.seg("sync")
                def _(e, qg=qg, hc=hc, sl=sl):
                    if hc >= 2:
                        e.wait_ge(v_ev, ev[("hq", qg, hc - 2)])
                    elif qg >= 1 and hc < 2:
                        e.wait_ge(t_ev, ev[("f3m", qg - 1, 15)])
                    ev[("hl", qg, hc)] = P.dma(
                        e, hland[:, sl * 512:(sl + 1) * 512],
                        t_scr["hT"][hc * 128:(hc + 1) * 128,
                                    qg * 512:(qg + 1) * 512], "d_s")

                @P.seg("vector")
                def _(e, qg=qg, hc=hc, sl=sl):
                    P.wge(e, ev[("hl", qg, hc)])
                    if (qg, hc) == (0, 0):
                        P.wge(e, ev["g3t"])
                        for c in range(4):
                            e.wait_ge(v_ev, ev[("sc3b", c)])
                    if qg >= 1 and hc == 0:
                        e.wait_ge(t_ev, ev[("f3m", qg - 1, 15)])
                    e.scalar_tensor_tensor(out=wbuf[:, sl * 512:(sl + 1) * 512],
                                           in0=hland[:, sl * 512:(sl + 1) * 512],
                                           scalar=g3t[:, hc:hc + 1],
                                           in1=sc3b[:, qg * 512:(qg + 1) * 512],
                                           op0=ALU.mult, op1=ALU.mult)
                    P.vbar(e)
                    e.tensor_scalar(out=wbuf[:, sl * 512:(sl + 1) * 512],
                                    in0=wbuf[:, sl * 512:(sl + 1) * 512],
                                    scalar1=g["meff"][:], scalar2=None,
                                    op0=ALU.add, op1=ALU.bypass)
                    P.vbar(e)
                    ins = e.tensor_scalar(out=hq[:, hc * 512:(hc + 1) * 512],
                                          in0=wbuf[:, sl * 512:(sl + 1) * 512],
                                          scalar1=g["meff"][:], scalar2=None,
                                          op0=ALU.subtract, op1=ALU.bypass)
                    ev[("hq", qg, hc)] = P.ev(ins, "v_ev")

            for dt in range(16):
                sl = dt % 2

                @P.seg("sync")
                def _(e, qg=qg, dt=dt, sl=sl):
                    if dt >= 2 or qg >= 1:
                        prev = ("f3m", qg, dt - 2) if dt >= 2 else ("f3m", qg - 1, 14 + dt)
                        e.wait_ge(t_ev, ev[prev])
                    ev[("t3", qg, dt)] = P.dma(
                        e, t3sb[:, sl * 8192:(sl + 1) * 8192]
                        .rearrange("p (a d) -> p a d", a=64),
                        t_scr["t3"][dt].rearrange("(a p) d -> p a d", p=128), "d_s")

                @P.seg("tensor")
                def _(e, qg=qg, dt=dt, sl=sl):
                    P.wge(e, ev[("t3", qg, dt)])
                    if dt == 0:
                        e.wait_ge(v_ev, ev[("hq", qg, 63)])
                    if dt >= 2:
                        e.wait_ge(v_ev, ev[("o3e", qg, dt - 2)])
                    last = None
                    for hc in range(64):
                        last = e.matmul(o3ps[dt % 2][:],
                                        t3sb[:, sl * 8192 + hc * 128:sl * 8192 + (hc + 1) * 128],
                                        hq[:, hc * 512:(hc + 1) * 512],
                                        start=(hc == 0), stop=(hc == 63))
                    ev[("f3m", qg, dt)] = P.ev(last, "t_ev")

                @P.seg("vector")
                def _(e, qg=qg, dt=dt):
                    e.wait_ge(t_ev, ev[("f3m", qg, dt)])
                    P.vbar(e)
                    ins = e.tensor_copy(out=o3t[:, dt * 512:(dt + 1) * 512],
                                        in_=o3ps[dt % 2][:])
                    ev[("o3e", qg, dt)] = P.ev(ins, "v_ev")

            for tq in range(4):
                qt = qg * 4 + tq

                @P.seg("gpsimd")
                def _(e, qt=qt, tq=tq, qg=qg):
                    if (qg, tq) != (0, 0):
                        e.wait_ge(v_ev, ev[("x2", qg, tq - 1) if tq else ("x2", qg - 1, 3)])
                    ev[("x1l", qg, tq)] = P.dma(
                        e, x1sl[:], t_scr["x1"][qt * 128:(qt + 1) * 128, :], "d_g")

                for grp in range(4):
                    @P.seg("tensor")
                    def _(e, qg=qg, tq=tq, grp=grp):
                        if grp == 0:
                            e.wait_ge(v_ev, ev[("o3e", qg, 15)])
                        if (tq, grp) != (0, 0):
                            pg = ("tre", qg, tq, grp - 1) if grp else ("tre", qg, tq - 1, 3)
                            e.wait_ge(v_ev, ev[pg])
                        elif qg >= 1:
                            e.wait_ge(v_ev, ev[("tre", qg - 1, 3, 2)])
                            e.wait_ge(v_ev, ev[("tre", qg - 1, 3, 3)])
                        for k in range(4):
                            dt = grp * 4 + k
                            ins = e.transpose(trps[grp % 2][:, k * 128:(k + 1) * 128],
                                              o3t[:, dt * 512 + tq * 128:dt * 512 + (tq + 1) * 128],
                                              g["ident"][:])
                        ev[("tr", qg, tq, grp)] = P.ev(ins, "t_ev")

                    @P.seg("vector")
                    def _(e, qg=qg, tq=tq, grp=grp, qt=qt):
                        e.wait_ge(t_ev, ev[("tr", qg, tq, grp)])
                        if grp == 0:
                            P.wge(e, ev[("x1l", qg, tq)])
                            if (qg, tq) != (0, 0):
                                P.wge(e, ev[("outw", qg, tq - 1) if tq else ("outw", qg - 1, 3)])
                        P.vbar(e)
                        ins = e.scalar_tensor_tensor(
                            out=x2[:, grp * 512:(grp + 1) * 512],
                            in0=trps[grp % 2][:], scalar=g["gam2"][:, qt:qt + 1],
                            op0=ALU.mult,
                            in1=x1sl[:, grp * 512:(grp + 1) * 512], op1=ALU.add)
                        ev[("tre", qg, tq, grp)] = P.ev(ins, "v_ev")

                @P.seg("scalar")
                def _(e, qg=qg, tq=tq):
                    e.wait_ge(v_ev, ev[("tre", qg, tq, 3)])
                    e.activation(out=x2[:], in_=x2[:], func=AF.Copy,
                                 bias=0.0, scale=1.0, accum_out=st3[:, 0:1])
                    P.sbar(e)
                    ins = e.activation(out=x1sl[:], in_=x2[:], func=AF.Square,
                                       accum_out=st3[:, 1:2])
                    P.ev(ins, "s_ev")
                    ev[("ln2s", qg, tq)] = P.sfence(e, "s_ev")

                @P.seg("vector")
                def _(e, qg=qg, tq=tq):
                    e.wait_ge(s_ev, ev[("ln2s", qg, tq)])
                    P.vbar(e)
                    e.tensor_scalar(out=st3[:, 2:3], in0=st3[:, 0:1], scalar1=1.0 / D,
                                    scalar2=None, op0=ALU.mult, op1=ALU.bypass)
                    e.tensor_scalar(out=st3[:, 3:4], in0=st3[:, 1:2], scalar1=1.0 / D,
                                    scalar2=None, op0=ALU.mult, op1=ALU.bypass)
                    P.vbar(e)
                    e.tensor_scalar(out=st3[:, 4:5], in0=st3[:, 2:3],
                                    scalar1=st3[:, 2:3], scalar2=None,
                                    op0=ALU.mult, op1=ALU.bypass)
                    P.vbar(e)
                    e.tensor_tensor(out=st3[:, 5:6], in0=st3[:, 3:4], in1=st3[:, 4:5],
                                    op=ALU.subtract)
                    P.vbar(e)
                    ins = e.tensor_scalar(out=st3[:, 5:6], in0=st3[:, 5:6], scalar1=1e-5,
                                          scalar2=None, op0=ALU.add, op1=ALU.bypass)
                    ev[("var2", qg, tq)] = P.vfence(e, "v_ev")

                @P.seg("scalar")
                def _(e, qg=qg, tq=tq):
                    e.wait_ge(v_ev, ev[("var2", qg, tq)])
                    e.activation(out=st3[:, 6:7], in_=st3[:, 5:6], func=AF.Ln)
                    P.sbar(e)
                    ins = e.activation(out=st3[:, 7:8], in_=st3[:, 6:7], func=AF.Exp,
                                       scale=-0.5)
                    P.ev(ins, "s_ev")
                    ev[("rs20", qg, tq)] = P.sfence(e, "s_ev")

                @P.seg("vector")
                def _(e, qg=qg, tq=tq):
                    e.wait_ge(s_ev, ev[("rs20", qg, tq)])
                    P.vbar(e)
                    for it in range(2):
                        y = st3[:, 7:8]
                        e.tensor_scalar(out=st3[:, 8:9], in0=y, scalar1=st3[:, 5:6],
                                        scalar2=None, op0=ALU.mult, op1=ALU.bypass)
                        P.vbar(e)
                        e.tensor_scalar(out=st3[:, 8:9], in0=st3[:, 8:9], scalar1=y,
                                        scalar2=-0.5, op0=ALU.mult, op1=ALU.mult)
                        P.vbar(e)
                        e.tensor_scalar(out=st3[:, 8:9], in0=st3[:, 8:9], scalar1=1.5,
                                        scalar2=None, op0=ALU.add, op1=ALU.bypass)
                        P.vbar(e)
                        e.tensor_scalar(out=st3[:, 7:8], in0=y, scalar1=st3[:, 8:9],
                                        scalar2=None, op0=ALU.mult, op1=ALU.bypass)
                        P.vbar(e)
                    e.tensor_scalar(out=st3[:, 9:10], in0=st3[:, 2:3], scalar1=st3[:, 7:8],
                                    scalar2=-1.0, op0=ALU.mult, op1=ALU.mult)
                    P.vbar(e)
                    ins = e.tensor_scalar(out=x2[:], in0=x2[:],
                                          scalar1=st3[:, 7:8], scalar2=st3[:, 9:10],
                                          op0=ALU.mult, op1=ALU.add)
                    ev[("x2", qg, tq)] = P.ev(ins, "v_ev")

                @P.seg("gpsimd")
                def _(e, qt=qt, qg=qg, tq=tq):
                    e.wait_ge(v_ev, ev[("x2", qg, tq)])
                    ev[("outw", qg, tq)] = P.dma(
                        e, out[qt * 128:(qt + 1) * 128, :], x2[:], "d_g")



_NC_CACHE = []


def kernel(L0_attn_x, L0_attn_Wq, L0_attn_Wk, L0_attn_Wv, L0_attn_Wo,
           L0_attn_bo, L0_ffn_w1, L0_ffn_g1, L0_ffn_w2, L0_ffn_g2,
           L0_ffn_w3, L0_ffn_g3):
    x = np.ascontiguousarray(np.asarray(L0_attn_x, dtype=np.float32))
    Wq = np.ascontiguousarray(np.asarray(L0_attn_Wq, dtype=np.float32))
    Wk = np.ascontiguousarray(np.asarray(L0_attn_Wk, dtype=np.float32))
    Wv = np.ascontiguousarray(np.asarray(L0_attn_Wv, dtype=np.float32))
    Wo = np.ascontiguousarray(np.asarray(L0_attn_Wo, dtype=np.float32))
    bo = np.ascontiguousarray(np.asarray(L0_attn_bo, dtype=np.float32))
    w1T = np.ascontiguousarray(np.asarray(L0_ffn_w1, dtype=np.float32).T)
    w2T = np.ascontiguousarray(np.asarray(L0_ffn_w2, dtype=np.float32).T)
    w3T = np.ascontiguousarray(np.asarray(L0_ffn_w3, dtype=np.float32).T)
    g1 = np.ascontiguousarray(np.asarray(L0_ffn_g1, dtype=np.float32))
    g2 = np.ascontiguousarray(np.asarray(L0_ffn_g2, dtype=np.float32))
    g3 = np.ascontiguousarray(np.asarray(L0_ffn_g3, dtype=np.float32))

    if not _NC_CACHE:
        _NC_CACHE.append(build_program())
    nc = _NC_CACHE[0]
    in_maps = [_stage_core(c, x, Wq, Wk, Wv, Wo, bo, w1T, w2T, w3T, g1, g2, g3)
               for c in range(8)]
    res = run_bass_kernel_spmd(nc, in_maps, core_ids=list(range(8)))
    outp = np.empty((B, T, D), dtype=np.float32)
    for c in range(8):
        b, half = c // 2, c % 2
        outp[b, half * NQ:(half + 1) * NQ] = res.results[c]["out"]
    return outp


# version 02
# speedup vs baseline: 1.0219x; 1.0219x over previous
import numpy as np
import ml_dtypes
import concourse.bass as bass
import concourse.mybir as mybir
from concourse.bass_utils import run_bass_kernel_spmd
from contextlib import ExitStack

F32 = mybir.dt.float32
BF16 = mybir.dt.bfloat16
ALU = mybir.AluOpType
AF = mybir.ActivationFunctionType

BF = ml_dtypes.bfloat16

B, T, D = 4, 4096, 2048
H = 8192
HEADS, GROUPS, DH = 16, 4, 128
QG = HEADS // GROUPS
SWA = 1024
NQ = 2048
NS = 3072
NQT = 16
QB = 512
NQB = 4
MEGA_QB = 2
SMEGA = 2048
UN = 12
UNW = UN * 128
NEG = -1.0e30
KV_DO_V = True
KV_DO_NORM = True
KV_DO_K = True

M32 = 12582912.0
M64 = 6755399441055744.0
ATANH05 = 0.5493061443340548457
A05_HI = float(np.float32(ATANH05))
A05_LO = float(np.float32(ATANH05 - A05_HI))
ISQ128 = 1.0 / np.sqrt(np.float64(128.0))
ISQ_HI = float(np.float32(ISQ128))
ISQ_LO = float(np.float32(ISQ128 - ISQ_HI))



class Prog:

    ENGINES = ("sync", "tensor", "vector", "scalar", "gpsimd")

    def __init__(self, nc, ctx):
        self.nc = nc
        self.ctx = ctx
        self.block = ctx.enter_context(nc.Block())
        self.sems = {}
        self.cnt = {}
        self.idx = {}
        self.segs = {e: [] for e in self.ENGINES}

    def sem(self, name):
        if name not in self.sems:
            self.sems[name] = self.ctx.enter_context(self.nc.semaphore(name))
            self.cnt[name] = 0
        return self.sems[name]

    def bump(self, name, k=1):
        self.sem(name)
        self.cnt[name] += k
        return self.cnt[name]

    def seg(self, engine):
        dec = getattr(self.block, engine)

        def wrap(fn):
            dec(fn)
            return None
        return wrap

    NDMA = 4

    def tok(self, q, i):
        return (f"{q}{i % self.NDMA}", (i // self.NDMA + 1) * 16)

    def wge(self, e, token):
        e.wait_ge(self.sem(token[0]), token[1])

    def dma(self, e, out, in_, q):
        i = self.idx.setdefault(q, 0)
        self.idx[q] = i + 1
        name, val = self.tok(q, i)
        s = self.sem(name)
        if val > 16:
            e.wait_ge(s, val - 16)
        e.dma_start(out=out, in_=in_).then_inc(s, 16)
        self.cnt[name] = val
        return (name, val)

    def ev(self, instr, q, k=1):
        s = self.sem(q)
        instr.then_inc(s, k)
        self.cnt[q] += k
        return self.cnt[q]

    def _sp(self, e, eng, col):
        sc = self.spc if eng == "v" else self.spc2
        w = sc.shape[1] // 4
        sl = slice(col * w, (col + 1) * w)
        if eng == "v":
            return e.tensor_scalar(out=sc[:, sl], in0=sc[:, sl], scalar1=1.0,
                                   scalar2=None, op0=ALU.mult, op1=ALU.bypass)
        return e.copy(out=sc[:, sl], in_=sc[:, sl])

    def vsp(self, e, n=1):
        ins = None
        for k in range(n):
            ins = self._sp(e, "v", k % 2)
        return ins

    def ssp(self, e, n=1):
        ins = None
        for k in range(n):
            ins = self._sp(e, "s", k % 2)
        return ins

    def vbar(self, e):
        n = self.ev(self._sp(e, "v", 0), "vsq")
        e.wait_ge(self.sems["vsq"], n)

    def sbar(self, e):
        n = self.ev(self._sp(e, "s", 0), "ssq")
        e.wait_ge(self.sems["ssq"], n)

    def vfence(self, e, q, k=1):
        r = self.ev(self._sp(e, "v", 1), q, k)
        n = self.ev(self._sp(e, "v", 2), "vsq")
        e.wait_ge(self.sems["vsq"], n)
        return r

    def sfence(self, e, q, k=1):
        r = self.ev(self._sp(e, "s", 1), q, k)
        n = self.ev(self._sp(e, "s", 2), "ssq")
        e.wait_ge(self.sems["ssq"], n)
        return r

    def tfence(self, e, q, k=1):
        ins = e.transpose(self.dps2[0:128, 0:128], self.g_ident[:], self.g_ident[:])
        return self.ev(ins, q, k)



def _masks_for_core(half):
    out = np.zeros((NQT, 128, UNW), dtype=np.float32)
    for qt in range(NQT):
        j = qt // 4
        g = half * 16 + qt
        qidx = g * 128 + np.arange(128)[:, None]
        kbase = (4 * j + half * 16 - 8) * 128
        kidx = kbase + np.arange(UNW)[None, :]
        allowed = (kidx <= qidx) & (qidx - kidx < SWA) & (kidx >= 0)
        out[qt] = np.where(allowed, 0.0, NEG)
    return out.astype(BF)


def _stage_core(core, x, Wq, Wk, Wv, Wo, bo, w1T, w2T, w3T, g1, g2, g3):
    b, half = core // 2, core % 2
    xb = x[b]
    lo = half * NQ - SWA
    hi = half * NQ + NQ
    xt = np.zeros((D, NS), dtype=np.float32)
    s0 = max(lo, 0)
    xt[:, s0 - lo:hi - lo] = xb[s0:hi].T
    d = {
        "xT": np.ascontiguousarray(xt),
        "x_nat": np.ascontiguousarray(xb[half * NQ:half * NQ + NQ]),
        "Wq": Wq, "Wk": Wk, "Wv": Wv, "Wo": Wo,
        "bo_b": np.ascontiguousarray(np.broadcast_to(bo, (128, D))),
        "w1T": w1T, "w2T": w2T, "w3T": w3T,
        "g1b": np.ascontiguousarray(np.broadcast_to(g1, (128, D))),
        "g2b": np.ascontiguousarray(np.broadcast_to(g2, (128, D))),
        "g3b": np.ascontiguousarray(np.broadcast_to(g3, (128, H))),
        "g3c": np.ascontiguousarray(g3.reshape(H, 1)),
        "g3t": np.ascontiguousarray(g3.reshape(64, 128).T),
        "masks": _masks_for_core(half),
        "ident": np.eye(128, dtype=np.float32),
    }
    return d



def build_program(debug_taps=(), stop_after=None):
    nc = bass.Bass("TRN2", target_bir_lowering=False, debug=False)
    ctx = ExitStack()

    def inp(name, shape, dt=F32):
        return nc.dram_tensor(name, list(shape), dt, kind="ExternalInput").ap()

    def scr(name, shape, dt=F32):
        kind = "ExternalOutput" if name in debug_taps else "Internal"
        return nc.dram_tensor(name, list(shape), dt, kind=kind).ap()

    t_in = {
        "xT": inp("xT", (D, NS)),
        "x_nat": inp("x_nat", (NQ, D)),
        "Wq": inp("Wq", (D, D)),
        "Wk": inp("Wk", (D, 512)),
        "Wv": inp("Wv", (D, 512)),
        "Wo": inp("Wo", (D, D)),
        "bo_b": inp("bo_b", (128, D)),
        "w1T": inp("w1T", (D, H)),
        "w2T": inp("w2T", (D, H)),
        "w3T": inp("w3T", (H, D)),
        "g1b": inp("g1b", (128, D)),
        "g2b": inp("g2b", (128, D)),
        "g3b": inp("g3b", (128, H)),
        "g3c": inp("g3c", (H, 1)),
        "g3t": inp("g3t", (128, 64)),
        "masks": inp("masks", (NQT, 128, UNW), BF16),
        "ident": inp("ident", (128, 128)),
    }
    t_scr = {
        "wq_hl": scr("wq_hl", (2, HEADS, D, DH), BF16),
        "wo_hl": scr("wo_hl", (2, 4, D, QB), BF16),
        "t1": scr("t1", (D, H), BF16),
        "t2": scr("t2", (D, H), BF16),
        "t3": scr("t3", (16, H, 128), BF16),
        "xq1": scr("xq1", (NQ, D), BF16),
        "xq2": scr("xq2", (NQ, D), BF16),
        "x1": scr("x1", (NQ, D)),
        "hT": scr("hT", (H, NQ)),
        "stats": scr("stats", (128, 64)),
    }
    out = nc.dram_tensor("out", [NQ, D], F32, kind="ExternalOutput").ap()

    P = Prog(nc, ctx)
    emit_all(P, t_in, t_scr, out, stop_after=stop_after)
    ctx.close()
    return nc



def emit_all(P, t_in, t_scr, out, stop_after=None):
    nc = P.nc

    sb = lambda n, s, d=F32: P.ctx.enter_context(nc.sbuf_tensor(n, s, d))

    g = {}
    g["ident"] = sb("c_ident", [128, 128])
    g["identb"] = sb("c_identb", [128, 128], BF16)
    g["ones_col"] = sb("c_ones_col", [128, 1])
    g["ones_row"] = sb("c_ones_row", [1, 128])
    g["meff"] = sb("c_meff", [128, 1])
    g["sig"] = sb("c_sig", [128, 1])
    g["cs"] = sb("c_cs", [128, 1])
    g["drv"] = sb("c_drv", [128, 1])
    P.spc = sb("c_spc", [128, 64])
    P.spc2 = sb("c_spc2", [128, 16])
    g["drs"] = sb("c_drs", [128, 1])
    g["drg"] = sb("c_drg", [128, 1])
    for n in ("bnd1", "bnd2", "bnd3", "nbnd1", "nbnd2", "nbnd3",
              "a1", "a2", "a3"):
        g[n] = sb("c_" + n, [128, 1])
    for n in ("beta1", "beta2", "gam1", "gam2", "ssq3", "am3"):
        g[n] = sb("st_" + n, [128, NQT])
    g["mk2"] = sb("st_mk2", [1, 8])
    g["scalars"] = sb("st_scalars", [1, 32])
    g["dps"] = P.ctx.enter_context(nc.psum_tensor("dps", [128, 512], F32))
    P.g_ident = g["ident"]

    emit_p0_consts(P, t_in, g)
    emit_p1_weight_prep(P, t_in, t_scr, g)
    barrier(P, g)
    if stop_after == "p1":
        emit_debug_dump(P, g, t_scr)
        return
    for mega in range(2):
        if stop_after == "p2" and mega == 1:
            emit_debug_dump(P, g, t_scr)
            return
        with ExitStack() as mctx:
            kv = {}
            kv["k_hi"] = mctx.enter_context(nc.sbuf_tensor(f"kv_khi{mega}", [128, 4 * SMEGA], BF16))
            kv["k_lo"] = mctx.enter_context(nc.sbuf_tensor(f"kv_klo{mega}", [128, 4 * SMEGA], BF16))
            kv["v"] = mctx.enter_context(nc.sbuf_tensor(f"kv_v{mega}", [128, 4 * SMEGA], F32))
            emit_kv_build(P, t_in, g, kv, mega)
            barrier(P, g)
            if stop_after == "p2":
                continue
            for j in range(MEGA_QB):
                emit_attention_qb(P, t_in, t_scr, g, kv, mega, mega * MEGA_QB + j)
                barrier(P, g)
    if stop_after == "p3":
        emit_debug_dump(P, g, t_scr)
        return
    emit_ffn12(P, t_in, t_scr, g)
    barrier(P, g)
    if stop_after == "p4":
        emit_stats_dump(P, g, t_scr)
        emit_debug_dump(P, g, t_scr)
        return
    emit_ffn3(P, t_in, t_scr, g, out)


def emit_stats_dump(P, g, t_scr):
    @P.seg("gpsimd")
    def _(e):
        t = P.dma(e, t_scr["stats"][:, 16:32], g["ssq3"][:], "d_g")
        P.wge(e, t)
        t = P.dma(e, t_scr["stats"][:, 48:64], g["am3"][:], "d_g")
        P.wge(e, t)


def barrier(P, g):
    bar = P.sem("bar")
    base = P.cnt["bar"]
    target = base + 5
    P.cnt["bar"] = target
    dma_waits = [(P.sems[n], P.cnt[n]) for n in list(P.cnt)
                 if n.startswith("d_") and P.cnt[n] > 0]

    @P.seg("sync")
    def _(e):
        for s, v in dma_waits:
            e.wait_ge(s, v)
        e.nop().then_inc(bar, 1)
        e.wait_ge(bar, target)

    @P.seg("tensor")
    def _(e):
        e.wait_ge(bar, base + 4)
        e.transpose(g["dps"][0:128, 0:128], g["ident"][:], g["ident"][:]).then_inc(bar, 1)
        e.wait_ge(bar, target)

    @P.seg("vector")
    def _(e):
        e.tensor_copy(out=g["drv"][:], in_=P.spc[:, 0:1]).then_inc(bar, 1)
        e.wait_ge(bar, target)

    @P.seg("scalar")
    def _(e):
        e.copy(out=g["drs"][:], in_=P.spc2[:, 0:1]).then_inc(bar, 1)
        e.wait_ge(bar, target)

    @P.seg("gpsimd")
    def _(e):
        e.tensor_copy(out=g["drg"][:], in_=g["drg"][:]).then_inc(bar, 1)
        e.wait_ge(bar, target)


def emit_p0_consts(P, t_in, g):
    w = {}

    @P.seg("sync")
    def _(e):
        w["c"] = P.dma(e, g["ident"][:], t_in["ident"], "d_s")

    @P.seg("vector")
    def _(e):
        P.wge(e, w["c"])

        def init(dst, val):
            ii = g["ident"][0:dst.shape[0], 0:dst.shape[1]]
            e.tensor_scalar(out=dst, in0=ii, scalar1=0.0, scalar2=val,
                            op0=ALU.mult, op1=ALU.add)

        init(P.spc[:, 0:32], 1.0)
        ins = e.tensor_scalar(out=P.spc[:, 32:64], in0=g["ident"][:, 32:64],
                              scalar1=0.0, scalar2=1.0, op0=ALU.mult, op1=ALU.add)
        n = P.ev(ins, "vsq")
        e.wait_ge(P.sem("vsq"), n)
        init(g["sig"][:], 0.75)
        init(g["ones_col"][:], 1.0)
        init(g["ones_row"][:], 1.0)
        init(g["cs"][:], ISQ_HI)
        init(g["scalars"][:], 0.0)
        init(g["mk2"][:], 0.0)
        P.vbar(e)
        e.tensor_scalar(out=g["sig"][:], in0=g["sig"][:], scalar1=M64,
                        scalar2=None, op0=ALU.add, op1=ALU.bypass)
        e.tensor_scalar(out=g["cs"][:], in0=g["cs"][:], scalar1=ISQ_LO,
                        scalar2=None, op0=ALU.add, op1=ALU.bypass)
        P.vbar(e)
        e.tensor_scalar(out=g["sig"][:], in0=g["sig"][:], scalar1=M64,
                        scalar2=None, op0=ALU.subtract, op1=ALU.bypass)
        P.vbar(e)
        e.tensor_scalar(out=g["meff"][:], in0=g["sig"][:], scalar1=(M64 - M32),
                        scalar2=M32, op0=ALU.mult, op1=ALU.add)
        e.tensor_copy(out=g["identb"][:], in_=g["ident"][:])
        P.vfence(e, "v_ev")

    @P.seg("scalar")
    def _(e):
        P.wge(e, w["c"])
        e.activation(out=P.spc2[:, 0:8], in_=g["ident"][:, 0:8], func=AF.Copy,
                     bias=1.0, scale=0.0)
        ins = e.activation(out=P.spc2[:, 8:16], in_=g["ident"][:, 8:16],
                           func=AF.Copy, bias=1.0, scale=0.0)
        n = P.ev(ins, "ssq")
        e.wait_ge(P.sem("ssq"), n)

    @P.seg("gpsimd")
    def _(e):
        P.wge(e, w["c"])
        ins = e.tensor_scalar(out=g["drg"][:], in0=g["ident"][:, 0:1], scalar1=0.0,
                              scalar2=1.0, op0=ALU.mult, op1=ALU.add)
        n = P.ev(ins, "gsq")
        e.wait_ge(P.sem("gsq"), n)


def emit_p1_weight_prep(P, t_in, t_scr, g):
    nc = P.nc
    with ExitStack() as ctx:
        sbf = lambda n, s, d=F32: ctx.enter_context(nc.sbuf_tensor(n, s, d))
        land = sbf("p1_land", [128, H])
        land2 = sbf("p1_land2", [128, H])
        stage_h = sbf("p1_sh", [128, 2 * D], BF16)
        stage_l = sbf("p1_sl", [128, 2 * D], BF16)
        parts = sbf("p1_parts", [128, 4])
        P.dps2 = ctx.enter_context(nc.psum_tensor("p1_fence", [128, 512], F32))
        waste = sbf("p1_waste", [128, H])
        accs = sbf("p1_accs", [128, 8])
        ubuf = sbf("p1_u", [128, 2 * D])
        tstage = sbf("p1_t", [128, 2 * D], BF16)

        v_ev, t_ev = P.sem("v_ev"), P.sem("t_ev")

        ds0 = P.idx.get("d_s", 0)
        dg0 = P.idx.get("d_g", 0)
        v0 = P.cnt["v_ev"]
        v1 = v0 + 97
        v4 = v1 + 32
        t0 = P.cnt["t_ev"]

        @P.seg("vector")
        def _(e):
            e.tensor_scalar(out=accs[:], in0=g["ident"][:, 0:8], scalar1=0.0,
                            scalar2=None, op0=ALU.mult, op1=ALU.bypass)
            e.tensor_scalar(out=parts[:], in0=g["ident"][:, 0:4], scalar1=0.0,
                            scalar2=None, op0=ALU.mult, op1=ALU.bypass)
            P.vbar(e)
            for c in range(96):
                w3c = c >= 32
                buf = land if c % 2 == 0 else land2
                width = D if w3c else H
                mm = 2 if w3c else c // 16
                P.wge(e, P.tok("d_s", ds0 + c))
                e.tensor_scalar(out=waste[:, 0:width], in0=buf[:, 0:width],
                                scalar1=0.0, scalar2=None,
                                op0=ALU.max, op1=ALU.add,
                                accum_out=parts[:, 0:1])
                P.vbar(e)
                ins = e.tensor_scalar(out=buf[:, 0:width], in0=buf[:, 0:width],
                                      scalar1=0.0, scalar2=None,
                                      op0=ALU.min, op1=ALU.add,
                                      accum_out=parts[:, 1:2])
                P.ev(ins, "v_ev")
                P.vbar(e)
                e.tensor_tensor(out=accs[:, mm:mm + 1], in0=accs[:, mm:mm + 1],
                                in1=parts[:, 0:1], op=ALU.add)
                e.tensor_tensor(out=accs[:, 4 + mm:5 + mm], in0=accs[:, 4 + mm:5 + mm],
                                in1=parts[:, 1:2], op=ALU.add)
                P.vbar(e)
            e.tensor_tensor(out=accs[:, 0:3], in0=accs[:, 0:3],
                            in1=accs[:, 4:7], op=ALU.subtract)
            P.vfence(e, "v_ev")
            assert P.cnt["v_ev"] == v1

            e.wait_ge(t_ev, t0 + 1)
            sc = g["scalars"]
            e.tensor_scalar(out=sc[0:1, 0:3], in0=g["dps"][0:1, 0:3],
                            scalar1=1.0 / (D * H), scalar2=None,
                            op0=ALU.mult, op1=ALU.bypass)
            e.tensor_scalar(out=sc[0:1, 21:22], in0=g["ident"][0:1, 0:1],
                            scalar1=0.0, scalar2=A05_HI, op0=ALU.mult, op1=ALU.add)
            P.vbar(e)
            e.tensor_scalar(out=sc[0:1, 3:6], in0=sc[0:1, 0:3],
                            scalar1=-1.0, scalar2=1.0,
                            op0=ALU.mult, op1=ALU.add)
            e.tensor_scalar(out=sc[0:1, 21:22], in0=sc[0:1, 21:22],
                            scalar1=A05_LO, scalar2=None, op0=ALU.add, op1=ALU.bypass)
            e.tensor_scalar(out=sc[0:1, 18:21], in0=sc[0:1, 0:3],
                            scalar1=1e-8, scalar2=None, op0=ALU.add, op1=ALU.bypass)
            P.vbar(e)
            e.reciprocal(out=sc[0:1, 6:9], in_=sc[0:1, 3:6])
            P.vbar(e)
            e.tensor_tensor(out=sc[0:1, 9:12], in0=sc[0:1, 6:9],
                            in1=sc[0:1, 0:3], op=ALU.mult)
            e.tensor_scalar(out=sc[0:1, 24:27], in0=sc[0:1, 18:21],
                            scalar1=sc[0:1, 21:22], scalar2=None,
                            op0=ALU.mult, op1=ALU.bypass)
            P.vbar(e)
            e.tensor_scalar(out=sc[0:1, 9:12], in0=sc[0:1, 9:12],
                            scalar1=2.0, scalar2=None, op0=ALU.mult, op1=ALU.bypass)
            P.vbar(e)
            e.tensor_scalar(out=sc[0:1, 12:15], in0=sc[0:1, 9:12],
                            scalar1=0.2, scalar2=-0.25, op0=ALU.mult, op1=ALU.add)
            P.vbar(e)
            for coef in (1.0 / 3.0, -0.5, 1.0):
                e.tensor_tensor(out=sc[0:1, 12:15], in0=sc[0:1, 12:15],
                                in1=sc[0:1, 9:12], op=ALU.mult)
                P.vbar(e)
                e.tensor_scalar(out=sc[0:1, 12:15], in0=sc[0:1, 12:15],
                                scalar1=coef, scalar2=None, op0=ALU.add, op1=ALU.bypass)
                P.vbar(e)
            e.tensor_tensor(out=sc[0:1, 15:18], in0=sc[0:1, 12:15],
                            in1=sc[0:1, 9:12], op=ALU.mult)
            P.vbar(e)
            e.tensor_scalar(out=sc[0:1, 15:18], in0=sc[0:1, 15:18],
                            scalar1=0.5, scalar2=None, op0=ALU.mult, op1=ALU.bypass)
            P.vfence(e, "v_ev")

            e.wait_ge(t_ev, t0 + 2)
            for m, n in enumerate(("a1", "a2", "a3")):
                e.tensor_copy(out=g[n][:], in_=g["dps"][:, 15 + m:16 + m])
            for m, n in enumerate(("bnd1", "bnd2", "bnd3")):
                e.tensor_copy(out=g[n][:], in_=g["dps"][:, 24 + m:25 + m])
            for m, n in enumerate(("nbnd1", "nbnd2", "nbnd3")):
                e.tensor_scalar(out=g[n][:], in0=g["dps"][:, 24 + m:25 + m],
                                scalar1=-1.0, scalar2=None,
                                op0=ALU.mult, op1=ALU.bypass)
            P.vfence(e, "v_ev")

        v1b = v1 + 2
        v4 = v1b + 32

        @P.seg("vector")
        def _(e):
            for c in range(32):
                buf = land if c % 2 == 0 else land2
                p = c % 2
                P.wge(e, P.tok("d_s", ds0 + 96 + c))
                if c >= 2:
                    P.wge(e, P.tok("d_g", dg0 + 2 * (c - 2)))
                    P.wge(e, P.tok("d_g", dg0 + 2 * (c - 2) + 1))
                e.tensor_copy(out=stage_h[:, p * D:(p + 1) * D], in_=buf[:, 0:D])
                P.vbar(e)
                ins = e.tensor_tensor(out=stage_l[:, p * D:(p + 1) * D],
                                      in0=buf[:, 0:D],
                                      in1=stage_h[:, p * D:(p + 1) * D],
                                      op=ALU.subtract)
                P.ev(ins, "v_ev")
            assert P.cnt["v_ev"] == v4, (P.cnt["v_ev"], v4)

        @P.seg("vector")
        def _(e):
            for c in range(64):
                sl = c % 4
                buf = (land, land2)[sl // 2]
                off = (sl % 2) * D
                p = c % 2
                P.wge(e, P.tok("d_s", ds0 + 128 + c))
                if c >= 2:
                    P.wge(e, P.tok("d_g", dg0 + 64 + (c - 2)))
                e.tensor_scalar(out=ubuf[:, p * D:(p + 1) * D],
                                in0=buf[:, off:off + D],
                                scalar1=g["nbnd3"][:], scalar2=1.0,
                                op0=ALU.is_ge, op1=ALU.subtract)
                P.vbar(e)
                ins = e.scalar_tensor_tensor(out=tstage[:, p * D:(p + 1) * D],
                                             in0=buf[:, off:off + D],
                                             scalar=g["bnd3"][:],
                                             in1=ubuf[:, p * D:(p + 1) * D],
                                             op0=ALU.is_ge, op1=ALU.add)
                P.ev(ins, "v_ev")
            assert P.cnt["v_ev"] == v4 + 64

        @P.seg("tensor")
        def _(e):
            e.wait_ge(v_ev, v1)
            e.matmul(g["dps"][0:1, 0:3], g["ones_col"][:],
                     accs[:, 0:3], start=True, stop=True)
            P.tfence(e, "t_ev")
            e.wait_ge(v_ev, v1 + 1)
            e.matmul(g["dps"][:, 15:27], g["ones_row"][:],
                     g["scalars"][0:1, 15:27], start=True, stop=True)
            P.tfence(e, "t_ev")

        @P.seg("gpsimd")
        def _(e):
            for c in range(32):
                p = c % 2
                e.wait_ge(v_ev, v1 + 2 + c + 1)
                r = c % 16
                for hl, stg in ((0, stage_h), (1, stage_l)):
                    src = stg[:, p * D:(p + 1) * D]
                    if c < 16:
                        dst = t_scr["wq_hl"][hl, :, r * 128:(r + 1) * 128, :]
                        P.dma(e, dst.rearrange("h p m -> p h m"),
                              src.rearrange("p (h m) -> p h m", h=16), "d_g")
                    else:
                        dst = t_scr["wo_hl"][hl, :, r * 128:(r + 1) * 128, :]
                        P.dma(e, dst.rearrange("a p b -> p a b"),
                              src.rearrange("p (a b) -> p a b", a=4), "d_g")
            for c in range(64):
                p = c % 2
                e.wait_ge(v_ev, v4 + c + 1)
                P.dma(e, t_scr["t3"][:, c * 128:(c + 1) * 128, :]
                      .rearrange("a p d -> p a d"),
                      tstage[:, p * D:(p + 1) * D]
                      .rearrange("p (a d) -> p a d", a=16), "d_g")

        @P.seg("sync")
        def _(e):
            mats = [t_in["w1T"], t_in["w2T"]]
            for c in range(96):
                if c >= 2:
                    e.wait_ge(v_ev, v0 + c - 1)
                buf = land if c % 2 == 0 else land2
                if c < 32:
                    m, r = divmod(c, 16)
                    P.dma(e, buf[:, 0:H], mats[m][r * 128:(r + 1) * 128, 0:H], "d_s")
                else:
                    r = c - 32
                    P.dma(e, buf[:, 0:D], t_in["w3T"][r * 128:(r + 1) * 128, 0:D], "d_s")
            for c in range(32):
                if c >= 2:
                    e.wait_ge(v_ev, v1 + 2 + c - 1)
                else:
                    e.wait_ge(v_ev, v0 + 96)
                src = t_in["Wq"] if c < 16 else t_in["Wo"]
                r = c % 16
                buf = land if c % 2 == 0 else land2
                P.dma(e, buf[:, 0:D], src[r * 128:(r + 1) * 128, 0:D], "d_s")
            for c in range(64):
                if c >= 4:
                    e.wait_ge(v_ev, v4 + c - 3)
                else:
                    e.wait_ge(v_ev, v4)
                sl = c % 4
                buf = (land, land2)[sl // 2]
                off = (sl % 2) * D
                P.dma(e, buf[:, off:off + D],
                      t_in["w3T"][c * 128:(c + 1) * 128, 0:D], "d_s")


def emit_debug_dump(P, g, t_scr):
    cols = [("sig", g["sig"]), ("meff", g["meff"]), ("cs", g["cs"]),
            ("a1", g["a1"]), ("a2", g["a2"]), ("a3", g["a3"]),
            ("bnd1", g["bnd1"]), ("bnd2", g["bnd2"]), ("bnd3", g["bnd3"]),
            ("nbnd1", g["nbnd1"]), ("nbnd2", g["nbnd2"]), ("nbnd3", g["nbnd3"])]

    @P.seg("gpsimd")
    def _(e):
        with P.nc.allow_non_contiguous_dma(reason="debug dump"):
            for i, (n, t) in enumerate(cols):
                tk = P.dma(e, t_scr["stats"][:, i:i + 1], t[:], "d_g")
                P.wge(e, tk)
            tk = P.dma(e, t_scr["stats"][0:1, 32:48][0:1, 0:16], g["scalars"][0:1, 0:16], "d_g")
            P.wge(e, tk)
        for s in range(P.NDMA):
            nm = f"d_g{s}"
            if nm in P.cnt:
                e.wait_ge(P.sems[nm], P.cnt[nm])


def emit_kv_build(P, t_in, g, kv, mega):
    nc = P.nc
    with ExitStack() as ctx:
        sbf = lambda n, s, d=F32: ctx.enter_context(nc.sbuf_tensor(n, s, d))
        wland = sbf(f"kv_wl{mega}", [128, 4096])
        wkhi = sbf(f"kv_wkhi{mega}", [128, H], BF16)
        wklo = sbf(f"kv_wklo{mega}", [128, H], BF16)
        wvhi = sbf(f"kv_wvhi{mega}", [128, H], BF16)
        wvlo = sbf(f"kv_wvlo{mega}", [128, H], BF16)
        xland = sbf(f"kv_xl{mega}", [128, 4096])
        xhi = sbf(f"kv_xhi{mega}", [128, 4096], BF16)
        xlo = sbf(f"kv_xlo{mega}", [128, 4096], BF16)
        ksq = sbf(f"kv_ksq{mega}", [128, 2 * 256])
        vt = sbf(f"kv_vt{mega}", [128, 2 * 256])
        kps = [ctx.enter_context(nc.psum_tensor(f"kv_kp{mega}_{i}", [128, 512], F32))[:, 0:256] for i in range(2)]
        vps = [ctx.enter_context(nc.psum_tensor(f"kv_vp{mega}_{i}", [128, 512], F32))[:, 0:256] for i in range(2)]
        tps = [ctx.enter_context(nc.psum_tensor(f"kv_tp{mega}_{i}", [128, 512], F32))[:, 0:256] for i in range(2)]

        v_ev, s_ev, t_ev = P.sem("v_ev"), P.sem("s_ev"), P.sem("t_ev")

        ds0 = P.idx.get("d_s", 0)
        v0 = P.cnt["v_ev"]
        s0 = P.cnt["s_ev"]
        t0 = P.cnt["t_ev"]
        scol0 = 1024 * mega

        VSPLIT = {}
        VKLO = {}
        VKSQ = {}
        VVT = {}
        VVFIN = {}
        VMK = {}
        SKHI = {idx: s0 + 2 * idx + 1 for idx in range(32)}
        SKSQ = {idx: s0 + 2 * idx + 2 for idx in range(32)}
        TK = {idx: t0 + idx * 4 + 1 for idx in range(32)}
        TV = {idx: t0 + idx * 4 + 2 for idx in range(32)}
        TVT = {idx: t0 + idx * 4 + 3 for idx in range(32)}
        TNORM = {idx: t0 + idx * 4 + 4 for idx in range(32)}
        P.cnt["s_ev"] += 64
        P.cnt["t_ev"] += 128

        @P.seg("vector")
        def _(e):
            for hidx, (whi, wlo) in enumerate(((wkhi, wklo), (wkhi, wklo),
                                               (wvhi, wvlo), (wvhi, wvlo))):
                hf = hidx % 2
                sl = slice(hf * 4096, (hf + 1) * 4096)
                P.wge(e, P.tok("d_s", ds0 + hidx))
                e.tensor_copy(out=whi[:, sl], in_=wland[:])
                P.vbar(e)
                ins = e.tensor_tensor(out=wlo[:, sl], in0=wland[:],
                                      in1=whi[:, sl], op=ALU.subtract)
                VSPLIT["w%d" % hidx] = P.ev(ins, "v_ev")
            VSPLIT["v"] = VSPLIT["w3"]

        @P.seg("vector")
        def _(e):
            for c in range(8):
                xsl = slice(0, 4096)
                P.wge(e, P.tok("d_s", ds0 + 4 + c))
                if c >= 1:
                    e.wait_ge(t_ev, TV[(c - 1) * 4 + 3])
                e.tensor_copy(out=xhi[:, xsl], in_=xland[:, xsl])
                P.vbar(e)
                ins = e.tensor_tensor(out=xlo[:, xsl], in0=xland[:, xsl],
                                      in1=xhi[:, xsl], op=ALU.subtract)
                VSPLIT[c] = P.ev(ins, "v_ev")
                for gg in range(4):
                    idx = c * 4 + gg
                    pb = idx % 2
                    off = 2048 * gg + 256 * c
                    if KV_DO_K:
                        e.wait_ge(t_ev, TK[idx])
                        P.vbar(e)
                        e.wait_ge(s_ev, SKSQ[idx])
                        ins = e.tensor_tensor(out=kv["k_lo"][:, off:off + 256],
                                              in0=kps[pb][:],
                                              in1=kv["k_hi"][:, off:off + 256],
                                              op=ALU.subtract)
                        VKLO[idx] = P.ev(ins, "v_ev")
                    else:
                        VKLO[idx] = P.vfence(e, "v_ev")
                    if KV_DO_V:
                        e.wait_ge(t_ev, TV[idx])
                        P.vbar(e)
                        ins = e.tensor_copy(out=vt[:, pb * 256:(pb + 1) * 256],
                                            in_=vps[pb][:])
                        VVT[idx] = P.ev(ins, "v_ev")
                        e.wait_ge(t_ev, TVT[idx])
                        P.vbar(e)
                        ins = e.tensor_copy(out=kv["v"][:, off:off + 256], in_=tps[pb][:])
                        VVFIN[idx] = P.ev(ins, "v_ev")
                    else:
                        VVT[idx] = P.vfence(e, "v_ev")
                        VVFIN[idx] = P.vfence(e, "v_ev")
                    if not KV_DO_NORM:
                        VMK[idx] = P.vfence(e, "v_ev")
                        continue
                    e.wait_ge(t_ev, TNORM[idx])
                    P.vbar(e)
                    e.reduce_max(out=g["scalars"][0:1, 28 + pb:29 + pb],
                                 in_=g["dps"][0:1, 0:256],
                                 axis=mybir.AxisListType.X)
                    P.vbar(e)
                    ins = e.tensor_tensor(out=g["mk2"][0:1, gg:gg + 1],
                                          in0=g["mk2"][0:1, gg:gg + 1],
                                          in1=g["scalars"][0:1, 28 + pb:29 + pb],
                                          op=ALU.max)
                    VMK[idx] = P.ev(ins, "v_ev")

        @P.seg("scalar")
        def _(e):
            for c in range(8):
                for gg in range(4):
                    idx = c * 4 + gg
                    pb = idx % 2
                    off = 2048 * gg + 256 * c
                    if KV_DO_K:
                        e.wait_ge(t_ev, TK[idx])
                        P.sbar(e)
                        e.copy(out=kv["k_hi"][:, off:off + 256],
                               in_=kps[pb][:]).then_inc(s_ev, 1)
                        e.activation(out=ksq[:, pb * 256:(pb + 1) * 256],
                                     in_=kps[pb][:], func=AF.Square).then_inc(s_ev, 1)
                    else:
                        P.sbar(e)
                        e.copy(out=ksq[:, 0:16], in_=P.spc2[:]).then_inc(s_ev, 1)
                        e.copy(out=ksq[:, 0:16], in_=P.spc2[:]).then_inc(s_ev, 1)

        @P.seg("tensor")
        def _(e):
            e.wait_ge(P.sems["v_ev"], VSPLIT["v"])
            for c in range(8):
                xsl0 = 0
                for gg in range(4):
                    idx = c * 4 + gg
                    pb = idx % 2
                    if gg == 0:
                        e.wait_ge(P.sems["v_ev"], VSPLIT[c])
                    if not KV_DO_K:
                        for _ in range(4 if not KV_DO_V else 2):
                            e.transpose(tps[pb][:, 0:128], g["ident"][:],
                                        g["ident"][:]).then_inc(t_ev, 1)
                        continue
                    if idx >= 2:
                        e.wait_ge(P.sems["v_ev"], VKLO[idx - 2])
                        e.wait_ge(P.sems["s_ev"], SKSQ[idx - 2])
                    last = None
                    for ec in range(16):
                        lh = wkhi[:, ec * 512 + gg * 128: ec * 512 + gg * 128 + 128]
                        ll = wklo[:, ec * 512 + gg * 128: ec * 512 + gg * 128 + 128]
                        rh = xhi[:, xsl0 + ec * 256: xsl0 + (ec + 1) * 256]
                        rl = xlo[:, xsl0 + ec * 256: xsl0 + (ec + 1) * 256]
                        e.matmul(kps[pb][:], lh, rh, start=(ec == 0), stop=False)
                        e.matmul(kps[pb][:], lh, rl, start=False, stop=False)
                        last = e.matmul(kps[pb][:], ll, rh, start=False,
                                        stop=(ec == 15))
                    last.then_inc(t_ev, 1)
                    if not KV_DO_V:
                        e.transpose(tps[pb][:, 0:128], g["ident"][:], g["ident"][:]).then_inc(t_ev, 1)
                        e.transpose(tps[pb][:, 0:128], g["ident"][:], g["ident"][:]).then_inc(t_ev, 1)
                        if KV_DO_NORM:
                            e.wait_ge(P.sems["s_ev"], SKSQ[idx])
                            if idx >= 1:
                                e.wait_ge(P.sems["v_ev"], VMK[idx - 1])
                            ins = e.matmul(g["dps"][0:1, 0:256], g["ones_col"][:],
                                           ksq[:, pb * 256:(pb + 1) * 256],
                                           start=True, stop=True)
                            ins.then_inc(t_ev, 1)
                        else:
                            e.transpose(tps[pb][:, 0:128], g["ident"][:], g["ident"][:]).then_inc(t_ev, 1)
                        continue
                    if idx >= 2:
                        e.wait_ge(P.sems["v_ev"], VVT[idx - 2])
                    for ec in range(16):
                        lh = wvhi[:, ec * 512 + gg * 128: ec * 512 + gg * 128 + 128]
                        ll = wvlo[:, ec * 512 + gg * 128: ec * 512 + gg * 128 + 128]
                        rh = xhi[:, xsl0 + ec * 256: xsl0 + (ec + 1) * 256]
                        rl = xlo[:, xsl0 + ec * 256: xsl0 + (ec + 1) * 256]
                        e.matmul(vps[pb][:], lh, rh, start=(ec == 0), stop=False)
                        e.matmul(vps[pb][:], lh, rl, start=False, stop=False)
                        last = e.matmul(vps[pb][:], ll, rh, start=False,
                                        stop=(ec == 15))
                    last.then_inc(t_ev, 1)
                    e.wait_ge(P.sems["v_ev"], VVT[idx])
                    if idx >= 2:
                        e.wait_ge(P.sems["v_ev"], VVFIN[idx - 2])
                    e.transpose(tps[pb][:, 0:128], vt[:, pb * 256:pb * 256 + 128],
                                g["ident"][:])
                    ins = e.transpose(tps[pb][:, 128:256],
                                      vt[:, pb * 256 + 128:pb * 256 + 256],
                                      g["ident"][:])
                    ins.then_inc(t_ev, 1)
                    e.wait_ge(P.sems["s_ev"], SKSQ[idx])
                    if idx >= 1:
                        e.wait_ge(P.sems["v_ev"], VMK[idx - 1])
                    ins = e.matmul(g["dps"][0:1, 0:256], g["ones_col"][:],
                                   ksq[:, pb * 256:(pb + 1) * 256],
                                   start=True, stop=True)
                    ins.then_inc(t_ev, 1)

        @P.seg("sync")
        def _(e):
            for hidx, wsrc in enumerate((t_in["Wk"], t_in["Wk"],
                                         t_in["Wv"], t_in["Wv"])):
                if hidx >= 1:
                    e.wait_ge(P.sems["v_ev"], VSPLIT["w%d" % (hidx - 1)])
                hf = hidx % 2
                P.dma(e, wland[:].rearrange("p (a d) -> p a d", a=8),
                      wsrc[hf * 1024:(hf + 1) * 1024].rearrange(
                          "(a p) d -> p a d", p=128), "d_s")
            for c in range(8):
                if c == 0:
                    e.wait_ge(P.sems["v_ev"], VSPLIT["w3"])
                else:
                    e.wait_ge(P.sems["v_ev"], VSPLIT[c - 1])
                col = scol0 + c * 256
                P.dma(e, xland[:].rearrange("p (a s) -> p a s", a=16),
                      t_in["xT"][:, col:col + 256].rearrange("(a p) s -> p a s", p=128),
                      "d_s")


def emit_attention_qb(P, t_in, t_scr, g, kv, mega, qb):
    nc = P.nc
    j = qb - 2 * mega
    v_ev, s_ev, t_ev = P.sem("v_ev"), P.sem("s_ev"), P.sem("t_ev")

    with ExitStack() as qctx:
        sbq = lambda n, s, d=F32: qctx.enter_context(nc.sbuf_tensor(f"{n}_{qb}", s, d))
        att_hi = sbq("at_hi", [128, 16 * 512], BF16)
        att_lo = sbq("at_lo", [128, 16 * 512], BF16)
        S_a = qctx.enter_context(nc.psum_tensor(f"ps_sa{qb}", [128, 1536], F32))
        S_b = qctx.enter_context(nc.psum_tensor(f"ps_sb{qb}", [128, 1536], F32))
        G = qctx.enter_context(nc.psum_tensor(f"ps_g{qb}", [128, 512], F32))
        SPS = [S_a, S_b]

        ev = {}

        with ExitStack() as actx:
            sba = lambda n, s, d=F32: actx.enter_context(nc.sbuf_tensor(f"{n}_{qb}", s, d))
            xqhi = sba("xqhi", [128, 8192], BF16)
            xqlo = sba("xqlo", [128, 8192], BF16)
            xql_ctx = ExitStack()
            xql = xql_ctx.enter_context(nc.sbuf_tensor(f"xql_{qb}", [128, 4096], F32))

            qc = (4 * qb + 8) * 128

            @P.seg("sync")
            def _(e):
                ev["xq_dma0"] = P.dma(
                    e, xql[:].rearrange("p (a s) -> p a s", a=8),
                    t_in["xT"][0:1024, qc:qc + 512].rearrange("(a p) s -> p a s", p=128),
                    "d_s")

            def xqsplit_half(half, key, dkey):
                @P.seg("vector")
                def _(e):
                    P.wge(e, ev[dkey])
                    sl = slice(half * 4096, (half + 1) * 4096)
                    e.tensor_copy(out=xqhi[:, sl], in_=xql[:])
                    P.vbar(e)
                    ins = e.tensor_tensor(out=xqlo[:, sl], in0=xql[:],
                                          in1=xqhi[:, sl], op=ALU.subtract)
                    ev[key] = P.ev(ins, "v_ev")

            xqsplit_half(0, "xqsplit0", "xq_dma0")

            @P.seg("sync")
            def _(e):
                e.wait_ge(P.sems["v_ev"], ev["xqsplit0"])
                ev["xq_dma1"] = P.dma(
                    e, xql[:].rearrange("p (a s) -> p a s", a=8),
                    t_in["xT"][1024:2048, qc:qc + 512].rearrange("(a p) s -> p a s", p=128),
                    "d_s")

            xqsplit_half(1, "xqsplit", "xq_dma1")
            xql_ctx.close()

            wqh = [sba(f"wqh{k}", [128, 2048], BF16) for k in range(2)]
            wql = [sba(f"wql{k}", [128, 2048], BF16) for k in range(2)]
            qth = sba("qth", [128, 512], BF16)
            qtl = sba("qtl", [128, 512], BF16)
            p_sb = sba("p_sb", [128, 2 * 1536])
            pt_sb = sba("pt_sb", [128, 12 * 512])
            mask_sb = sba("mask_sb", [128, 4 * 1536], BF16)
            qsq = sba("qsq", [128, 512])
            rrow = sba("rrow", [1, 512])
            recipb = sba("recipb", [128, 512])
            attn_sb = sba("attn_sb", [128, 512])
            rows = sba("rows", [128, 16])

            @P.seg("gpsimd")
            def _(e):
                ev["mask_dma"] = [
                    P.dma(e, mask_sb[:, i * 1536:(i + 1) * 1536],
                          t_in["masks"][4 * qb + i], "d_g")
                    for i in range(4)]

            for h in range(16):
                gg = h // 4
                kvo = gg * 2048 + 4 * j * 128

                @P.seg("sync")
                def _(e):
                    if h >= 2:
                        e.wait_ge(t_ev, ev[("qp", h - 2)])
                    elif h == 0:
                        e.wait_ge(v_ev, ev["xqsplit"])
                    toks = []
                    for hl, buf in ((0, wqh[h % 2]), (1, wql[h % 2])):
                        toks.append(P.dma(
                            e, buf[:].rearrange("p (a m) -> p a m", a=16),
                            t_scr["wq_hl"][hl, h].rearrange("(a p) m -> p a m", p=128),
                            "d_s"))
                    ev[("wq", h)] = toks

                @P.seg("tensor")
                def _(e):
                    for t in ev[("wq", h)]:
                        P.wge(e, t)
                    if h == 0:
                        e.wait_ge(v_ev, ev["xqsplit"])
                    else:
                        e.wait_ge(v_ev, ev[("attn", h - 1)])
                    last = None
                    for ec in range(16):
                        lh = wqh[h % 2][:, ec * 128:(ec + 1) * 128]
                        ll = wql[h % 2][:, ec * 128:(ec + 1) * 128]
                        rh = xqhi[:, ec * 512:(ec + 1) * 512]
                        rl = xqlo[:, ec * 512:(ec + 1) * 512]
                        e.matmul(G[:], lh, rh, start=(ec == 0), stop=False)
                        e.matmul(G[:], lh, rl, start=False, stop=False)
                        last = e.matmul(G[:], ll, rh, start=False, stop=(ec == 15))
                    ev[("qp", h)] = P.ev(last, "t_ev")

                @P.seg("scalar")
                def _(e):
                    e.wait_ge(t_ev, ev[("qp", h)])
                    P.sbar(e)
                    ins = e.copy(out=qth[:], in_=G[:])
                    ev[("qh", h)] = P.ev(ins, "s_ev")
                    ins = e.activation(out=qsq[:], in_=G[:], func=AF.Square)
                    ev[("qsq", h)] = P.ev(ins, "s_ev")

                @P.seg("vector")
                def _(e):
                    e.wait_ge(s_ev, ev[("qsq", h)])
                    ins = e.tensor_tensor(out=qtl[:], in0=G[:], in1=qth[:],
                                          op=ALU.subtract)
                    ev[("ql", h)] = P.ev(ins, "v_ev")

                @P.seg("tensor")
                def _(e):
                    e.wait_ge(s_ev, ev[("qsq", h)])
                    if h >= 1:
                        e.wait_ge(v_ev, ev[("attn", h - 1)])
                    ins = e.matmul(g["dps"][0:1, 0:512], g["ones_col"][:],
                                   qsq[:], start=True, stop=True)
                    ev[("qsqm", h)] = P.ev(ins, "t_ev")

                @P.seg("vector")
                def _(e):
                    e.wait_ge(t_ev, ev[("qsqm", h)])
                    P.vbar(e)
                    e.reduce_max(out=rows[0:1, 9:10], in_=g["dps"][0:1, 0:512],
                                 axis=mybir.AxisListType.X)
                    P.vbar(e)
                    e.tensor_tensor(out=rows[0:1, 10:11], in0=rows[0:1, 9:10],
                                    in1=g["mk2"][0:1, gg:gg + 1], op=ALU.add)
                    P.vbar(e)
                    e.tensor_scalar(out=rows[0:1, 11:12], in0=rows[0:1, 10:11],
                                    scalar1=g["cs"][0:1, 0:1], scalar2=-0.5,
                                    op0=ALU.mult, op1=ALU.mult)
                    ev[("bias1", h)] = P.vfence(e, "v_ev")

                @P.seg("tensor")
                def _(e):
                    e.wait_ge(v_ev, ev[("bias1", h)])
                    ins = e.matmul(g["dps"][:, 0:1], g["ones_row"][:],
                                   rows[0:1, 11:12], start=True, stop=True)
                    ev[("bbc", h)] = P.ev(ins, "t_ev")

                @P.seg("vector")
                def _(e):
                    e.wait_ge(t_ev, ev[("bbc", h)])
                    P.vbar(e)
                    e.tensor_copy(out=rows[:, 8:9], in_=g["dps"][:, 0:1])
                    ev[("bias", h)] = P.vfence(e, "v_ev")

                def sc(i):
                    si = h * 4 + i

                    @P.seg("tensor")
                    def _(e):
                        if si >= 2:
                            e.wait_ge(v_ev, ev[("ptev", si - 2)])
                        e.wait_ge(v_ev, ev[("ql", h)])
                        S = SPS[si % 2]
                        for li, (lt, rsel) in enumerate(
                                ((qth, "h"), (qth, "l"), (qtl, "h"))):
                            for c in range(3):
                                rh = (kv["k_hi"] if rsel == "h" else kv["k_lo"])
                                rhs = rh[:, kvo + c * 512:kvo + (c + 1) * 512]
                                ins = e.matmul(S[:, c * 512:(c + 1) * 512],
                                               lt[:, i * 128:(i + 1) * 128], rhs,
                                               start=(li == 0), stop=(li == 2))
                        ev[("sc", si)] = P.ev(ins, "t_ev")

                    @P.seg("vector")
                    def _(e):
                        e.wait_ge(t_ev, ev[("sc", si)])
                        if h == 0:
                            for t in ev["mask_dma"]:
                                P.wge(e, t)
                        S = SPS[si % 2]
                        P.vbar(e)
                        for c in range(3):
                            ins = e.tensor_tensor(
                                out=S[:, c * 512:(c + 1) * 512],
                                in0=S[:, c * 512:(c + 1) * 512],
                                in1=mask_sb[:, i * 1536 + c * 512:i * 1536 + (c + 1) * 512],
                                op=ALU.add)
                        ev[("mask", si)] = P.ev(ins, "v_ev")

                    @P.seg("scalar")
                    def _(e):
                        e.wait_ge(v_ev, ev[("mask", si)])
                        e.wait_ge(v_ev, ev[("bias", h)])
                        if si >= 2:
                            e.wait_ge(t_ev, ev[("pt", si - 2)])
                        psl = slice((si % 2) * 1536, (si % 2 + 1) * 1536)
                        ins = e.activation(out=p_sb[:, psl], in_=SPS[si % 2][:],
                                           func=AF.Exp, bias=rows[:, 8:9],
                                           scale=g["cs"][:],
                                           accum_out=rows[:, i:i + 1])
                        P.ev(ins, "s_ev")
                        ev[("exp", si)] = P.sfence(e, "s_ev")

                    @P.seg("vector")
                    def _(e):
                        e.wait_ge(s_ev, ev[("exp", si)])
                        P.vbar(e)
                        e.reciprocal(out=rows[:, 4 + i:5 + i], in_=rows[:, i:i + 1])

                def pt(i):
                    si = h * 4 + i

                    @P.seg("tensor")
                    def _(e):
                        e.wait_ge(s_ev, ev[("exp", si)])
                        S = SPS[si % 2]
                        po = (si % 2) * 1536
                        for k in range(12):
                            ins = e.transpose(S[:, k * 128:(k + 1) * 128],
                                              p_sb[:, po + k * 128:po + (k + 1) * 128],
                                              g["ident"][:])
                        ev[("pt", si)] = P.ev(ins, "t_ev")

                    @P.seg("vector")
                    def _(e):
                        e.wait_ge(t_ev, ev[("pt", si)])
                        P.vbar(e)
                        S = SPS[si % 2]
                        ptv = pt_sb[:].rearrange("p (k q) -> p k q", k=12)
                        for c in range(3):
                            ins = e.tensor_copy(
                                out=ptv[:, 4 * c:4 * c + 4, i * 128:(i + 1) * 128],
                                in_=S[:, c * 512:(c + 1) * 512].rearrange("p (a q) -> p a q", a=4))
                        ev[("ptev", si)] = P.ev(ins, "v_ev")

                for i in (0, 1):
                    sc(i)
                pt(0)
                sc(2)
                pt(1)
                sc(3)
                pt(2)
                pt(3)

                @P.seg("vector")
                def _(e):
                    ev[("rec", h)] = P.vfence(e, "v_ev")

                @P.seg("tensor")
                def _(e):
                    e.wait_ge(v_ev, ev[("rec", h)])
                    for i in range(4):
                        ins = e.transpose(g["dps"][0:1, i * 128:(i + 1) * 128],
                                          rows[:, 4 + i:5 + i], g["ident"][:])
                    ev[("rt", h)] = P.ev(ins, "t_ev")

                @P.seg("vector")
                def _(e):
                    e.wait_ge(t_ev, ev[("rt", h)])
                    P.vbar(e)
                    e.tensor_copy(out=rrow[:], in_=g["dps"][0:1, 0:512])
                    ev[("rr", h)] = P.vfence(e, "v_ev")

                @P.seg("tensor")
                def _(e):
                    e.wait_ge(v_ev, ev[("rr", h)])
                    ins = e.matmul(g["dps"][:, 0:512], g["ones_row"][:], rrow[:],
                                   start=True, stop=True)
                    ev[("rb", h)] = P.ev(ins, "t_ev")

                @P.seg("vector")
                def _(e):
                    e.wait_ge(t_ev, ev[("rb", h)])
                    P.vbar(e)
                    ins = e.tensor_copy(out=recipb[:], in_=g["dps"][:, 0:512])
                    ev[("rbe", h)] = P.ev(ins, "v_ev")

                @P.seg("tensor")
                def _(e):
                    for i in range(4):
                        e.wait_ge(v_ev, ev[("ptev", h * 4 + i)])
                    e.wait_ge(v_ev, ev[("ql", h)])
                    for k in range(12):
                        ins = e.matmul(G[:], kv["v"][:, kvo + k * 128:kvo + (k + 1) * 128],
                                       pt_sb[:, k * 512:(k + 1) * 512],
                                       start=(k == 0), stop=(k == 11))
                    ev[("av", h)] = P.ev(ins, "t_ev")

                @P.seg("vector")
                def _(e):
                    e.wait_ge(t_ev, ev[("av", h)])
                    e.wait_ge(v_ev, ev[("rbe", h)])
                    P.vbar(e)
                    e.tensor_tensor(out=attn_sb[:], in0=G[:], in1=recipb[:],
                                    op=ALU.mult)
                    P.vbar(e)
                    e.tensor_copy(out=att_hi[:, h * 512:(h + 1) * 512], in_=attn_sb[:])
                    P.vbar(e)
                    ins = e.tensor_tensor(out=att_lo[:, h * 512:(h + 1) * 512],
                                          in0=attn_sb[:],
                                          in1=att_hi[:, h * 512:(h + 1) * 512],
                                          op=ALU.subtract)
                    ev[("attn", h)] = P.ev(ins, "v_ev")

        emit_wo_epilogue(P, t_in, t_scr, g, qb, ev, att_hi, att_lo, SPS, G, qctx)


def emit_wo_epilogue(P, t_in, t_scr, g, qb, ev, att_hi, att_lo, SPS, G, qctx):
    nc = P.nc
    v_ev, s_ev, t_ev = P.sems["v_ev"], P.sems["s_ev"], P.sems["t_ev"]
    S_a, S_b = SPS

    sbb = lambda n, s, d=F32: qctx.enter_context(nc.sbuf_tensor(f"{n}_{qb}", s, d))
    wohs = sbb("wohs", [128, 2 * 4096], BF16)
    wols = sbb("wols", [128, 2 * 4096], BF16)
    x1pre = sbb("x1pre", [128, 4 * 2048])
    xnat = sbb("xnat", [128, 2048])
    gbuf = sbb("gbuf", [128, 2048])
    ubuf = sbb("ubuf", [128, 2048])
    qbuf = sbb("qbuf", [128, 2048], BF16)
    st2 = sbb("st2", [128, 32])

    def wbank(k):
        b = k % 6
        S = S_a if b < 3 else S_b
        return S[:, (b % 3) * 512:(b % 3) * 512 + 256]

    def wo_dma(e, ec, slot):
        ecb, hf = ec // 2, ec % 2
        toks = []
        for hl, buf in ((0, wohs), (1, wols)):
            src = t_scr["wo_hl"][hl, ecb][:, hf * 256:(hf + 1) * 256]
            toks.append(P.dma(e, buf[:, slot * 4096:(slot + 1) * 4096]
                              .rearrange("p (a e) -> p a e", a=16),
                              src.rearrange("(a p) e -> p a e", p=128), "d_s"))
        return toks

    @P.seg("sync")
    def _(e):
        e.wait_ge(v_ev, ev[("attn", 15)])
        ev[("wo", 0)] = wo_dma(e, 0, 0)

    @P.seg("gpsimd")
    def _(e):
        e.wait_ge(v_ev, ev[("attn", 15)])
        ev["bo_dma"] = P.dma(e, gbuf[:], t_in["bo_b"], "d_g")

    for ec in range(8):
        if ec + 1 < 8:
            @P.seg("sync")
            def _(e, ec=ec):
                if ec >= 1:
                    e.wait_ge(t_ev, ev[("wom", ec - 1, 3)])
                ev[("wo", ec + 1)] = wo_dma(e, ec + 1, (ec + 1) % 2)

        for i in range(4):
            k = ec * 4 + i

            @P.seg("tensor")
            def _(e, ec=ec, i=i, k=k):
                for t in ev[("wo", ec)]:
                    P.wge(e, t)
                if k >= 6:
                    e.wait_ge(v_ev, ev[("woe", k - 6)])
                sl = ec % 2
                last = None
                for mc in range(16):
                    ah = att_hi[:, mc * 512 + i * 128:mc * 512 + (i + 1) * 128]
                    al = att_lo[:, mc * 512 + i * 128:mc * 512 + (i + 1) * 128]
                    wh = wohs[:, sl * 4096 + mc * 256:sl * 4096 + (mc + 1) * 256]
                    wl = wols[:, sl * 4096 + mc * 256:sl * 4096 + (mc + 1) * 256]
                    e.matmul(wbank(k), ah, wh, start=(mc == 0), stop=False)
                    e.matmul(wbank(k), ah, wl, start=False, stop=False)
                    last = e.matmul(wbank(k), al, wh, start=False, stop=(mc == 15))
                ev[("wom", ec, i)] = P.ev(last, "t_ev")

            @P.seg("vector")
            def _(e, ec=ec, i=i, k=k):
                e.wait_ge(t_ev, ev[("wom", ec, i)])
                if k == 0:
                    P.wge(e, ev["bo_dma"])
                P.vbar(e)
                ins = e.scalar_tensor_tensor(
                    out=x1pre[:, i * 2048 + ec * 256:i * 2048 + (ec + 1) * 256],
                    in0=wbank(k), scalar=0.0, op0=ALU.add,
                    in1=gbuf[:, ec * 256:(ec + 1) * 256], op1=ALU.add)
                ev[("woe", k)] = P.ev(ins, "v_ev")

    for i in range(4):
        qt = 4 * qb + i

        @P.seg("gpsimd")
        def _(e, i=i, qt=qt):
            if i >= 1:
                e.wait_ge(v_ev, ev[("xadd", i - 1)])
            ev[("xnat", i)] = P.dma(e, xnat[:],
                                    t_in["x_nat"][qt * 128:(qt + 1) * 128, :], "d_g")

        @P.seg("vector")
        def _(e, i=i):
            P.wge(e, ev[("xnat", i)])
            e.wait_ge(v_ev, ev[("woe", 7 * 4 + i)])
            xsl = slice(i * 2048, (i + 1) * 2048)
            ins = e.tensor_tensor(out=x1pre[:, xsl], in0=x1pre[:, xsl],
                                  in1=xnat[:], op=ALU.add)
            ev[("xadd", i)] = P.ev(ins, "v_ev")

        @P.seg("scalar")
        def _(e, i=i):
            e.wait_ge(v_ev, ev[("xadd", i)])
            xsl = slice(i * 2048, (i + 1) * 2048)
            e.activation(out=x1pre[:, xsl], in_=x1pre[:, xsl], func=AF.Copy,
                         bias=0.0, scale=1.0, accum_out=st2[:, 0:1])
            P.sbar(e)
            ins = e.activation(out=ubuf[:], in_=x1pre[:, xsl], func=AF.Square,
                               accum_out=st2[:, 1:2])
            P.ev(ins, "s_ev")
            ev[("lns", i)] = P.sfence(e, "s_ev")

        @P.seg("vector")
        def _(e, i=i):
            e.wait_ge(s_ev, ev[("lns", i)])
            P.vbar(e)
            e.tensor_scalar(out=st2[:, 2:3], in0=st2[:, 0:1], scalar1=1.0 / D,
                            scalar2=None, op0=ALU.mult, op1=ALU.bypass)
            e.tensor_scalar(out=st2[:, 3:4], in0=st2[:, 1:2], scalar1=1.0 / D,
                            scalar2=None, op0=ALU.mult, op1=ALU.bypass)
            P.vbar(e)
            e.tensor_scalar(out=st2[:, 4:5], in0=st2[:, 2:3],
                            scalar1=st2[:, 2:3], scalar2=None,
                            op0=ALU.mult, op1=ALU.bypass)
            P.vbar(e)
            e.tensor_tensor(out=st2[:, 5:6], in0=st2[:, 3:4], in1=st2[:, 4:5],
                            op=ALU.subtract)
            P.vbar(e)
            ins = e.tensor_scalar(out=st2[:, 5:6], in0=st2[:, 5:6], scalar1=1e-5,
                                  scalar2=None, op0=ALU.add, op1=ALU.bypass)
            ev[("var", i)] = P.vfence(e, "v_ev")

        @P.seg("scalar")
        def _(e, i=i):
            e.wait_ge(v_ev, ev[("var", i)])
            e.activation(out=st2[:, 6:7], in_=st2[:, 5:6], func=AF.Ln)
            P.sbar(e)
            ins = e.activation(out=st2[:, 7:8], in_=st2[:, 6:7], func=AF.Exp,
                               scale=-0.5)
            P.ev(ins, "s_ev")
            ev[("rst0", i)] = P.sfence(e, "s_ev")

        @P.seg("vector")
        def _(e, i=i):
            e.wait_ge(s_ev, ev[("rst0", i)])
            P.vbar(e)
            for it in range(2):
                y = st2[:, 7:8]
                e.tensor_scalar(out=st2[:, 8:9], in0=y, scalar1=st2[:, 5:6],
                                scalar2=None, op0=ALU.mult, op1=ALU.bypass)
                P.vbar(e)
                e.tensor_scalar(out=st2[:, 8:9], in0=st2[:, 8:9], scalar1=y,
                                scalar2=-0.5, op0=ALU.mult, op1=ALU.mult)
                P.vbar(e)
                e.tensor_scalar(out=st2[:, 8:9], in0=st2[:, 8:9], scalar1=1.5,
                                scalar2=None, op0=ALU.add, op1=ALU.bypass)
                P.vbar(e)
                e.tensor_scalar(out=st2[:, 7:8], in0=y, scalar1=st2[:, 8:9],
                                scalar2=None, op0=ALU.mult, op1=ALU.bypass)
                P.vbar(e)
            e.tensor_scalar(out=st2[:, 9:10], in0=st2[:, 2:3], scalar1=st2[:, 7:8],
                            scalar2=-1.0, op0=ALU.mult, op1=ALU.mult)
            P.vbar(e)
            xsl = slice(i * 2048, (i + 1) * 2048)
            ins = e.tensor_scalar(out=x1pre[:, xsl], in0=x1pre[:, xsl],
                                  scalar1=st2[:, 7:8], scalar2=st2[:, 9:10],
                                  op0=ALU.mult, op1=ALU.add)
            ev[("x1", i)] = P.ev(ins, "v_ev")

        @P.seg("gpsimd")
        def _(e, i=i, qt=qt):
            e.wait_ge(v_ev, ev[("x1", i)])
            P.dma(e, t_scr["x1"][qt * 128:(qt + 1) * 128, :],
                  x1pre[:, i * 2048:(i + 1) * 2048], "d_g")

        @P.seg("scalar")
        def _(e, i=i):
            e.wait_ge(v_ev, ev[("x1", i)])
            ins = e.activation(out=ubuf[:], in_=x1pre[:, i * 2048:(i + 1) * 2048],
                               func=AF.Square, accum_out=st2[:, 10:11])
            P.ev(ins, "s_ev")
            ev[("ssq1", i)] = P.sfence(e, "s_ev")

        @P.seg("vector")
        def _(e, i=i):
            e.wait_ge(s_ev, ev[("ssq1", i)])
            P.vbar(e)
            ins = e.tensor_scalar(out=st2[:, 11:12], in0=st2[:, 10:11],
                                  scalar1=1.0 / D, scalar2=1e-8,
                                  op0=ALU.mult, op1=ALU.add)
            ev[("v3", i)] = P.vfence(e, "v_ev")

        @P.seg("scalar")
        def _(e, i=i):
            e.wait_ge(v_ev, ev[("v3", i)])
            e.activation(out=st2[:, 12:13], in_=st2[:, 11:12], func=AF.Ln)
            P.sbar(e)
            ins = e.activation(out=st2[:, 13:14], in_=st2[:, 12:13], func=AF.Exp,
                               scale=-0.5)
            P.ev(ins, "s_ev")
            ev[("r10", i)] = P.sfence(e, "s_ev")

        @P.seg("vector")
        def _(e, i=i):
            e.wait_ge(s_ev, ev[("r10", i)])
            P.vbar(e)
            for it in range(2):
                y = st2[:, 13:14]
                e.tensor_scalar(out=st2[:, 8:9], in0=y, scalar1=st2[:, 11:12],
                                scalar2=None, op0=ALU.mult, op1=ALU.bypass)
                P.vbar(e)
                e.tensor_scalar(out=st2[:, 8:9], in0=st2[:, 8:9], scalar1=y,
                                scalar2=-0.5, op0=ALU.mult, op1=ALU.mult)
                P.vbar(e)
                e.tensor_scalar(out=st2[:, 8:9], in0=st2[:, 8:9], scalar1=1.5,
                                scalar2=None, op0=ALU.add, op1=ALU.bypass)
                P.vbar(e)
                e.tensor_scalar(out=st2[:, 13:14], in0=y, scalar1=st2[:, 8:9],
                                scalar2=None, op0=ALU.mult, op1=ALU.bypass)
                P.vbar(e)
            ev[("r1", i)] = P.vfence(e, "v_ev")

    for gi, (gname, beta, xqs) in enumerate(
            (("g1b", "beta1", "xq1"), ("g2b", "beta2", "xq2"))):
        @P.seg("gpsimd")
        def _(e, gi=gi, gname=gname):
            if gi == 0:
                e.wait_ge(v_ev, ev[("woe", 31)])
            else:
                e.wait_ge(v_ev, ev[("qdone", 0, 3)])
            ev[("g_dma", gi)] = P.dma(e, gbuf[:], t_in[gname], "d_g")

        for i in range(4):
            qt = 4 * qb + i
            acol = "a1" if gi == 0 else "a2"

            @P.seg("vector")
            def _(e, gi=gi, i=i, qt=qt, acol=acol, beta=beta):
                P.wge(e, ev[("g_dma", gi)])
                e.wait_ge(v_ev, ev[("r1", i)])
                if (gi, i) != (0, 0):
                    P.wge(e, ev[("q_dma", gi, i - 1) if i >= 1 else ("q_dma", 0, 3)])
                xsl = slice(i * 2048, (i + 1) * 2048)
                e.tensor_tensor(out=ubuf[:], in0=x1pre[:, xsl], in1=gbuf[:],
                                op=ALU.mult)
                P.vbar(e)
                e.tensor_scalar(out=xnat[:], in0=ubuf[:], scalar1=1.0,
                                scalar2=None, op0=ALU.mult, op1=ALU.max,
                                accum_out=st2[:, 14:15])
                P.vbar(e)
                e.tensor_scalar(out=xnat[:], in0=ubuf[:], scalar1=1.0,
                                scalar2=None, op0=ALU.mult, op1=ALU.min,
                                accum_out=st2[:, 15:16])
                P.vbar(e)
                e.tensor_scalar(out=st2[:, 16:17], in0=st2[:, 15:16], scalar1=-1.0,
                                scalar2=None, op0=ALU.mult, op1=ALU.bypass)
                P.vbar(e)
                e.tensor_tensor(out=st2[:, 17:18], in0=st2[:, 14:15],
                                in1=st2[:, 16:17], op=ALU.max)
                P.vbar(e)
                e.tensor_scalar(out=st2[:, 18:19], in0=st2[:, 17:18],
                                scalar1=st2[:, 13:14], scalar2=None,
                                op0=ALU.mult, op1=ALU.bypass)
                P.vbar(e)
                e.tensor_scalar(out=st2[:, 18:19], in0=st2[:, 18:19], scalar1=1e-4,
                                scalar2=None, op0=ALU.max, op1=ALU.bypass)
                P.vbar(e)
                e.tensor_scalar(out=g[beta][:, qt:qt + 1], in0=st2[:, 18:19],
                                scalar1=g[acol][:], scalar2=1.0 / 127.0,
                                op0=ALU.mult, op1=ALU.mult)
                e.reciprocal(out=st2[:, 19:20], in_=st2[:, 18:19])
                P.vbar(e)
                e.tensor_scalar(out=st2[:, 20:21], in0=st2[:, 19:20],
                                scalar1=st2[:, 13:14], scalar2=127.0,
                                op0=ALU.mult, op1=ALU.mult)
                P.vbar(e)
                e.tensor_scalar(out=ubuf[:], in0=ubuf[:], scalar1=st2[:, 20:21],
                                scalar2=g["meff"][:], op0=ALU.mult, op1=ALU.add)
                P.vbar(e)
                ins = e.tensor_scalar(out=qbuf[:], in0=ubuf[:],
                                      scalar1=g["meff"][:], scalar2=None,
                                      op0=ALU.subtract, op1=ALU.bypass)
                ev[("qdone", gi, i)] = P.ev(ins, "v_ev")

            @P.seg("gpsimd")
            def _(e, gi=gi, i=i, qt=qt, xqs=xqs):
                e.wait_ge(v_ev, ev[("qdone", gi, i)])
                ev[("q_dma", gi, i)] = P.dma(
                    e, t_scr[xqs][qt * 128:(qt + 1) * 128, :], qbuf[:], "d_g")


def emit_ffn12(P, t_in, t_scr, g):
    nc = P.nc
    v_ev, s_ev, t_ev = P.sem("v_ev"), P.sem("s_ev"), P.sem("t_ev")

    for half in range(2):
        with ExitStack() as ctx:
            sbf = lambda n, s, d=F32: ctx.enter_context(
                nc.sbuf_tensor(f"f12_{n}_{half}", s, d))
            xq1t = sbf("xq1t", [128, 16 * 1024], BF16)
            xq2t = sbf("xq2t", [128, 16 * 1024], BF16)
            land = sbf("land", [128, H])
            t1 = sbf("t1", [128, H], BF16)
            t2 = sbf("t2", [128, H], BF16)
            u1 = sbf("u1", [128, H])
            g3c = sbf("g3c", [128, 512])
            s1 = sbf("s1", [128, 512])
            hsb = sbf("hsb", [128, 512])
            u3 = sbf("u3", [128, 512])
            stt3 = sbf("stt3", [128, 24])
            htst = sbf("htst", [128, 4 * 1024])
            qland = sbf("qland", [128, 2048], BF16)
            ps_o1 = ctx.enter_context(nc.psum_tensor(f"f12_o1_{half}", [128, 512], F32))
            ps_o2 = ctx.enter_context(nc.psum_tensor(f"f12_o2_{half}", [128, 512], F32))
            ps_tr = ctx.enter_context(nc.psum_tensor(f"f12_tr_{half}", [128, 1024], BF16))
            ps_ht = ctx.enter_context(nc.psum_tensor(f"f12_ht_{half}", [128, 512], F32))
            ev = {}

            for gi, (xqs, xqt) in enumerate((("xq1", xq1t), ("xq2", xq2t))):
                for r in range(8):
                    qt = half * 8 + r

                    @P.seg("gpsimd")
                    def _(e, qt=qt, gi=gi, r=r):
                        if (gi, r) != (0, 0):
                            e.wait_ge(t_ev, ev[("qtrt2", gi, r - 1) if r else ("qtrt2", 0, 7)])
                        ev[("qld", gi, r)] = P.dma(
                            e, qland[:], t_scr[xqs][qt * 128:(qt + 1) * 128, :], "d_g")

                    @P.seg("tensor")
                    def _(e, gi=gi, r=r):
                        P.wge(e, ev[("qld", gi, r)])
                        if (gi, r) != (0, 0):
                            e.wait_ge(v_ev, ev[("qtr", gi, r - 1) if r else ("qtr", 0, 7)])
                        for c in range(8):
                            ins = e.transpose(ps_tr[:, c * 128:(c + 1) * 128],
                                              qland[:, c * 128:(c + 1) * 128],
                                              g["identb"][:])
                        ev[("qtrt1", gi, r)] = P.ev(ins, "t_ev")

                    @P.seg("vector")
                    def _(e, gi=gi, r=r, xqt=xqt):
                        e.wait_ge(t_ev, ev[("qtrt1", gi, r)])
                        for c in range(8):
                            ins = e.tensor_copy(
                                out=xqt[:, c * 1024 + r * 128:c * 1024 + (r + 1) * 128],
                                in_=ps_tr[:, c * 128:(c + 1) * 128])
                        ev[("qev1", gi, r)] = P.ev(ins, "v_ev")

                    @P.seg("tensor")
                    def _(e, gi=gi, r=r):
                        e.wait_ge(v_ev, ev[("qev1", gi, r)])
                        for c in range(8):
                            ins = e.transpose(ps_tr[:, c * 128:(c + 1) * 128],
                                              qland[:, (8 + c) * 128:(9 + c) * 128],
                                              g["identb"][:])
                        ev[("qtrt2", gi, r)] = P.ev(ins, "t_ev")

                    @P.seg("vector")
                    def _(e, gi=gi, r=r, xqt=xqt):
                        e.wait_ge(t_ev, ev[("qtrt2", gi, r)])
                        for c in range(8):
                            ins = e.tensor_copy(
                                out=xqt[:, (8 + c) * 1024 + r * 128:(8 + c) * 1024 + (r + 1) * 128],
                                in_=ps_tr[:, c * 128:(c + 1) * 128])
                        ev[("qtr", gi, r)] = P.ev(ins, "v_ev")

            if half == 0:
                @P.seg("vector")
                def _(e):
                    e.tensor_scalar(out=g["ssq3"][:], in0=g["ident"][:, 0:NQT],
                                    scalar1=0.0, scalar2=None, op0=ALU.mult,
                                    op1=ALU.bypass)
                    ins = e.tensor_scalar(out=g["am3"][:], in0=g["ident"][:, 0:NQT],
                                          scalar1=0.0, scalar2=None, op0=ALU.mult,
                                          op1=ALU.bypass)
                    P.ev(ins, "v_ev")
                    P.vbar(e)

            for hc in range(16):
                hc0 = hc * 512

                @P.seg("sync")
                def _(e, hc=hc, hc0=hc0):
                    if hc >= 1:
                        e.wait_ge(v_ev, ev[("tern", hc - 1, 1)])
                    ev[("wl", hc, 0)] = P.dma(
                        e, land[:].rearrange("p (a d) -> p a d", a=16),
                        t_in["w1T"][:, hc0:hc0 + 512].rearrange("(a p) d -> p a d", p=128),
                        "d_s")

                @P.seg("vector")
                def _(e, hc=hc):
                    P.wge(e, ev[("wl", hc, 0)])
                    if hc >= 1:
                        e.wait_ge(t_ev, ev[("mm2", hc - 1, 7)])
                    e.tensor_scalar(out=u1[:], in0=land[:], scalar1=g["nbnd1"][:],
                                    scalar2=1.0, op0=ALU.is_ge, op1=ALU.subtract)
                    P.vbar(e)
                    ins = e.scalar_tensor_tensor(out=t1[:], in0=land[:],
                                                 scalar=g["bnd1"][:], in1=u1[:],
                                                 op0=ALU.is_ge, op1=ALU.add)
                    ev[("tern", hc, 0)] = P.ev(ins, "v_ev")

                @P.seg("sync")
                def _(e, hc=hc, hc0=hc0):
                    e.wait_ge(v_ev, ev[("tern", hc, 0)])
                    ev[("wl", hc, 1)] = P.dma(
                        e, land[:].rearrange("p (a d) -> p a d", a=16),
                        t_in["w2T"][:, hc0:hc0 + 512].rearrange("(a p) d -> p a d", p=128),
                        "d_s")

                @P.seg("gpsimd")
                def _(e, hc=hc, hc0=hc0):
                    if hc >= 1:
                        e.wait_ge(v_ev, ev[("m3", hc - 1, 7)])
                    ev[("g3", hc)] = P.dma(e, g3c[:, 0:512],
                                           t_in["g3b"][:, hc0:hc0 + 512], "d_g")

                @P.seg("vector")
                def _(e, hc=hc):
                    P.wge(e, ev[("wl", hc, 1)])
                    e.tensor_scalar(out=u1[:], in0=land[:], scalar1=g["nbnd2"][:],
                                    scalar2=1.0, op0=ALU.is_ge, op1=ALU.subtract)
                    P.vbar(e)
                    ins = e.scalar_tensor_tensor(out=t2[:], in0=land[:],
                                                 scalar=g["bnd2"][:], in1=u1[:],
                                                 op0=ALU.is_ge, op1=ALU.add)
                    ev[("tern", hc, 1)] = P.ev(ins, "v_ev")

                for r in range(8):
                    qt = half * 8 + r
                    qtc = qt

                    @P.seg("tensor")
                    def _(e, hc=hc, r=r):
                        if (hc, r) == (0, 0):
                            e.wait_ge(v_ev, ev[("qtr", 1, 7)])
                        e.wait_ge(v_ev, ev[("tern", hc, 1)])
                        if (hc + r) > 0:
                            e.wait_ge(s_ev, ev[("silu", hc, r - 1) if r else ("silu", hc - 1, 7)])
                        last = None
                        for ec in range(16):
                            last = e.matmul(ps_o1[:],
                                            xq1t[:, ec * 1024 + r * 128:ec * 1024 + (r + 1) * 128],
                                            t1[:, ec * 512:(ec + 1) * 512],
                                            start=(ec == 0), stop=(ec == 15))
                        ev[("mm1", hc, r)] = P.ev(last, "t_ev")
                        if (hc + r) > 0:
                            e.wait_ge(v_ev, ev[("h", hc, r - 1) if r else ("h", hc - 1, 7)])
                        for ec in range(16):
                            last = e.matmul(ps_o2[:],
                                            xq2t[:, ec * 1024 + r * 128:ec * 1024 + (r + 1) * 128],
                                            t2[:, ec * 512:(ec + 1) * 512],
                                            start=(ec == 0), stop=(ec == 15))
                        ev[("mm2", hc, r)] = P.ev(last, "t_ev")

                    @P.seg("scalar")
                    def _(e, hc=hc, r=r, qtc=qtc):
                        e.wait_ge(t_ev, ev[("mm1", hc, r)])
                        ins = e.activation(out=s1[:], in_=ps_o1[:], func=AF.Silu,
                                           scale=g["beta1"][:, qtc:qtc + 1])
                        ev[("silu", hc, r)] = P.ev(ins, "s_ev")

                    @P.seg("vector")
                    def _(e, hc=hc, r=r, qtc=qtc):
                        e.wait_ge(t_ev, ev[("mm2", hc, r)])
                        e.wait_ge(s_ev, ev[("silu", hc, r)])
                        if (hc + r) > 0:
                            e.wait_ge(t_ev, ev[("htr", hc, r - 1) if r else ("htr", hc - 1, 7)])
                            e.wait_ge(s_ev, ev[("sqr", hc, r - 1) if r else ("sqr", hc - 1, 7)])
                        ins = e.scalar_tensor_tensor(out=hsb[:], in0=ps_o2[:],
                                                     scalar=g["beta2"][:, qtc:qtc + 1],
                                                     op0=ALU.mult, in1=s1[:],
                                                     op1=ALU.mult)
                        ev[("h", hc, r)] = P.ev(ins, "v_ev")

                    @P.seg("scalar")
                    def _(e, hc=hc, r=r, qtc=qtc):
                        e.wait_ge(v_ev, ev[("h", hc, r)])
                        ins = e.activation(out=s1[:], in_=hsb[:], func=AF.Square,
                                           accum_out=stt3[:, 16 + r:17 + r])
                        ev[("sqr", hc, r)] = P.ev(ins, "s_ev")
                        if r == 7:
                            ev[("sq", hc)] = P.sfence(e, "s_ev")

                    @P.seg("vector")
                    def _(e, hc=hc, r=r, qtc=qtc, half=half):
                        P.wge(e, ev[("g3", hc)])
                        P.vbar(e)
                        ins = e.tensor_tensor(out=u3[:], in0=hsb[:], in1=g3c[:, 0:512],
                                              op=ALU.mult)
                        P.vbar(e)
                        e.tensor_scalar(out=u3[:], in0=u3[:], scalar1=-1.0,
                                        scalar2=None, op0=ALU.mult, op1=ALU.max,
                                        accum_out=stt3[:, r:r + 1])
                        P.vbar(e)
                        ins = e.tensor_scalar(out=u3[:], in0=u3[:], scalar1=-1.0,
                                              scalar2=None, op0=ALU.mult, op1=ALU.max,
                                              accum_out=stt3[:, 8 + r:9 + r])
                        ev[("m3", hc, r)] = P.ev(ins, "v_ev")
                        if r == 7:
                            qsl = slice(half * 8, half * 8 + 8)
                            P.vbar(e)
                            e.tensor_tensor(out=stt3[:, 0:8], in0=stt3[:, 0:8],
                                            in1=stt3[:, 8:16], op=ALU.max)
                            P.vbar(e)
                            e.tensor_tensor(out=g["am3"][:, qsl], in0=g["am3"][:, qsl],
                                            in1=stt3[:, 0:8], op=ALU.max)
                            e.wait_ge(s_ev, ev[("sq", hc)])
                            ins = e.tensor_tensor(out=g["ssq3"][:, qsl],
                                                  in0=g["ssq3"][:, qsl],
                                                  in1=stt3[:, 16:24], op=ALU.add)
                            ev[("m3f", hc)] = P.ev(ins, "v_ev")
                            P.vbar(e)

                    @P.seg("tensor")
                    def _(e, hc=hc, r=r):
                        e.wait_ge(v_ev, ev[("h", hc, r)])
                        if (hc + r) > 0:
                            e.wait_ge(v_ev, ev[("hte", hc, r - 1) if r else ("hte", hc - 1, 7)])
                        for c in range(4):
                            ins = e.transpose(ps_ht[:, c * 128:(c + 1) * 128],
                                              hsb[:, c * 128:(c + 1) * 128],
                                              g["ident"][:])
                        ev[("htr", hc, r)] = P.ev(ins, "t_ev")

                    @P.seg("vector")
                    def _(e, hc=hc, r=r):
                        e.wait_ge(t_ev, ev[("htr", hc, r)])
                        P.vbar(e)
                        ins = e.tensor_copy(
                            out=htst[:, 0:4096].rearrange("p (a q) -> p a q", a=4)[:, :, r * 128:(r + 1) * 128],
                            in_=ps_ht[:, 0:512].rearrange("p (a q) -> p a q", a=4))
                        ev[("hte", hc, r)] = P.ev(ins, "v_ev")

                @P.seg("gpsimd")
                def _(e, hc=hc, half=half):
                    e.wait_ge(v_ev, ev[("hte", hc, 7)])
                    toks = []
                    for hs in range(4):
                        toks.append(P.dma(
                            e, t_scr["hT"][hc * 512 + hs * 128:hc * 512 + (hs + 1) * 128,
                                           half * 1024:(half + 1) * 1024],
                            htst[:, hs * 1024:(hs + 1) * 1024], "d_g"))
                    ev[("htw", hc)] = toks

                @P.seg("vector")
                def _(e, hc=hc):
                    for t in ev[("htw", hc)]:
                        P.wge(e, t)
        barrier(P, g)


def emit_ffn3(P, t_in, t_scr, g, out):
    nc = P.nc
    v_ev, s_ev, t_ev = P.sem("v_ev"), P.sem("s_ev"), P.sem("t_ev")
    ev = {}
    with ExitStack() as ctx:
        sbf = lambda n, s, d=F32: ctx.enter_context(nc.sbuf_tensor(f"f3_{n}", s, d))
        hq = sbf("hq", [128, 64 * 512], BF16)
        hland = sbf("hland", [128, 2 * 512])
        wbuf = sbf("wbuf", [128, 2 * 512])
        t3sb = sbf("t3sb", [128, 2 * 64 * 128], BF16)
        sc3b = sbf("sc3b", [128, 2048])
        g3t = sbf("g3t", [128, 64])
        o3t = sbf("o3t", [128, 16 * 512])
        x1sl = sbf("x1sl", [128, 2048])
        x2 = sbf("x2", [128, 2048])
        rrow3 = sbf("rrow3", [1, 2048])
        st3 = sbf("st3", [128, 16])
        o3ps = [ctx.enter_context(nc.psum_tensor(f"f3_o{i}", [128, 512], F32)) for i in range(2)]
        trps = [ctx.enter_context(nc.psum_tensor(f"f3_t{i}", [128, 512], F32)) for i in range(2)]

        @P.seg("gpsimd")
        def _(e):
            ev["g3t"] = P.dma(e, g3t[:, 0:64], t_in["g3t"], "d_g")

        @P.seg("vector")
        def _(e):
            e.tensor_scalar(out=st3[:, 0:16], in0=g["ssq3"][:], scalar1=1.0 / H,
                            scalar2=1e-8, op0=ALU.mult, op1=ALU.add)
            ev["v3"] = P.vfence(e, "v_ev")

        @P.seg("scalar")
        def _(e):
            e.wait_ge(v_ev, ev["v3"])
            e.activation(out=st3[:, 0:16], in_=st3[:, 0:16], func=AF.Ln)
            P.sbar(e)
            ins = e.activation(out=st3[:, 0:16], in_=st3[:, 0:16], func=AF.Exp,
                               scale=-0.5)
            P.ev(ins, "s_ev")
            ev["r30"] = P.sfence(e, "s_ev")

        @P.seg("vector")
        def _(e):
            e.wait_ge(s_ev, ev["r30"])
            P.vbar(e)
            for it in range(2):
                e.tensor_scalar(out=g["gam1"][:], in0=g["ssq3"][:], scalar1=1.0 / H,
                                scalar2=1e-8, op0=ALU.mult, op1=ALU.add)
                P.vbar(e)
                e.tensor_tensor(out=g["gam1"][:], in0=g["gam1"][:],
                                in1=st3[:, 0:16], op=ALU.mult)
                P.vbar(e)
                e.tensor_tensor(out=g["gam1"][:], in0=g["gam1"][:],
                                in1=st3[:, 0:16], op=ALU.mult)
                P.vbar(e)
                e.tensor_scalar(out=g["gam1"][:], in0=g["gam1"][:], scalar1=-0.5,
                                scalar2=1.5, op0=ALU.mult, op1=ALU.add)
                P.vbar(e)
                e.tensor_tensor(out=st3[:, 0:16], in0=st3[:, 0:16],
                                in1=g["gam1"][:], op=ALU.mult)
                P.vbar(e)
            e.tensor_tensor(out=g["gam1"][:], in0=g["am3"][:], in1=st3[:, 0:16],
                            op=ALU.mult)
            P.vbar(e)
            e.tensor_scalar(out=g["gam1"][:], in0=g["gam1"][:], scalar1=1e-4,
                            scalar2=None, op0=ALU.max, op1=ALU.bypass)
            P.vbar(e)
            e.tensor_scalar(out=g["gam2"][:], in0=g["gam1"][:],
                            scalar1=g["a3"][:], scalar2=1.0 / 127.0,
                            op0=ALU.mult, op1=ALU.mult)
            e.reciprocal(out=g["gam1"][:], in_=g["gam1"][:])
            P.vbar(e)
            e.tensor_tensor(out=g["gam1"][:], in0=g["gam1"][:], in1=st3[:, 0:16],
                            op=ALU.mult)
            P.vbar(e)
            ins = e.tensor_scalar(out=g["gam1"][:], in0=g["gam1"][:], scalar1=127.0,
                                  scalar2=None, op0=ALU.mult, op1=ALU.bypass)
            ev["m3tok"] = P.vfence(e, "v_ev")

        for grp in range(4):
            @P.seg("tensor")
            def _(e, grp=grp):
                if grp == 0:
                    e.wait_ge(v_ev, ev["m3tok"])
                else:
                    e.wait_ge(v_ev, ev[("sc3e", grp - 1)])
                for k in range(4):
                    col = grp * 4 + k
                    ins = e.transpose(g["dps"][0:1, k * 128:(k + 1) * 128],
                                      g["gam1"][:, col:col + 1], g["ident"][:])
                ev[("sc3t", grp)] = P.ev(ins, "t_ev")

            @P.seg("vector")
            def _(e, grp=grp):
                e.wait_ge(t_ev, ev[("sc3t", grp)])
                P.vbar(e)
                ins = e.tensor_copy(out=rrow3[0:1, grp * 512:(grp + 1) * 512],
                                    in_=g["dps"][0:1, 0:512])
                ev[("sc3e", grp)] = P.ev(ins, "v_ev")

        @P.seg("tensor")
        def _(e):
            e.wait_ge(v_ev, ev[("sc3e", 3)])
            ins = e.matmul(g["dps"][:, 0:512], g["ones_row"][:],
                           rrow3[0:1, 0:512], start=True, stop=True)
            ev["bc0"] = P.ev(ins, "t_ev")

        for c in range(4):
            @P.seg("vector")
            def _(e, c=c):
                e.wait_ge(t_ev, ev["bc0"] + c)
                P.vbar(e)
                ins = e.tensor_copy(out=sc3b[:, c * 512:(c + 1) * 512],
                                    in_=g["dps"][:, 0:512])
                ev[("sc3b", c)] = P.ev(ins, "v_ev")

            if c + 1 < 4:
                @P.seg("tensor")
                def _(e, c=c):
                    e.wait_ge(v_ev, ev[("sc3b", c)])
                    ins = e.matmul(g["dps"][:, 0:512], g["ones_row"][:],
                                   rrow3[0:1, (c + 1) * 512:(c + 2) * 512],
                                   start=True, stop=True)
                    P.ev(ins, "t_ev")

        for qg in range(4):
            for hc in range(64):
                sl = hc % 2

                @P.seg("sync")
                def _(e, qg=qg, hc=hc, sl=sl):
                    if hc >= 2:
                        e.wait_ge(v_ev, ev[("hq", qg, hc - 2)])
                    elif qg >= 1 and hc < 2:
                        e.wait_ge(t_ev, ev[("f3m", qg - 1, 15)])
                    ev[("hl", qg, hc)] = P.dma(
                        e, hland[:, sl * 512:(sl + 1) * 512],
                        t_scr["hT"][hc * 128:(hc + 1) * 128,
                                    qg * 512:(qg + 1) * 512], "d_s")

                @P.seg("vector")
                def _(e, qg=qg, hc=hc, sl=sl):
                    P.wge(e, ev[("hl", qg, hc)])
                    if (qg, hc) == (0, 0):
                        P.wge(e, ev["g3t"])
                        for c in range(4):
                            e.wait_ge(v_ev, ev[("sc3b", c)])
                    if qg >= 1 and hc == 0:
                        e.wait_ge(t_ev, ev[("f3m", qg - 1, 15)])
                    e.scalar_tensor_tensor(out=wbuf[:, sl * 512:(sl + 1) * 512],
                                           in0=hland[:, sl * 512:(sl + 1) * 512],
                                           scalar=g3t[:, hc:hc + 1],
                                           in1=sc3b[:, qg * 512:(qg + 1) * 512],
                                           op0=ALU.mult, op1=ALU.mult)
                    P.vbar(e)
                    e.tensor_scalar(out=wbuf[:, sl * 512:(sl + 1) * 512],
                                    in0=wbuf[:, sl * 512:(sl + 1) * 512],
                                    scalar1=g["meff"][:], scalar2=None,
                                    op0=ALU.add, op1=ALU.bypass)
                    P.vbar(e)
                    ins = e.tensor_scalar(out=hq[:, hc * 512:(hc + 1) * 512],
                                          in0=wbuf[:, sl * 512:(sl + 1) * 512],
                                          scalar1=g["meff"][:], scalar2=None,
                                          op0=ALU.subtract, op1=ALU.bypass)
                    ev[("hq", qg, hc)] = P.ev(ins, "v_ev")

            for dt in range(16):
                sl = dt % 2

                @P.seg("sync")
                def _(e, qg=qg, dt=dt, sl=sl):
                    if dt >= 2 or qg >= 1:
                        prev = ("f3m", qg, dt - 2) if dt >= 2 else ("f3m", qg - 1, 14 + dt)
                        e.wait_ge(t_ev, ev[prev])
                    ev[("t3", qg, dt)] = P.dma(
                        e, t3sb[:, sl * 8192:(sl + 1) * 8192]
                        .rearrange("p (a d) -> p a d", a=64),
                        t_scr["t3"][dt].rearrange("(a p) d -> p a d", p=128), "d_s")

                @P.seg("tensor")
                def _(e, qg=qg, dt=dt, sl=sl):
                    P.wge(e, ev[("t3", qg, dt)])
                    if dt == 0:
                        e.wait_ge(v_ev, ev[("hq", qg, 63)])
                    if dt >= 2:
                        e.wait_ge(v_ev, ev[("o3e", qg, dt - 2)])
                    last = None
                    for hc in range(64):
                        last = e.matmul(o3ps[dt % 2][:],
                                        t3sb[:, sl * 8192 + hc * 128:sl * 8192 + (hc + 1) * 128],
                                        hq[:, hc * 512:(hc + 1) * 512],
                                        start=(hc == 0), stop=(hc == 63))
                    ev[("f3m", qg, dt)] = P.ev(last, "t_ev")

                @P.seg("vector")
                def _(e, qg=qg, dt=dt):
                    e.wait_ge(t_ev, ev[("f3m", qg, dt)])
                    P.vbar(e)
                    ins = e.tensor_copy(out=o3t[:, dt * 512:(dt + 1) * 512],
                                        in_=o3ps[dt % 2][:])
                    ev[("o3e", qg, dt)] = P.ev(ins, "v_ev")

            for tq in range(4):
                qt = qg * 4 + tq

                @P.seg("gpsimd")
                def _(e, qt=qt, tq=tq, qg=qg):
                    if (qg, tq) != (0, 0):
                        e.wait_ge(v_ev, ev[("x2", qg, tq - 1) if tq else ("x2", qg - 1, 3)])
                    ev[("x1l", qg, tq)] = P.dma(
                        e, x1sl[:], t_scr["x1"][qt * 128:(qt + 1) * 128, :], "d_g")

                for grp in range(4):
                    @P.seg("tensor")
                    def _(e, qg=qg, tq=tq, grp=grp):
                        if grp == 0:
                            e.wait_ge(v_ev, ev[("o3e", qg, 15)])
                        if (tq, grp) != (0, 0):
                            pg = ("tre", qg, tq, grp - 1) if grp else ("tre", qg, tq - 1, 3)
                            e.wait_ge(v_ev, ev[pg])
                        elif qg >= 1:
                            e.wait_ge(v_ev, ev[("tre", qg - 1, 3, 2)])
                            e.wait_ge(v_ev, ev[("tre", qg - 1, 3, 3)])
                        for k in range(4):
                            dt = grp * 4 + k
                            ins = e.transpose(trps[grp % 2][:, k * 128:(k + 1) * 128],
                                              o3t[:, dt * 512 + tq * 128:dt * 512 + (tq + 1) * 128],
                                              g["ident"][:])
                        ev[("tr", qg, tq, grp)] = P.ev(ins, "t_ev")

                    @P.seg("vector")
                    def _(e, qg=qg, tq=tq, grp=grp, qt=qt):
                        e.wait_ge(t_ev, ev[("tr", qg, tq, grp)])
                        if grp == 0:
                            P.wge(e, ev[("x1l", qg, tq)])
                            if (qg, tq) != (0, 0):
                                P.wge(e, ev[("outw", qg, tq - 1) if tq else ("outw", qg - 1, 3)])
                        P.vbar(e)
                        ins = e.scalar_tensor_tensor(
                            out=x2[:, grp * 512:(grp + 1) * 512],
                            in0=trps[grp % 2][:], scalar=g["gam2"][:, qt:qt + 1],
                            op0=ALU.mult,
                            in1=x1sl[:, grp * 512:(grp + 1) * 512], op1=ALU.add)
                        ev[("tre", qg, tq, grp)] = P.ev(ins, "v_ev")

                @P.seg("scalar")
                def _(e, qg=qg, tq=tq):
                    e.wait_ge(v_ev, ev[("tre", qg, tq, 3)])
                    e.activation(out=x2[:], in_=x2[:], func=AF.Copy,
                                 bias=0.0, scale=1.0, accum_out=st3[:, 0:1])
                    P.sbar(e)
                    ins = e.activation(out=x1sl[:], in_=x2[:], func=AF.Square,
                                       accum_out=st3[:, 1:2])
                    P.ev(ins, "s_ev")
                    ev[("ln2s", qg, tq)] = P.sfence(e, "s_ev")

                @P.seg("vector")
                def _(e, qg=qg, tq=tq):
                    e.wait_ge(s_ev, ev[("ln2s", qg, tq)])
                    P.vbar(e)
                    e.tensor_scalar(out=st3[:, 2:3], in0=st3[:, 0:1], scalar1=1.0 / D,
                                    scalar2=None, op0=ALU.mult, op1=ALU.bypass)
                    e.tensor_scalar(out=st3[:, 3:4], in0=st3[:, 1:2], scalar1=1.0 / D,
                                    scalar2=None, op0=ALU.mult, op1=ALU.bypass)
                    P.vbar(e)
                    e.tensor_scalar(out=st3[:, 4:5], in0=st3[:, 2:3],
                                    scalar1=st3[:, 2:3], scalar2=None,
                                    op0=ALU.mult, op1=ALU.bypass)
                    P.vbar(e)
                    e.tensor_tensor(out=st3[:, 5:6], in0=st3[:, 3:4], in1=st3[:, 4:5],
                                    op=ALU.subtract)
                    P.vbar(e)
                    ins = e.tensor_scalar(out=st3[:, 5:6], in0=st3[:, 5:6], scalar1=1e-5,
                                          scalar2=None, op0=ALU.add, op1=ALU.bypass)
                    ev[("var2", qg, tq)] = P.vfence(e, "v_ev")

                @P.seg("scalar")
                def _(e, qg=qg, tq=tq):
                    e.wait_ge(v_ev, ev[("var2", qg, tq)])
                    e.activation(out=st3[:, 6:7], in_=st3[:, 5:6], func=AF.Ln)
                    P.sbar(e)
                    ins = e.activation(out=st3[:, 7:8], in_=st3[:, 6:7], func=AF.Exp,
                                       scale=-0.5)
                    P.ev(ins, "s_ev")
                    ev[("rs20", qg, tq)] = P.sfence(e, "s_ev")

                @P.seg("vector")
                def _(e, qg=qg, tq=tq):
                    e.wait_ge(s_ev, ev[("rs20", qg, tq)])
                    P.vbar(e)
                    for it in range(2):
                        y = st3[:, 7:8]
                        e.tensor_scalar(out=st3[:, 8:9], in0=y, scalar1=st3[:, 5:6],
                                        scalar2=None, op0=ALU.mult, op1=ALU.bypass)
                        P.vbar(e)
                        e.tensor_scalar(out=st3[:, 8:9], in0=st3[:, 8:9], scalar1=y,
                                        scalar2=-0.5, op0=ALU.mult, op1=ALU.mult)
                        P.vbar(e)
                        e.tensor_scalar(out=st3[:, 8:9], in0=st3[:, 8:9], scalar1=1.5,
                                        scalar2=None, op0=ALU.add, op1=ALU.bypass)
                        P.vbar(e)
                        e.tensor_scalar(out=st3[:, 7:8], in0=y, scalar1=st3[:, 8:9],
                                        scalar2=None, op0=ALU.mult, op1=ALU.bypass)
                        P.vbar(e)
                    e.tensor_scalar(out=st3[:, 9:10], in0=st3[:, 2:3], scalar1=st3[:, 7:8],
                                    scalar2=-1.0, op0=ALU.mult, op1=ALU.mult)
                    P.vbar(e)
                    ins = e.tensor_scalar(out=x2[:], in0=x2[:],
                                          scalar1=st3[:, 7:8], scalar2=st3[:, 9:10],
                                          op0=ALU.mult, op1=ALU.add)
                    ev[("x2", qg, tq)] = P.ev(ins, "v_ev")

                @P.seg("gpsimd")
                def _(e, qt=qt, qg=qg, tq=tq):
                    e.wait_ge(v_ev, ev[("x2", qg, tq)])
                    ev[("outw", qg, tq)] = P.dma(
                        e, out[qt * 128:(qt + 1) * 128, :], x2[:], "d_g")



_NC_CACHE = []


def kernel(L0_attn_x, L0_attn_Wq, L0_attn_Wk, L0_attn_Wv, L0_attn_Wo,
           L0_attn_bo, L0_ffn_w1, L0_ffn_g1, L0_ffn_w2, L0_ffn_g2,
           L0_ffn_w3, L0_ffn_g3):
    x = np.ascontiguousarray(np.asarray(L0_attn_x, dtype=np.float32))
    Wq = np.ascontiguousarray(np.asarray(L0_attn_Wq, dtype=np.float32))
    Wk = np.ascontiguousarray(np.asarray(L0_attn_Wk, dtype=np.float32))
    Wv = np.ascontiguousarray(np.asarray(L0_attn_Wv, dtype=np.float32))
    Wo = np.ascontiguousarray(np.asarray(L0_attn_Wo, dtype=np.float32))
    bo = np.ascontiguousarray(np.asarray(L0_attn_bo, dtype=np.float32))
    w1T = np.ascontiguousarray(np.asarray(L0_ffn_w1, dtype=np.float32).T)
    w2T = np.ascontiguousarray(np.asarray(L0_ffn_w2, dtype=np.float32).T)
    w3T = np.ascontiguousarray(np.asarray(L0_ffn_w3, dtype=np.float32).T)
    g1 = np.ascontiguousarray(np.asarray(L0_ffn_g1, dtype=np.float32))
    g2 = np.ascontiguousarray(np.asarray(L0_ffn_g2, dtype=np.float32))
    g3 = np.ascontiguousarray(np.asarray(L0_ffn_g3, dtype=np.float32))

    if not _NC_CACHE:
        _NC_CACHE.append(build_program())
    nc = _NC_CACHE[0]
    in_maps = [_stage_core(c, x, Wq, Wk, Wv, Wo, bo, w1T, w2T, w3T, g1, g2, g3)
               for c in range(8)]
    res = run_bass_kernel_spmd(nc, in_maps, core_ids=list(range(8)))
    outp = np.empty((B, T, D), dtype=np.float32)
    for c in range(8):
        b, half = c // 2, c % 2
        outp[b, half * NQ:(half + 1) * NQ] = res.results[c]["out"]
    return outp
